# version 05
# baseline (speedup 1.0000x reference)
import numpy as np

import concourse.bass as bass
import concourse.mybir as mybir
from concourse.bass_utils import run_bass_kernel_spmd

F32 = mybir.dt.float32
BF16 = mybir.dt.bfloat16
AF = mybir.ActivationFunctionType
ALU = mybir.AluOpType
AXX = mybir.AxisListType.X

DM = 1024
DI = 1024
T = 4096
TOWN = 2048
NST = 16
TC1 = 512
NCH1 = T // TC1
TC2 = 256
NCH2 = T // TC2
EPS = 1e-5
PAIRS = [[0, 1], [2, 3], [4, 5], [6, 7]]
ALL8 = [[0, 1, 2, 3, 4, 5, 6, 7]]
MIDSQ_A = [0.0625, 0.5625, 1.5625, 3.0625]
MIDSQ_B = [6.25, 12.25, 25.0, 25.0]
WSEC = 128 * 4096
DSEC = 512 * 1024
AGN = 2 * WSEC + DSEC
ENGS = ("tensor", "vector", "scalar", "gpsimd", "sync")


class Rec:
    class _Inst:
        def __init__(self, item):
            self.item = item

        def then_inc(self, sem, val):
            self.item[3].append((sem, val))
            return self

    def __init__(self, name):
        self.name = name
        self.items = []

    def __getattr__(self, meth):
        if meth.startswith("_"):
            raise AttributeError(meth)

        def call(*args, **kwargs):
            item = (meth, args, kwargs, [])
            self.items.append(item)
            return Rec._Inst(item)
        return call

    def replay(self, eng):
        for meth, args, kwargs, incs in self.items:
            inst = getattr(eng, meth)(*args, **kwargs)
            for sem, val in incs:
                inst.then_inc(sem, val)


class Ctr:
    def __init__(self, sem):
        self.sem = sem
        self.val = 0

    def bump(self, n=1):
        self.val += n
        return self.val


class Arena:
    def __init__(self, tensor, cols):
        self.t = tensor
        self.cols = cols
        self.off = 0
        self.marks = []
        self.hi = 0

    def alloc(self, ncols, dtype=F32):
        assert self.off + ncols <= self.cols, \
            f"arena overflow {self.off}+{ncols}>{self.cols}"
        ap = self.t[:, self.off:self.off + ncols]
        self.off += ncols
        self.hi = max(self.hi, self.off)
        if dtype == BF16:
            ap = ap.bitcast(BF16)
        return ap

    def push(self):
        self.marks.append(self.off)

    def pop(self):
        self.off = self.marks.pop()


def build_program(dbg=False):
    nc = bass.Bass("TRN2", target_bir_lowering=False, debug=False,
                   detect_race_conditions=False)

    def inp(name, shape, dtype=F32):
        return nc.dram_tensor(name, shape, dtype, kind="ExternalInput").ap()

    x_full = inp("x_full", [T, DM])
    x_own = inp("x_own", [TOWN, DM])
    wxT = inp("wxT", [DM, DI])
    wzT = inp("wzT", [DM, DI])
    wxpT = inp("wxpT", [DI, 96])
    wdtT = inp("wdtT", [64, DI])
    woutT = inp("woutT", [DI, DM])
    bxv = inp("bxv", [128, 8])
    bzv = inp("bzv", [128, 8])
    bdtv = inp("bdtv", [128, 8])
    convbv = inp("convbv", [128, 8])
    dvecv = inp("dvecv", [128, 8])
    convwv = inp("convwv", [128, 32])
    alogv = inp("alogv", [128, 8 * NST])
    boutv = inp("boutv", [1, DM])
    normwv = inp("normwv", [128, 8])
    gsl = inp("gsl", [4096, 128])
    usl = inp("usl", [4096, 128])
    dsl = inp("dsl", [DM, 512])
    s_in = inp("s_in", [1, 1])
    s_ind = inp("s_ind", [1, 1])
    ident = inp("ident", [128, 128])

    out_own = nc.dram_tensor("out_own", [TOWN, DM], F32, kind="ExternalOutput").ap()
    dbg_outs = {}
    if dbg:
        for nm, shp, dt_ in [("d_xbr", [DI, T], F32), ("d_sz", [DI, T], F32),
                             ("d_xdbl", [96, T], F32), ("d_ar", [DM, TOWN], F32),
                             ("d_x2", [TOWN, DM], F32), ("d_dacc", [TOWN, DM], F32),
                             ("d_rho", [128, 32], F32), ("d_gs", [128, 16], F32),
                             ("d_sums", [128, 64], F32), ("d_aret", [TOWN, DM], F32), ("d_apl", [TOWN, DM], F32),
                             ("d_st3", [128, 16], F32), ("d_xo", [128, DM], F32)]:
            dbg_outs[nm] = nc.dram_tensor(nm, shp, dt_, kind="ExternalOutput").ap()

    xbr_dram = nc.dram_tensor("xbr_dram", [DI, T], F32).ap()
    sz_dram = nc.dram_tensor("sz_dram", [DI, T], F32).ap()
    x2_dram = nc.dram_tensor("x2_dram", [TOWN, DM], F32).ap()
    dacc_dram = nc.dram_tensor("dacc_dram", [TOWN, DM], F32).ap()
    wred_dram = nc.dram_tensor("wred_dram", [128, 3], F32).ap()
    cc_xdbl_in = nc.dram_tensor("cc_xdbl_in", [96, T], F32)
    cc_xdbl_out = nc.dram_tensor("cc_xdbl_out", [96, T], F32)
    cc_wmax_in = nc.dram_tensor("cc_wmax_in", [1, 4], F32)
    cc_wmax_out = nc.dram_tensor("cc_wmax_out", [1, 4], F32)
    cc_ag_in = nc.dram_tensor("cc_ag_in", [1, AGN], BF16)
    cc_ag_out = nc.dram_tensor("cc_ag_out", [8, AGN], BF16)
    cc_ar_in = nc.dram_tensor("cc_ar_in", [4, 2, DM, TC1], F32)
    cc_ar_out = nc.dram_tensor("cc_ar_out", [4, DM, TC1], F32)

    def sba(name, shape, dtype=F32):
        return nc.alloc_sbuf_tensor(name, shape, dtype).ap()

    ids = sba("ids", [128, 128])
    idsb = sba("idsb", [128, 128], BF16)
    ones_s = sba("ones_s", [1, TC1])
    bout_s = sba("bout_s", [1, DM])
    bxs = sba("bxs", [128, 8])
    bzs = sba("bzs", [128, 8])
    bdts = sba("bdts", [128, 8])
    convbs = sba("convbs", [128, 8])
    dvs = sba("dvs", [128, 8])
    convws = sba("convws", [128, 32])
    a_s = sba("a_s", [128, 8 * NST])
    normws = sba("normws", [128, 8])
    sum_all = sba("sum_all", [128, 32])
    ssq_all = sba("ssq_all", [128, 32])
    rho_all = sba("rho_all", [128, 32])
    nmr_all = sba("nmr_all", [128, 32])
    sscr = sba("sscr", [128, 6 * 32])
    hstA = sba("hstA", [128, 8 * NST])
    hstB = sba("hstB", [128, 8 * NST])
    gs_cols = sba("gs_cols", [128, 16])
    wmax_s = sba("wmax_s", [1, 4])
    wred_s = sba("wred_s", [1, 3 * 128])

    AR_COLS = 48900
    arena_t = nc.alloc_sbuf_tensor("arena", [128, AR_COLS], F32).ap()
    AR = Arena(arena_t, AR_COLS)

    PS = nc.alloc_psum_tensor("psum", [128, 4096], F32).ap()

    def bank(i):
        return PS[:, i * 512:(i + 1) * 512]

    def bankb(i):
        return PS[:, i * 512:(i + 1) * 512].bitcast(BF16)

    sems = {n: nc.alloc_semaphore(name=n) for n in
            ["sP", "sV", "sC", "sG", "sCC", "sBar"]}
    C = {n: Ctr(sems[n]) for n in sems}
    R = {n: Rec(n) for n in ENGS}
    TN, V, S, G, SY = (R[n] for n in ENGS)
    nbar = [0]
    dsems = {}

    def dsem(fam):
        if fam not in dsems:
            dsems[fam] = Ctr(nc.alloc_semaphore(name="d_" + fam))
        return dsems[fam]

    def dma(rec, fam, out, in_, **kw):
        c = dsem(fam)
        rec.dma_start(out=out, in_=in_, **kw).then_inc(c.sem, 16)
        return c.bump(16)

    def dwait(rec, fam):
        c = dsem(fam)
        rec.wait_ge(c.sem, c.val)

    def barrier():
        for fam, c in dsems.items():
            SY.wait_ge(c.sem, c.val)
        nbar[0] += 1
        tgt = 5 * nbar[0]
        for n in ENGS:
            R[n].sem_inc(sems["sBar"], 1)
            R[n].wait_ge(sems["sBar"], tgt)

    inc_pos = {}

    def mark(rec, s):
        rec.items[-1][3].append((sems[s], 1))
        tgt = C[s].bump(1)
        inc_pos[(rec.name, len(rec.items) - 1)] = (sems[s], tgt)
        return tgt

    gapV = Ctr(nc.alloc_semaphore(name="gapV"))
    gapS = Ctr(nc.alloc_semaphore(name="gapS"))

    def _gap(rec, gap):
        assert rec.items, "gap needs a preceding instruction"
        key = (rec.name, len(rec.items) - 1)
        if key in inc_pos:
            sem, tgt = inc_pos[key]
        else:
            sem, tgt = gap.sem, gap.bump(1)
            rec.items[-1][3].append((sem, 1))
            inc_pos[key] = (sem, tgt)
        rec.wait_ge(sem, tgt)

    def VF():
        _gap(V, gapV)

    def SF():
        _gap(S, gapS)

    gapG = Ctr(nc.alloc_semaphore(name="gapG"))

    def GF():
        _gap(G, gapG)

    for dst, src in [(ids, ident), (bout_s, boutv), (bxs, bxv), (bzs, bzv),
                     (bdts, bdtv), (convbs, convbv), (dvs, dvecv),
                     (convws, convwv), (a_s, alogv), (normws, normwv)]:
        dma(SY, "pre", dst, src)
    dma(SY, "pre", gs_cols[:, 0:1], s_in.broadcast_to([128, 1]))
    dma(SY, "pre", gs_cols[:, 1:2], s_ind.broadcast_to([128, 1]))

    V.memset(ones_s, 1.0)
    V.memset(hstA, 0.0)
    V.memset(hstB, 0.0)
    dwait(V, "pre")
    V.tensor_copy(idsb, ids)
    m_idsb = mark(V, "sV")

    dwait(S, "pre")
    S.activation(a_s, a_s, AF.Exp)
    m_aexp = mark(S, "sC")
    V.wait_ge(sems["sC"], m_aexp)
    V.tensor_scalar(a_s, a_s, -1.0, None, ALU.mult)
    m_aneg = mark(V, "sV")

    AR.push()
    gsl_b = AR.alloc(32 * 128)
    usl_b = AR.alloc(32 * 128)
    dsl_b = AR.alloc(8 * 512)
    q_amax = AR.alloc(256)
    q_sc8 = AR.alloc(256)
    q_rg = AR.alloc(256)
    q_scr = AR.alloc(4 * 256)
    q_a = AR.alloc(4096)
    q_asq = AR.alloc(4096)
    q_sgn = AR.alloc(4096)
    q_tA = AR.alloc(4096)
    q_tB = AR.alloc(4096)
    q_out = AR.alloc(2048, BF16)
    q_outT = AR.alloc(2048, BF16)

    dma(SY, "wq", gsl_b.rearrange("p (t d) -> p t d", d=128),
        gsl.rearrange("(t p) d -> p t d", p=128))
    dma(SY, "wq", usl_b.rearrange("p (t d) -> p t d", d=128),
        usl.rearrange("(t p) d -> p t d", p=128))
    dma(SY, "wq", dsl_b.rearrange("p (t d) -> p t d", d=512),
        dsl.rearrange("(t p) d -> p t d", p=128))

    dwait(V, "wq")
    for i, (tag, buf) in enumerate([("g", gsl_b), ("u", usl_b), ("d", dsl_b)]):
        V.tensor_reduce(q_scr[:, i:i + 1], buf, AXX, ALU.max,
                        apply_absolute_value=True)
    m_am = mark(V, "sV")
    SY.wait_ge(sems["sV"], m_am)
    dma(SY, "wr1", wred_dram, q_scr[:, 0:3])
    dwait(SY, "wr1")
    dma(SY, "wr2", wred_s, wred_dram.rearrange("a b -> (a b)").unsqueeze(0))
    dwait(V, "wr2")
    wred3 = wred_s.rearrange("o (p i) -> o p i", i=3)
    for i in range(3):
        V.tensor_reduce(wmax_s[:, i:i + 1], wred3[:, :, i], AXX, ALU.max)
    V.tensor_copy(wmax_s[:, 3:4], wmax_s[:, 0:1])
    m_wm = mark(V, "sV")

    G.wait_ge(sems["sV"], m_wm)
    dma(G, "gwm1", cc_wmax_in.ap(), wmax_s)
    dwait(G, "gwm1")
    G.collective_compute("AllReduce", ALU.max, replica_groups=ALL8,
                         ins=[cc_wmax_in.ap().opt()],
                         outs=[cc_wmax_out.ap().opt()])
    m_ccw = mark(G, "sCC")
    G.wait_ge(sems["sCC"], m_ccw)
    dma(G, "gwm2", gs_cols[:, 2:5], cc_wmax_out.ap()[0:1, 0:3].broadcast_to([128, 3]))

    dwait(V, "gwm2")
    V.reciprocal(gs_cols[:, 2:5], gs_cols[:, 2:5])
    VF()
    V.tensor_scalar(gs_cols[:, 2:5], gs_cols[:, 2:5], 2688.0, None, ALU.mult)
    VF()
    V.tensor_tensor(gs_cols[:, 5:6], gs_cols[:, 0:1], gs_cols[:, 2:3], ALU.mult)
    V.tensor_tensor(gs_cols[:, 6:7], gs_cols[:, 0:1], gs_cols[:, 3:4], ALU.mult)
    V.tensor_tensor(gs_cols[:, 7:8], gs_cols[:, 1:2], gs_cols[:, 4:5], ALU.mult)
    VF()
    V.reciprocal(gs_cols[:, 5:8], gs_cols[:, 5:8])
    V.tensor_scalar(gs_cols[:, 8:13], gs_cols[:, 0:5], 1.0 / 6.0, None, ALU.mult)
    m_scales = mark(V, "sV")

    def emit_e4m3(sc_ap, sc8_ap, scr_ap, nb):
        s_ = scr_ap[:, 0:nb]
        p_ = scr_ap[:, nb:2 * nb]
        t_ = scr_ap[:, 2 * nb:3 * nb]
        w_ = scr_ap[:, 3 * nb:4 * nb]
        VF()
        V.tensor_scalar(s_, sc_ap, 64.0, None, ALU.mult)
        VF()
        first = True
        for w in (8, 4, 2, 1):
            th = float(2.0 ** w)
            cm1 = float(2.0 ** w - 1.0)
            im1 = float(2.0 ** (-w) - 1.0)
            V.tensor_scalar(w_, s_, th, im1, ALU.is_ge, ALU.mult)
            V.tensor_scalar(t_, s_, th, cm1, ALU.is_ge, ALU.mult)
            V.scalar_tensor_tensor(s_, w_, 1.0, s_, ALU.add, ALU.mult)
            if first:
                V.tensor_scalar(p_, t_, 1.0, 0.015625, ALU.add, ALU.mult)
                first = False
            else:
                V.scalar_tensor_tensor(p_, t_, 1.0, p_, ALU.add, ALU.mult)
        V.tensor_scalar(t_, s_, 0.0625, None, ALU.is_ge)
        V.tensor_scalar(w_, s_, 0.1875, None, ALU.is_ge)
        for k in range(1, 8):
            V.scalar_tensor_tensor(t_, s_, (2 * k + 0.5) * 0.125, t_,
                                   ALU.is_ge, ALU.add)
            V.scalar_tensor_tensor(w_, s_, (2 * k + 1.5) * 0.125, w_,
                                   ALU.is_ge, ALU.add)
        VF()
        V.scalar_tensor_tensor(t_, w_, 1.0, t_, ALU.mult, ALU.add)
        VF()
        V.scalar_tensor_tensor(sc8_ap, t_, 0.125, p_, ALU.mult, ALU.mult)

    def emit_quant(x_ap, out_ap, nb, gsi, amax_ap, sc8_ap, rg_ap, scr_ap,
                   a_ap, asq_ap, sgn_ap, tA_ap, tB_ap, wait_v=None):
        FD = nb * 16
        gs_col = gs_cols[:, gsi:gsi + 1]
        gs6_col = gs_cols[:, gsi + 8:gsi + 9]
        x3 = x_ap.rearrange("p (b e) -> p b e", e=16)
        if wait_v is not None:
            V.wait_ge(*wait_v)
        V.tensor_reduce(amax_ap[:, 0:nb], x3, AXX, ALU.max, apply_absolute_value=True)
        VF()
        V.tensor_scalar(amax_ap[:, 0:nb], amax_ap[:, 0:nb], gs6_col, 448.0,
                        ALU.mult, ALU.min)
        emit_e4m3(amax_ap[:, 0:nb], sc8_ap[:, 0:nb], scr_ap, nb)
        VF()
        V.tensor_scalar(rg_ap[:, 0:nb], sc8_ap[:, 0:nb], 1e-30, None, ALU.max)
        VF()
        V.reciprocal(rg_ap[:, 0:nb], rg_ap[:, 0:nb])
        VF()
        V.tensor_scalar(rg_ap[:, 0:nb], rg_ap[:, 0:nb], gs_col, None, ALU.mult)
        VF()
        V.tensor_tensor(a_ap[:, 0:FD].rearrange("p (b e) -> p b e", e=16), x3,
                        rg_ap[:, 0:nb, None].broadcast_to([128, nb, 16]), ALU.mult)
        mv_a = mark(V, "sV")
        S.wait_ge(sems["sV"], mv_a)
        S.activation(asq_ap[:, 0:FD], a_ap[:, 0:FD], AF.Square)
        S.activation(sgn_ap[:, 0:FD], a_ap[:, 0:FD], AF.Sign)
        ms_sq = mark(S, "sC")
        V.wait_ge(sems["sC"], ms_sq)
        V.tensor_scalar(tA_ap[:, 0:FD], asq_ap[:, 0:FD], MIDSQ_A[0], None, ALU.is_ge)
        V.tensor_scalar(tB_ap[:, 0:FD], asq_ap[:, 0:FD], MIDSQ_B[0], None, ALU.is_ge)
        for thA, thB in zip(MIDSQ_A[1:], MIDSQ_B[1:]):
            V.scalar_tensor_tensor(tA_ap[:, 0:FD], asq_ap[:, 0:FD], thA,
                                   tA_ap[:, 0:FD], ALU.is_ge, ALU.add)
            V.scalar_tensor_tensor(tB_ap[:, 0:FD], asq_ap[:, 0:FD], thB,
                                   tB_ap[:, 0:FD], ALU.is_ge, ALU.add)
        VF()
        V.scalar_tensor_tensor(tA_ap[:, 0:FD], tA_ap[:, 0:FD], 0.5, tB_ap[:, 0:FD],
                               ALU.mult, ALU.add)
        VF()
        V.tensor_tensor(tA_ap[:, 0:FD], tA_ap[:, 0:FD], sgn_ap[:, 0:FD], ALU.mult)
        VF()
        V.tensor_tensor(out_ap.rearrange("p (b e) -> p b e", e=16),
                        tA_ap[:, 0:FD].rearrange("p (b e) -> p b e", e=16),
                        sc8_ap[:, 0:nb, None].broadcast_to([128, nb, 16]), ALU.mult)
        return mark(V, "sV")

    ag_marks = []
    ev_marks_all = []
    tr_bank_state = [None, None]
    for wi, (tag, buf, gci, NB, BW) in enumerate([
            ("g", gsl_b, 2, 32, 128), ("u", usl_b, 3, 32, 128),
            ("d", dsl_b, 4, 8, 512)]):
        if wi == 0:
            V.wait_ge(sems["sV"], m_scales)
        nb = 256
        mq = emit_quant(buf, q_out, nb, gci, q_amax, q_sc8,
                        q_rg, q_scr, q_a, q_asq, q_sgn, q_tA, q_tB)
        TN.wait_ge(sems["sV"], mq)
        if wi == 0:
            TN.wait_ge(sems["sV"], m_idsb)
        total = NB * (BW // 128)
        for grp in range(total // 8):
            pb = grp % 2
            if tr_bank_state[pb] is not None:
                TN.wait_ge(sems["sC"], tr_bank_state[pb])
            for j in range(8):
                blkidx = grp * 8 + j
                TN.transpose(bankb(pb)[:, j * 128:(j + 1) * 128],
                             q_out[:, blkidx * 128:(blkidx + 1) * 128], idsb)
            mp = mark(TN, "sP")
            S.wait_ge(sems["sP"], mp)
            S.copy(q_outT[:, grp * 1024:(grp + 1) * 1024], bankb(pb)[:, 0:1024])
            tr_bank_state[pb] = mark(S, "sC")
        SY.wait_ge(sems["sC"], tr_bank_state[(total // 8 - 1) % 2])
        if total // 8 >= 2:
            SY.wait_ge(sems["sC"], tr_bank_state[(total // 8 - 2) % 2])
        if tag == "g":
            dst = cc_ag_in.ap()[0, 0:WSEC].rearrange("(p h) -> p h", p=128)
        elif tag == "u":
            dst = cc_ag_in.ap()[0, WSEC:2 * WSEC].rearrange("(p h) -> p h", p=128)
        else:
            dst = cc_ag_in.ap()[0, 2 * WSEC:AGN].rearrange("(p c) -> p c", p=128)
        dma(SY, "agst", dst, q_outT[:, 0:4096])
        V.wait_ge(sems["sP"], C["sP"].val)
        dwait(V, "agst")

    m_p0q = mq
    dwait(G, "agst")
    G.collective_compute("AllGather", ALU.bypass, replica_groups=ALL8,
                         ins=[cc_ag_in.ap().opt()],
                         outs=[cc_ag_out.ap().opt()])
    m_ag = mark(G, "sCC")
    AR.pop()

    SY.wait_ge(sems["sV"], m_p0q)
    dwait(SY, "agst")
    AR.push()
    xc_b = [AR.alloc(4 * DM) for _ in range(2)]
    sqscr = AR.alloc(DM)
    m1a = {}
    for ci in range(NCH1):
        par = ci % 2
        if ci >= 2:
            SY.wait_ge(sems["sV"], m1a[f"v{ci - 2}"])
        dma(SY, f"x1a_{par}", xc_b[par].rearrange("p (a b) -> p a b", b=DM),
            x_full[ci * TC1:(ci + 1) * TC1].rearrange("(a p) b -> p a b", p=128))
        dwait(V, f"x1a_{par}")
        for tau in range(4):
            g = ci * 4 + tau
            xt = xc_b[par][:, tau * DM:(tau + 1) * DM]
            V.tensor_tensor(sqscr, xt, xt, ALU.mult)
            V.tensor_reduce(sum_all[:, g:g + 1], xt, AXX, ALU.add)
            V.tensor_reduce(ssq_all[:, g:g + 1], sqscr, AXX, ALU.add)
        m1a[f"v{ci}"] = mark(V, "sV")
        m1a[f"s{ci}"] = m1a[f"v{ci}"]
    CN = 32
    mm_ = sscr[:, 0:CN]; var_ = sscr[:, CN:2 * CN]; a1_ = sscr[:, 2 * CN:3 * CN]
    t1_ = sscr[:, 3 * CN:4 * CN]; t2_ = sscr[:, 4 * CN:5 * CN]; t3_ = sscr[:, 5 * CN:6 * CN]
    V.tensor_scalar(mm_, sum_all, 1.0 / DM, None, ALU.mult)
    V.tensor_scalar(t2_, ssq_all, 1.0 / DM, None, ALU.mult)
    V.tensor_tensor(t1_, mm_, mm_, ALU.mult)
    VF()
    V.tensor_tensor(var_, t2_, t1_, ALU.subtract)
    VF()
    V.tensor_scalar(a1_, var_, EPS, None, ALU.add)
    m_v1 = mark(V, "sV")
    S.wait_ge(sems["sV"], m_v1)
    S.activation(t1_, a1_, AF.Sqrt)
    m_s1 = mark(S, "sC")
    V.wait_ge(sems["sC"], m_s1)
    V.reciprocal(t3_, a1_)
    V.reciprocal(t2_, t1_)
    V.tensor_tensor(t3_, var_, t3_, ALU.mult)
    VF()
    V.tensor_scalar(t3_, t3_, EPS, None, ALU.add)
    m_v2 = mark(V, "sV")
    S.wait_ge(sems["sV"], m_v2)
    S.activation(t3_, t3_, AF.Sqrt)
    m_s2 = mark(S, "sC")
    V.wait_ge(sems["sC"], m_s2)
    V.reciprocal(t1_, t3_)
    VF()
    V.tensor_tensor(rho_all, t2_, t1_, ALU.mult)
    VF()
    V.scalar_tensor_tensor(nmr_all, mm_, -1.0, rho_all, ALU.mult, ALU.mult)
    m_stats = mark(V, "sV")
    if dbg:
        SY.wait_ge(sems["sV"], m_stats)
        dma(SY, "dbg", dbg_outs["d_rho"], rho_all)
        dma(SY, "dbg", dbg_outs["d_gs"], gs_cols)
        dma(SY, "dbg", dbg_outs["d_sums"][:, 0:32], sum_all)
        dma(SY, "dbg", dbg_outs["d_sums"][:, 32:64], ssq_all)
    AR.pop()

    SY.wait_ge(sems["sV"], m_stats)
    AR.push()
    wx_b = AR.alloc(8 * DI)
    wz_b = AR.alloc(8 * DI)
    wxp_b = AR.alloc(8 * 96)
    xc2_b = AR.alloc(4 * DM)
    xn_b = AR.alloc(4 * DM)
    xnT_b = AR.alloc(8 * TC1)
    xbrp_b = AR.alloc(8 * (TC1 + 3) + 8)
    xbr_b = AR.alloc(8 * TC1)
    sz_b = [AR.alloc(TC1) for _ in range(2)]
    sqp_b = [AR.alloc(TC1) for _ in range(2)]
    xdbl_b = AR.alloc(T)[0:96, :]

    wxv = wx_b.rearrange("p (k m2) -> p k m2", m2=DI)
    wzv = wz_b.rearrange("p (k m2) -> p k m2", m2=DI)
    wxpv = wxp_b.rearrange("p (k m2) -> p k m2", m2=96)
    xbrpv = xbrp_b[:, 0:8 * (TC1 + 3)].rearrange("p (m c) -> p m c", c=TC1 + 3)

    dma(SY, "w1", wxv, wxT.rearrange("(k p) m -> p k m", p=128))
    dma(SY, "w1", wzv, wzT.rearrange("(k p) m -> p k m", p=128))
    dma(SY, "w1", wxpv, wxpT.rearrange("(k p) m -> p k m", p=128))
    dwait(V, "w1")
    for k in range(8):
        V.tensor_scalar(wxv[:, k, :], wxv[:, k, :], normws[:, k:k + 1], None, ALU.mult)
        V.tensor_scalar(wzv[:, k, :], wzv[:, k, :], normws[:, k:k + 1], None, ALU.mult)
    m_wfold = mark(V, "sV")
    V.memset(xbrpv[:, :, 0:3], 0.0)
    m_convpad = mark(V, "sV")

    p1 = {}
    for ci in range(NCH1):
        if ci >= 1:
            SY.wait_ge(sems["sC"], p1[f"xn{ci - 1}"])
        dma(SY, "x1b", xc2_b.rearrange("p (a b) -> p a b", b=DM),
            x_full[ci * TC1:(ci + 1) * TC1].rearrange("(a p) b -> p a b", p=128))
        dwait(S, "x1b")
        if ci == 0:
            S.wait_ge(sems["sV"], m_stats)
        else:
            S.wait_ge(sems["sP"], p1[f"tr{ci - 1}"])
        for tau in range(4):
            g = ci * 4 + tau
            S.activation(xn_b[:, tau * DM:(tau + 1) * DM],
                         xc2_b[:, tau * DM:(tau + 1) * DM], AF.Identity,
                         bias=nmr_all[:, g:g + 1], scale=rho_all[:, g:g + 1])
        p1[f"xn{ci}"] = mark(S, "sC")
        TN.wait_ge(sems["sC"], p1[f"xn{ci}"])
        for k in range(8):
            if ci > 0 or k >= 2:
                wmark = p1[f"trev{ci}_{k - 2}"] if k >= 2 else p1[f"trev{ci - 1}_{6 + k}"]
                TN.wait_ge(sems["sC"], wmark)
            for tau in range(4):
                TN.transpose(bank(k % 2)[:, tau * 128:(tau + 1) * 128],
                             xn_b[:, tau * DM + k * 128: tau * DM + (k + 1) * 128],
                             ids)
            p1[f"trmm{ci}_{k}"] = mark(TN, "sP")
            S.wait_ge(sems["sP"], p1[f"trmm{ci}_{k}"])
            S.copy(xnT_b[:, k * TC1:(k + 1) * TC1], bank(k % 2))
            p1[f"trev{ci}_{k}"] = mark(S, "sC")
        p1[f"tr{ci}"] = p1[f"trmm{ci}_7"]
        TN.wait_ge(sems["sC"], p1[f"trev{ci}_7"])
        if ci == 0:
            TN.wait_ge(sems["sV"], m_wfold)
        for m in range(8):
            bwx = 2 + (m % 2)
            bwz = 4 + (m % 2)
            if ci > 0 or m >= 2:
                wm1 = p1.get(f"wxev{ci}_{m - 2}", p1.get(f"wxev{ci - 1}_{6 + m}"))
                TN.wait_ge(sems["sC"], wm1)
                wm2 = p1.get(f"szev{ci}_{m - 2}", p1.get(f"szev{ci - 1}_{6 + m}"))
                TN.wait_ge(sems["sC"], wm2)
            for k in range(8):
                TN.matmul(bank(bwx), wxv[:, k, m * 128:(m + 1) * 128],
                          xnT_b[:, k * TC1:(k + 1) * TC1],
                          start=(k == 0), stop=(k == 7))
            p1[f"wxmm{ci}_{m}"] = mark(TN, "sP")
            for k in range(8):
                TN.matmul(bank(bwz), wzv[:, k, m * 128:(m + 1) * 128],
                          xnT_b[:, k * TC1:(k + 1) * TC1],
                          start=(k == 0), stop=(k == 7))
            p1[f"wzmm{ci}_{m}"] = mark(TN, "sP")
            S.wait_ge(sems["sP"], p1[f"wxmm{ci}_{m}"])
            if ci == 0 and m == 0:
                S.wait_ge(sems["sV"], m_convpad)
            if ci > 0:
                S.wait_ge(sems["sV"], p1[f"halo{ci - 1}_{m}"])
            S.activation(xbrpv[:, m, 3:3 + TC1], bank(bwx), AF.Identity,
                         bias=bxs[:, m:m + 1])
            p1[f"wxev{ci}_{m}"] = mark(S, "sC")
            S.wait_ge(sems["sP"], p1[f"wzmm{ci}_{m}"])
            if ci > 0 or m >= 2:
                dwait(S, f"szst{m % 2}")
            szt = sz_b[m % 2]
            S.activation(szt, bank(bwz), AF.Silu, bias=bzs[:, m:m + 1])
            p1[f"szev{ci}_{m}"] = mark(S, "sC")
            SY.wait_ge(sems["sC"], p1[f"szev{ci}_{m}"])
            dma(SY, f"szst{m % 2}",
                sz_dram[m * 128:(m + 1) * 128, ci * TC1:(ci + 1) * TC1], szt)
            V.wait_ge(sems["sC"], p1[f"wxev{ci}_{m}"])
            if ci > 0:
                V.wait_ge(sems["sP"], p1[f"wxpmm{ci - 1}"])
                dwait(V, f"xbst{m % 2}")
            xbr_m = xbr_b[:, m * TC1:(m + 1) * TC1]
            accB = sqp_b[m % 2]
            V.tensor_scalar(xbr_m, xbrpv[:, m, 0:TC1], convws[:, 4 * m:4 * m + 1],
                            None, ALU.mult)
            V.tensor_scalar(accB, xbrpv[:, m, 1:1 + TC1],
                            convws[:, 4 * m + 1:4 * m + 2], None, ALU.mult)
            V.scalar_tensor_tensor(xbr_m, xbrpv[:, m, 2:2 + TC1],
                                   convws[:, 4 * m + 2:4 * m + 3],
                                   xbr_m, ALU.mult, ALU.add)
            V.scalar_tensor_tensor(accB, xbrpv[:, m, 3:3 + TC1],
                                   convws[:, 4 * m + 3:4 * m + 4],
                                   accB, ALU.mult, ALU.add)
            V.tensor_copy(xbrpv[:, m, 0:3], xbrpv[:, m, TC1:TC1 + 3])
            p1[f"halo{ci}_{m}"] = mark(V, "sV")
            V.tensor_tensor(xbr_m, xbr_m, accB, ALU.add)
            mconv = mark(V, "sV")
            S.wait_ge(sems["sV"], mconv)
            S.activation(xbr_m, xbr_m, AF.Silu, bias=convbs[:, m:m + 1])
            p1[f"xbr{ci}_{m}"] = mark(S, "sC")
            SY.wait_ge(sems["sC"], p1[f"xbr{ci}_{m}"])
            dma(SY, f"xbst{m % 2}",
                xbr_dram[m * 128:(m + 1) * 128, ci * TC1:(ci + 1) * TC1], xbr_m)
        TN.wait_ge(sems["sC"], p1[f"xbr{ci}_7"])
        if ci > 0:
            TN.wait_ge(sems["sC"], p1[f"wxpev{ci - 1}"])
        for k in range(8):
            TN.matmul(bank(6)[0:96, :], wxpv[:, k, :],
                      xbr_b[:, k * TC1:(k + 1) * TC1], start=(k == 0), stop=(k == 7))
        p1[f"wxpmm{ci}"] = mark(TN, "sP")
        S.wait_ge(sems["sP"], p1[f"wxpmm{ci}"])
        S.copy(xdbl_b[:, ci * TC1:(ci + 1) * TC1], bank(6)[0:96, :])
        p1[f"wxpev{ci}"] = mark(S, "sC")

    SY.wait_ge(sems["sC"], p1[f"wxpev{NCH1 - 1}"])
    dma(SY, "xdw", cc_xdbl_in.ap(), xdbl_b)
    dwait(G, "xdw")
    G.collective_compute("AllReduce", ALU.add, replica_groups=PAIRS,
                         ins=[cc_xdbl_in.ap().opt()],
                         outs=[cc_xdbl_out.ap().opt()])
    m_ccx = mark(G, "sCC")
    if dbg:
        SY.wait_ge(sems["sCC"], m_ccx)
        dma(SY, "dbg", dbg_outs["d_xdbl"], cc_xdbl_out.ap())
        dma(SY, "dbg", dbg_outs["d_xbr"], xbr_dram)
        dma(SY, "dbg", dbg_outs["d_sz"], sz_dram)
    AR.pop()

    barrier()
    AR.push()
    wout_b = AR.alloc(8 * DM)
    bc_bs = [AR.alloc(32 * TC2) for _ in range(2)]
    h_bs = [AR.alloc(NST * TC2) for _ in range(2)]
    da3_b = AR.alloc(NST * TC2)
    db3_b = AR.alloc(NST * TC2)
    dt_b = [AR.alloc(TC2) for _ in range(2)]
    dts_b = [AR.alloc(TC2) for _ in range(2)]
    u_b = AR.alloc(TC2)
    xbrr_b = [AR.alloc(TC2) for _ in range(4)]
    szr_b = [AR.alloc(TC2) for _ in range(4)]
    yacc_b = AR.alloc(TC2)
    yf_b = AR.alloc(TC2)
    p_b = AR.alloc(8 * TC2)
    aev_b = [AR.alloc(TC2) for _ in range(2)]
    wdt_b2 = AR.alloc(DI)
    dtl_bs = [AR.alloc(TC2) for _ in range(2)]

    woutv = wout_b.rearrange("p (k m2) -> p k m2", m2=DM)
    h3s = [h.rearrange("p (t n) -> p t n", n=NST) for h in h_bs]
    bc3s = [b.rearrange("p (n t) -> p n t", t=TC2) for b in bc_bs]
    cv3s = [b[:, NST * TC2:32 * TC2].rearrange("p (n t) -> p t n", t=TC2)
            for b in bc_bs]
    da3 = da3_b.rearrange("p (n t) -> p n t", t=TC2)
    db3 = db3_b.rearrange("p (n t) -> p n t", t=TC2)

    dma(SY, "w2", woutv, woutT.rearrange("(k p) m -> p k m", p=128))
    dma(SY, "w2", wdt_b2[0:64, :], wdtT)

    p2 = {}

    def bc_load(cj):
        cb = cj % 2
        cwj = slice(cj * TC2, (cj + 1) * TC2)
        if cj == 0:
            G.wait_ge(sems["sCC"], m_ccx)
        elif cj >= 2:
            G.wait_ge(sems["sV"], p2[f"pfin{cj - 2}_7"])
        dma(G, f"bc{cb}", bc3s[cb],
            cc_xdbl_out.ap()[64:96, cwj].unsqueeze(0).broadcast_to([128, 32, TC2]))
        if cj == 0:
            SY.wait_ge(sems["sCC"], m_ccx)
        elif cj >= 2:
            SY.wait_ge(sems["sP"], p2[f"dtmm{cj - 2}_7"])
        dma(SY, f"dtl{cb}", dtl_bs[cb][0:64, :], cc_xdbl_out.ap()[0:64, cwj])

    def emit_wout(cj, d0, d1):
        if d0 == 0:
            TN.wait_ge(sems["sV"], p2[f"pfin{cj}_7"])
        for dmt in range(d0, d1):
            pb = 1 + (dmt % 2)
            if cj > 0 or dmt >= 2:
                wm = p2.get(f"oev{cj}_{dmt - 2}", p2.get(f"oev{cj - 1}_{6 + dmt}"))
                TN.wait_ge(sems["sC"], wm)
            for k in range(8):
                TN.matmul(bank(pb)[:, 0:TC2], woutv[:, k, dmt * 128:(dmt + 1) * 128],
                          p_b[:, k * TC2:(k + 1) * TC2], start=(k == 0), stop=False)
            TN.matmul(bank(pb)[:, 0:TC2], bout_s[:, dmt * 128:(dmt + 1) * 128],
                      ones_s[:, 0:TC2], start=False, stop=True)
            p2[f"omm{cj}_{dmt}"] = mark(TN, "sP")
            S.wait_ge(sems["sP"], p2[f"omm{cj}_{dmt}"])
            if cj > 0 or dmt >= 2:
                dwait(S, f"ost{dmt % 2}")
            aev = aev_b[dmt % 2]
            S.copy(aev, bank(pb)[:, 0:TC2])
            p2[f"oev{cj}_{dmt}"] = mark(S, "sC")
            SY.wait_ge(sems["sC"], p2[f"oev{cj}_{dmt}"])
            hi = cj // 8
            g4 = (cj % 8) // 2
            tloc = (cj % 2) * TC2
            dma(SY, f"ost{dmt % 2}",
                cc_ar_in.ap()[g4, hi, dmt * 128:(dmt + 1) * 128,
                              tloc:tloc + TC2], aev)
        if d1 == 8 and cj >= 9 and cj % 2 == 1:
            g4 = (cj % 8) // 2
            G.wait_ge(sems["sC"], p2[f"oev{cj}_7"])
            dwait(G, "ost0")
            dwait(G, "ost1")
            G.collective_compute("ReduceScatter", ALU.add, replica_groups=PAIRS,
                                 ins=[cc_ar_in.ap()[g4].opt()],
                                 outs=[cc_ar_out.ap()[g4].opt()])
            p2[f"ccar{g4}"] = mark(G, "sCC")

    bc_load(0)

    def epi_hc(eidx):
        eci = eidx // 8
        V.tensor_tensor(h3s[eidx % 2], h3s[eidx % 2], cv3s[eci % 2], ALU.mult)

    def epi_reduce(eidx):
        V.tensor_reduce(yacc_b, h3s[eidx % 2], AXX, ALU.add)

    def epi_yfull(eidx):
        eci, em = eidx // 8, eidx % 8
        V.scalar_tensor_tensor(yf_b, xbrr_b[eidx % 4], dvs[:, em:em + 1],
                               yacc_b, ALU.mult, ALU.add)

    def epi_p(eidx):
        eci, em = eidx // 8, eidx % 8
        if em == 0 and eci > 0:
            V.wait_ge(sems["sP"], p2[f"omm{eci - 1}_7"])
        V.tensor_tensor(p_b[:, em * TC2:(em + 1) * TC2], yf_b,
                        szr_b[eidx % 4], ALU.mult)
        p2[f"pfin{eci}_{em}"] = mark(V, "sV")

    for ci in range(NCH2):
        cbuf = ci % 2
        cw = slice(ci * TC2, (ci + 1) * TC2)
        hin = hstA if ci % 2 == 0 else hstB
        hout = hstB if ci % 2 == 0 else hstA
        for m in range(8):
            if m == 2 and ci + 1 < NCH2:
                bc_load(ci + 1)
            par = m % 2
            idx = ci * 8 + m
            eidx = idx - 2
            tb = idx % 2
            qb = idx % 4
            if idx >= 4:
                qci, qm = (idx - 4) // 8, (idx - 4) % 8
                SY.wait_ge(sems["sV"], p2[f"pfin{qci}_{qm}"])
            dma(SY, f"ld2_{qb}", xbrr_b[qb], xbr_dram[m * 128:(m + 1) * 128, cw])
            dma(SY, f"ld2_{qb}", szr_b[qb], sz_dram[m * 128:(m + 1) * 128, cw])
            if m == 0:
                dwait(TN, f"dtl{cbuf}")
            if ci == 0 and m == 0:
                dwait(TN, "w2")
            if idx >= 1:
                TN.wait_ge(sems["sC"], p2[f"dtev{idx - 1}"])
            TN.matmul(bank(0)[:, 0:TC2], wdt_b2[0:64, m * 128:(m + 1) * 128],
                      dtl_bs[cbuf][0:64, :], start=True, stop=True)
            p2[f"dtmm{ci}_{m}"] = mark(TN, "sP")
            S.wait_ge(sems["sP"], p2[f"dtmm{ci}_{m}"])
            if idx >= 2:
                S.wait_ge(sems["sV"], p2[f"hout{idx - 2}"])
            S.activation(dts_b[par], bank(0)[:, 0:TC2], AF.Exp, bias=bdts[:, m:m + 1])
            SF()
            S.activation(dt_b[par], dts_b[par], AF.Ln, bias=1.0)
            SF()
            p2[f"dtev{idx}"] = mark(S, "sC")
            if idx == 0:
                S.wait_ge(sems["sV"], m_aneg)
            if idx >= 1:
                S.wait_ge(sems["sV"], p2[f"hout{idx - 1}"])
            for n in range(NST):
                S.activation(da3[:, n, :], dt_b[par], AF.Exp,
                             scale=a_s[:, m * NST + n:m * NST + n + 1])
            p2[f"da{idx}"] = mark(S, "sC")
            if ci > 0 and m == 2:
                emit_wout(ci - 1, 4, 8)
            V.wait_ge(sems["sC"], p2[f"dtev{idx}"])
            dwait(V, f"ld2_{qb}")
            if m == 0:
                dwait(V, f"bc{cbuf}")
            V.tensor_tensor(u_b, dt_b[par], xbrr_b[qb], ALU.mult)
            if eidx >= 0:
                epi_hc(eidx)
            else:
                VF()
            V.tensor_tensor(db3, u_b[:, None, 0:TC2].broadcast_to([128, NST, TC2]),
                            bc3s[cbuf][:, 0:NST, :], ALU.mult)
            if eidx >= 0:
                epi_reduce(eidx)
            else:
                VF()
            V.wait_ge(sems["sC"], p2[f"da{idx}"])
            V.tensor_tensor_scan(h3s[tb][:, :, 0], da3[:, 0, :], db3[:, 0, :],
                                 hin[:, m * NST:m * NST + 1], ALU.mult, ALU.add)
            if eidx >= 0:
                epi_yfull(eidx)
            V.tensor_tensor_scan(h3s[tb][:, :, 1], da3[:, 1, :], db3[:, 1, :],
                                 hin[:, m * NST + 1:m * NST + 2], ALU.mult, ALU.add)
            if eidx >= 0:
                epi_p(eidx)
            for n in range(2, NST):
                V.tensor_tensor_scan(h3s[tb][:, :, n], da3[:, n, :], db3[:, n, :],
                                     hin[:, m * NST + n:m * NST + n + 1],
                                     ALU.mult, ALU.add)
            V.tensor_copy(hout[:, m * NST:m * NST + 14], h3s[tb][:, TC2 - 1, 0:14])
            V.tensor_copy(hout[:, m * NST + 14:m * NST + 16],
                          h3s[tb][:, TC2 - 1, 14:16])
            p2[f"hout{idx}"] = mark(V, "sV")
            if ci > 0 and m == 1:
                emit_wout(ci - 1, 0, 4)
        del cw
    for eidx in (NCH2 * 8 - 2, NCH2 * 8 - 1):
        epi_hc(eidx)
        VF()
        epi_reduce(eidx)
        VF()
        epi_yfull(eidx)
        VF()
        epi_p(eidx)
    emit_wout(NCH2 - 1, 0, 8)

    if dbg:
        SY.wait_ge(sems["sCC"], p2["ccar3"])
        dma(SY, "dbg", dbg_outs["d_ar"].rearrange("d (g t) -> g d t", t=TC1),
            cc_ar_out.ap())
    AR.pop()
    barrier()
    AR.push()
    xs_dm = AR.alloc(8 * 16 * 64, BF16)
    xsv = xs_dm.rearrange("p (k g2 d) -> p k g2 d", g2=16, d=128)

    AR.push()
    at_b = AR.alloc(8 * TC1)
    atT_b = AR.alloc(DM)
    xo_b = AR.alloc(4 * DM)
    x2_b = AR.alloc(DM)
    apl_b = AR.alloc(DM)
    sq2 = AR.alloc(DM)
    st3 = AR.alloc(16)
    f_amax = AR.alloc(64)
    f_sc8 = AR.alloc(64)
    f_rg = AR.alloc(64)
    f_scr = AR.alloc(4 * 64)
    f_a = AR.alloc(DM)
    f_asq = AR.alloc(DM)
    f_sgn = AR.alloc(DM)
    f_tA = AR.alloc(DM)
    f_tB = AR.alloc(DM)
    xs_t = AR.alloc(DM // 2, BF16)

    atv = at_b.rearrange("p (k t) -> p k t", t=TC1)

    def ln_stats_tile(x_ap, rho_c, nmr_c, pre_v=None, pre_s=None):
        if pre_v is not None:
            V.wait_ge(*pre_v)
        V.tensor_tensor(sq2, x_ap, x_ap, ALU.mult)
        V.tensor_reduce(st3[:, 0:1], x_ap, AXX, ALU.add)
        V.tensor_reduce(st3[:, 1:2], sq2, AXX, ALU.add)
        a_ = st3[:, 2:3]; t_ = st3[:, 3:4]; m_ = st3[:, 4:5]
        V.tensor_scalar(m_, st3[:, 0:1], 1.0 / DM, None, ALU.mult)
        V.tensor_scalar(t_, st3[:, 1:2], 1.0 / DM, None, ALU.mult)
        V.tensor_tensor(a_, m_, m_, ALU.mult)
        VF()
        V.tensor_tensor(a_, t_, a_, ALU.subtract)
        VF()
        V.tensor_scalar(a_, a_, EPS, None, ALU.add)
        mv2 = mark(V, "sV")
        S.wait_ge(sems["sV"], mv2)
        S.activation(t_, a_, AF.Sqrt)
        ms2 = mark(S, "sC")
        V.wait_ge(sems["sC"], ms2)
        V.reciprocal(rho_c, t_)
        VF()
        V.scalar_tensor_tensor(nmr_c, m_, -1.0, rho_c, ALU.mult, ALU.mult)
        return mark(V, "sV")

    p3 = {}
    prev_tile_v = None
    prev_xtr = None
    for grp in range(4):
        if grp == 0:
            SY.wait_ge(sems["sCC"], p2["ccar0"])
        else:
            SY.wait_ge(sems["sCC"], p2[f"ccar{grp}"])
            SY.wait_ge(sems["sC"], p3[f"atT{grp * 4 - 1}"])
            SY.wait_ge(sems["sV"], prev_tile_v)
        dma(SY, "ld3a", atv,
            cc_ar_out.ap()[grp].rearrange("(k p) t -> p k t", p=128))
        dma(SY, "ld3a", xo_b.rearrange("p (a b) -> p a b", b=DM),
            x_own[grp * TC1:(grp + 1) * TC1].rearrange("(a p) b -> p a b", p=128))
        for tau in range(4):
            g = grp * 4 + tau
            xo_t = xo_b[:, tau * DM:(tau + 1) * DM]
            dwait(TN, "ld3a")
            if prev_tile_v is not None:
                TN.wait_ge(sems["sV"], prev_tile_v)
            for k in range(8):
                TN.transpose(PS[:, k * 128:(k + 1) * 128],
                             atv[:, k, tau * 128:(tau + 1) * 128], ids)
            m_attr = mark(TN, "sP")
            S.wait_ge(sems["sP"], m_attr)
            if prev_tile_v is not None:
                S.wait_ge(sems["sV"], prev_tile_v)
            S.copy(atT_b, PS[:, 0:1024])
            p3[f"atT{g}"] = mark(S, "sC")
            if tau == 0:
                dwait(V, "ld3a")
            mln1 = ln_stats_tile(xo_t, st3[:, 8:9], st3[:, 9:10])
            if dbg and g == 0:
                SY.wait_ge(sems["sV"], mln1)
                dma(SY, "dbg", dbg_outs["d_st3"], st3)
                dma(SY, "dbg", dbg_outs["d_xo"], xo_t)
                dwait(V, "dbg")
                dwait(S, "dbg")
            S.wait_ge(sems["sV"], mln1)
            S.activation(apl_b, xo_t, AF.Identity, bias=st3[:, 9:10],
                         scale=st3[:, 8:9])
            ms_lnx = mark(S, "sC")
            if dbg:
                SY.wait_ge(sems["sC"], p3[f"atT{g}"])
                dma(SY, "dbg", dbg_outs["d_aret"][g * 128:(g + 1) * 128, :], atT_b)
                SY.wait_ge(sems["sC"], ms_lnx)
                dma(SY, "dbg", dbg_outs["d_apl"][g * 128:(g + 1) * 128, :], apl_b)
                dwait(V, "dbg")
                dwait(S, "dbg")
            V.wait_ge(sems["sC"], ms_lnx)
            V.wait_ge(sems["sC"], p3[f"atT{g}"])
            V.tensor_tensor(atT_b, atT_b, apl_b, ALU.add)
            m_ar_ = mark(V, "sV")
            mln2 = ln_stats_tile(atT_b, st3[:, 10:11], st3[:, 11:12],
                                 pre_v=(sems["sV"], m_ar_),
                                 pre_s=(sems["sV"], m_ar_))
            S.wait_ge(sems["sV"], mln2)
            S.activation(apl_b, atT_b, AF.Identity, bias=st3[:, 11:12],
                         scale=st3[:, 10:11])
            ms_ln2 = mark(S, "sC")
            V.wait_ge(sems["sC"], ms_ln2)
            if g > 0:
                dwait(V, "x2st")
            V.tensor_tensor(x2_b, xo_t, apl_b, ALU.add)
            m_x2 = mark(V, "sV")
            SY.wait_ge(sems["sV"], m_x2)
            dma(SY, "x2st", x2_dram[g * 128:(g + 1) * 128, :], x2_b)
            mln3 = ln_stats_tile(x2_b, st3[:, 12:13], st3[:, 13:14],
                                 pre_v=(sems["sV"], m_x2),
                                 pre_s=(sems["sV"], m_x2))
            S.wait_ge(sems["sV"], mln3)
            S.activation(apl_b, x2_b, AF.Identity, bias=st3[:, 13:14],
                         scale=st3[:, 12:13])
            ms_ln3 = mark(S, "sC")
            if prev_xtr is not None:
                V.wait_ge(sems["sP"], prev_xtr)
            mq = emit_quant(apl_b, xs_t, 64, 0, f_amax, f_sc8,
                            f_rg, f_scr, f_a, f_asq, f_sgn, f_tA, f_tB,
                            wait_v=(sems["sC"], ms_ln3))
            prev_tile_v = mq
            TN.wait_ge(sems["sV"], mq)
            if g > 0:
                TN.wait_ge(sems["sC"], p3[f"xsg{g - 1}"])
            for k in range(8):
                TN.transpose(bankb(2)[:, k * 128:(k + 1) * 128],
                             xs_t[:, k * 128:(k + 1) * 128], idsb)
            prev_xtr = mark(TN, "sP")
            S.wait_ge(sems["sP"], prev_xtr)
            S.copy(xsv[:, :, g, :], bankb(2)[:, 0:1024].rearrange("p (k d) -> p k d", d=128))
            p3[f"xsg{g}"] = mark(S, "sC")
    m_xs_all = p3["xsg15"]
    AR.pop()

    AR.push()
    gws_b = AR.alloc(8 * 512, BF16)
    ups_b = AR.alloc(8 * 512, BF16)
    dws_b = AR.alloc(8 * 512, BF16)
    gb_b = AR.alloc(1024)
    ub_b = AR.alloc(1024)
    hid_b = AR.alloc(1024)
    h_amax = AR.alloc(64)
    h_sc8 = AR.alloc(64)
    h_rg = AR.alloc(64)
    h_scr = AR.alloc(4 * 64)
    h_a = AR.alloc(1024)
    h_asq = AR.alloc(1024)
    h_sgn = AR.alloc(1024)
    h_tA = AR.alloc(1024)
    h_tB = AR.alloc(1024)
    hs_b = AR.alloc(512, BF16)
    hsT_b = AR.alloc(512, BF16)
    do_b = [AR.alloc(DM) for _ in range(2)]

    gwv = gws_b.rearrange("p (k h) -> p k h", h=1024)
    upv = ups_b.rearrange("p (k h) -> p k h", h=1024)
    dwv = dws_b.rearrange("p (j d) -> p j d", d=1024)
    hstv = hsT_b.rearrange("p (j d) -> p j d", d=128)

    for hq in range(4):
        if hq == 0:
            SY.wait_ge(sems["sCC"], m_ag)
        else:
            SY.wait_ge(sems["sP"], p3[f"dmm{hq - 1}_15"])
        for k in range(8):
            sec = cc_ag_out.ap()[k, 0:WSEC].rearrange("(p h) -> p h", p=128)
            dma(SY, "w3", gwv[:, k, :], sec[:, hq * 1024:(hq + 1) * 1024])
            sec2 = cc_ag_out.ap()[k, WSEC:2 * WSEC].rearrange("(p h) -> p h", p=128)
            dma(SY, "w3", upv[:, k, :], sec2[:, hq * 1024:(hq + 1) * 1024])
        for j in range(8):
            gr = hq * 1024 + j * 128
            rk = gr // 512
            jloc = (gr % 512) // 128
            sec3 = cc_ag_out.ap()[rk, 2 * WSEC:AGN].rearrange(
                "(p i j2 d) -> p i j2 d", p=128, i=8, j2=4)
            dma(SY, "w3", dwv[:, j, :].rearrange("p (i d) -> p i d", d=128),
                sec3[:, :, jloc, :])
        def emit_gu(g):
            if g == 0:
                dwait(TN, "w3")
            if hq == 0 and g == 0:
                TN.wait_ge(sems["sC"], m_xs_all)
            for s2 in range(2):
                if hq > 0 or g > 0:
                    wm = p3.get(f"gev{hq}_{g - 1}_{s2}", p3.get(f"gev{hq - 1}_15_{s2}"))
                    TN.wait_ge(sems["sC"], wm)
                for k in range(8):
                    TN.matmul(bank(s2), xsv[:, k, g, :],
                              gwv[:, k, s2 * 512:(s2 + 1) * 512],
                              start=(k == 0), stop=(k == 7))
                m_gmm = mark(TN, "sP")
                if hq > 0 or g > 0:
                    wm = p3.get(f"uev{hq}_{g - 1}_{s2}", p3.get(f"uev{hq - 1}_15_{s2}"))
                    TN.wait_ge(sems["sC"], wm)
                for k in range(8):
                    TN.matmul(bank(2 + s2), xsv[:, k, g, :],
                              upv[:, k, s2 * 512:(s2 + 1) * 512],
                              start=(k == 0), stop=(k == 7))
                m_umm = mark(TN, "sP")
                S.wait_ge(sems["sP"], m_gmm)
                if hq > 0 or g > 0:
                    S.wait_ge(sems["sV"], p3.get(f"hid{hq}_{g - 1}",
                                                 p3.get(f"hid{hq - 1}_15")))
                S.activation(gb_b[:, s2 * 512:(s2 + 1) * 512], bank(s2), AF.Silu,
                             scale=gs_cols[:, 5:6])
                p3[f"gev{hq}_{g}_{s2}"] = mark(S, "sC")
                S.wait_ge(sems["sP"], m_umm)
                S.activation(ub_b[:, s2 * 512:(s2 + 1) * 512], bank(2 + s2),
                             AF.Identity, scale=gs_cols[:, 6:7])
                p3[f"uev{hq}_{g}_{s2}"] = mark(S, "sC")

        for g in range(16):
            if g == 0:
                emit_gu(0)
            V.wait_ge(sems["sC"], p3[f"uev{hq}_{g}_1"])
            V.wait_ge(sems["sC"], p3[f"gev{hq}_{g}_1"])
            V.tensor_tensor(hid_b, gb_b, ub_b, ALU.mult)
            p3[f"hid{hq}_{g}"] = mark(V, "sV")
            if hq > 0 or g > 0:
                V.wait_ge(sems["sP"], p3.get(f"htr{hq}_{g - 1}",
                                             p3.get(f"htr{hq - 1}_15")))
            mqh = emit_quant(hid_b, hs_b, 64, 1, h_amax, h_sc8,
                             h_rg, h_scr, h_a, h_asq, h_sgn, h_tA, h_tB,
                             wait_v=(sems["sV"], p3[f"hid{hq}_{g}"]))
            if g + 1 < 16:
                emit_gu(g + 1)
            TN.wait_ge(sems["sV"], mqh)
            if hq > 0 or g > 0:
                TN.wait_ge(sems["sC"], p3.get(f"hsT{hq}_{g - 1}",
                                              p3.get(f"hsT{hq - 1}_15")))
            for j in range(8):
                TN.transpose(bankb(6)[:, j * 128:(j + 1) * 128],
                             hs_b[:, j * 128:(j + 1) * 128], idsb)
            p3[f"htr{hq}_{g}"] = mark(TN, "sP")
            S.wait_ge(sems["sP"], p3[f"htr{hq}_{g}"])
            S.copy(hsT_b, bankb(6)[:, 0:1024])
            p3[f"hsT{hq}_{g}"] = mark(S, "sC")
            TN.wait_ge(sems["sC"], p3[f"hsT{hq}_{g}"])
            if hq > 0 or g > 0:
                wm = p3.get(f"dev{hq}_{g - 1}", p3.get(f"dev{hq - 1}_15"))
                TN.wait_ge(sems["sC"], wm)
            for s2 in range(2):
                for j in range(8):
                    TN.matmul(bank(4 + s2), hstv[:, j, :],
                              dwv[:, j, s2 * 512:(s2 + 1) * 512],
                              start=(j == 0), stop=(j == 7))
            p3[f"dmm{hq}_{g}"] = mark(TN, "sP")
            dpar = g % 2
            S.wait_ge(sems["sP"], p3[f"dmm{hq}_{g}"])
            if hq > 0 or g >= 2:
                dwait(S, f"dst{dpar}")
            S.activation(do_b[dpar][:, 0:512], bank(4), AF.Identity,
                         scale=gs_cols[:, 7:8])
            S.activation(do_b[dpar][:, 512:1024], bank(5), AF.Identity,
                         scale=gs_cols[:, 7:8])
            p3[f"dev{hq}_{g}"] = mark(S, "sC")
            G.wait_ge(sems["sC"], p3[f"dev{hq}_{g}"])
            if hq == 0:
                dma(G, f"dst{dpar}", dacc_dram[g * 128:(g + 1) * 128, :], do_b[dpar])
            else:
                dma(G, f"dst{dpar}", dacc_dram[g * 128:(g + 1) * 128, :], do_b[dpar],
                    accum_op=ALU.add)
    AR.pop()
    AR.pop()

    barrier()
    AR.push()
    da_in = [AR.alloc(DM) for _ in range(2)]
    x2_in = [AR.alloc(DM) for _ in range(2)]
    o_b = [AR.alloc(DM) for _ in range(2)]
    sq3 = AR.alloc(DM)
    st4 = AR.alloc(16)
    mo = {}
    for g in range(16):
        par = g % 2
        if g >= 2:
            SY.wait_ge(sems["sV"], mo[f"o{g - 2}"])
        dma(SY, f"ld3c_{par}", da_in[par], dacc_dram[g * 128:(g + 1) * 128, :])
        dma(SY, f"ld3c_{par}", x2_in[par], x2_dram[g * 128:(g + 1) * 128, :])
        dwait(V, f"ld3c_{par}")
        V.tensor_tensor(sq3, da_in[par], da_in[par], ALU.mult)
        V.tensor_reduce(st4[:, 0:1], da_in[par], AXX, ALU.add)
        V.tensor_reduce(st4[:, 1:2], sq3, AXX, ALU.add)
        a_ = st4[:, 2:3]; t_ = st4[:, 3:4]; m_ = st4[:, 4:5]
        V.tensor_scalar(m_, st4[:, 0:1], 1.0 / DM, None, ALU.mult)
        V.tensor_scalar(t_, st4[:, 1:2], 1.0 / DM, None, ALU.mult)
        V.tensor_tensor(a_, m_, m_, ALU.mult)
        VF()
        V.tensor_tensor(a_, t_, a_, ALU.subtract)
        VF()
        V.tensor_scalar(a_, a_, EPS, None, ALU.add)
        mv2 = mark(V, "sV")
        S.wait_ge(sems["sV"], mv2)
        S.activation(t_, a_, AF.Sqrt)
        ms2 = mark(S, "sC")
        V.wait_ge(sems["sC"], ms2)
        V.reciprocal(st4[:, 5:6], t_)
        VF()
        V.scalar_tensor_tensor(st4[:, 6:7], m_, -1.0, st4[:, 5:6], ALU.mult, ALU.mult)
        mv3 = mark(V, "sV")
        S.wait_ge(sems["sV"], mv3)
        S.activation(o_b[par], da_in[par], AF.Identity, bias=st4[:, 6:7],
                     scale=st4[:, 5:6])
        ms3 = mark(S, "sC")
        V.wait_ge(sems["sC"], ms3)
        V.tensor_tensor(o_b[par], o_b[par], x2_in[par], ALU.add)
        mo[f"o{g}"] = mark(V, "sV")
        SY.wait_ge(sems["sV"], mo[f"o{g}"])
        dma(SY, "outst", out_own[g * 128:(g + 1) * 128, :], o_b[par])
    if dbg:
        dma(SY, "dbg", dbg_outs["d_x2"], x2_dram)
        dma(SY, "dbg", dbg_outs["d_dacc"], dacc_dram)
    for fam in list(dsems):
        dwait(SY, fam)
    AR.pop()

    with nc.Block() as blk:
        for n in ENGS:
            getattr(blk, n)(lambda eng, n=n: R[n].replay(eng))

    counts = {n: len(R[n].items) for n in ENGS}
    counts["arena_hi"] = AR.hi
    return nc, counts


def make_in_maps(inputs):
    x = np.ascontiguousarray(inputs["L0_attn_x"], dtype=np.float32)
    Wx = np.asarray(inputs["L0_attn_Wx"])
    Wz = np.asarray(inputs["L0_attn_Wz"])
    Wxp = np.asarray(inputs["L0_attn_Wxp"])
    Wdt = np.asarray(inputs["L0_attn_Wdt"])
    Wout = np.asarray(inputs["L0_attn_Wout"])
    gw = np.asarray(inputs["L0_ffn_gate_w"])
    uw = np.asarray(inputs["L0_ffn_up_w"])
    dw = np.asarray(inputs["L0_ffn_down_w"])

    def col8(vec):
        return np.ascontiguousarray(np.asarray(vec).reshape(8, 128).T, np.float32)

    maps = []
    for c in range(8):
        b, hf = c // 2, c % 2
        sl = slice(hf * DI, (hf + 1) * DI)
        m = {
            "x_full": np.ascontiguousarray(x[b]),
            "x_own": np.ascontiguousarray(x[b, hf * TOWN:(hf + 1) * TOWN]),
            "wxT": np.ascontiguousarray(Wx[sl].T, np.float32),
            "wzT": np.ascontiguousarray(Wz[sl].T, np.float32),
            "wxpT": np.ascontiguousarray(Wxp[:, sl].T, np.float32),
            "wdtT": np.ascontiguousarray(Wdt[sl].T, np.float32),
            "woutT": np.ascontiguousarray(Wout[:, sl].T, np.float32),
            "bxv": col8(np.asarray(inputs["L0_attn_bx"])[sl]),
            "bzv": col8(np.asarray(inputs["L0_attn_bz"])[sl]),
            "bdtv": col8(np.asarray(inputs["L0_attn_bdt"])[sl]),
            "convbv": col8(np.asarray(inputs["L0_attn_conv_b"])[sl]),
            "dvecv": col8(np.asarray(inputs["L0_attn_D"])[sl]),
            "convwv": np.ascontiguousarray(
                np.asarray(inputs["L0_attn_conv_w"])[sl, 0, :].reshape(8, 128, 4)
                .transpose(1, 0, 2).reshape(128, 32), np.float32),
            "alogv": np.ascontiguousarray(
                np.asarray(inputs["L0_attn_A_log"])[sl].reshape(8, 128, NST)
                .transpose(1, 0, 2).reshape(128, 8 * NST), np.float32),
            "boutv": (np.asarray(inputs["L0_attn_bout"], np.float32).reshape(1, DM)
                      if hf == 0 else np.zeros((1, DM), np.float32)),
            "normwv": col8(np.asarray(inputs["L0_attn_norm_w"])),
            "gsl": np.ascontiguousarray(gw[:, c * 128:(c + 1) * 128], np.float32),
            "usl": np.ascontiguousarray(uw[:, c * 128:(c + 1) * 128], np.float32),
            "dsl": np.ascontiguousarray(dw[:, c * 512:(c + 1) * 512], np.float32),
            "s_in": np.asarray(inputs["L0_ffn_s_in"], np.float32).reshape(1, 1),
            "s_ind": np.asarray(inputs["L0_ffn_s_ind" if "L0_ffn_s_ind" in inputs
                                       else "L0_ffn_s_in_down"], np.float32).reshape(1, 1),
            "ident": np.eye(128, dtype=np.float32),
        }
        maps.append(m)
    return maps


def kernel(**inputs):
    nc, counts = build_program(dbg=False)
    maps = make_in_maps(inputs)
    res = run_bass_kernel_spmd(nc, maps, core_ids=list(range(8)))
    out = np.empty((4, T, DM), np.float32)
    for c in range(8):
        b, hf = c // 2, c % 2
        out[b, hf * TOWN:(hf + 1) * TOWN] = res.results[c]["out_own"]
    return out


if __name__ == "__main__":
    nc, counts = build_program(dbg=False)
    print("built:", counts)




# version 06
# speedup vs baseline: 1.0711x; 1.0711x over previous
import numpy as np

import concourse.bass as bass
import concourse.mybir as mybir
from concourse.bass_utils import run_bass_kernel_spmd

F32 = mybir.dt.float32
BF16 = mybir.dt.bfloat16
AF = mybir.ActivationFunctionType
ALU = mybir.AluOpType
AXX = mybir.AxisListType.X

DM = 1024
DI = 1024
T = 4096
TOWN = 2048
NST = 16
TC1 = 512
NCH1 = T // TC1
TC2 = 256
NCH2 = T // TC2
EPS = 1e-5
PAIRS = [[0, 1], [2, 3], [4, 5], [6, 7]]
ALL8 = [[0, 1, 2, 3, 4, 5, 6, 7]]
MIDSQ_A = [0.0625, 0.5625, 1.5625, 3.0625]
MIDSQ_B = [6.25, 12.25, 25.0, 25.0]
WSEC = 128 * 4096
DSEC = 512 * 1024
AGN = 2 * WSEC + DSEC
ENGS = ("tensor", "vector", "scalar", "gpsimd", "sync")


class Rec:
    class _Inst:
        def __init__(self, item):
            self.item = item

        def then_inc(self, sem, val):
            self.item[3].append((sem, val))
            return self

    def __init__(self, name):
        self.name = name
        self.items = []

    def __getattr__(self, meth):
        if meth.startswith("_"):
            raise AttributeError(meth)

        def call(*args, **kwargs):
            item = (meth, args, kwargs, [])
            self.items.append(item)
            return Rec._Inst(item)
        return call

    def replay(self, eng):
        for meth, args, kwargs, incs in self.items:
            inst = getattr(eng, meth)(*args, **kwargs)
            for sem, val in incs:
                inst.then_inc(sem, val)


class Ctr:
    def __init__(self, sem):
        self.sem = sem
        self.val = 0

    def bump(self, n=1):
        self.val += n
        return self.val


class Arena:
    def __init__(self, tensor, cols):
        self.t = tensor
        self.cols = cols
        self.off = 0
        self.marks = []
        self.hi = 0

    def alloc(self, ncols, dtype=F32):
        assert self.off + ncols <= self.cols, \
            f"arena overflow {self.off}+{ncols}>{self.cols}"
        ap = self.t[:, self.off:self.off + ncols]
        self.off += ncols
        self.hi = max(self.hi, self.off)
        if dtype == BF16:
            ap = ap.bitcast(BF16)
        return ap

    def push(self):
        self.marks.append(self.off)

    def pop(self):
        self.off = self.marks.pop()


def build_program(dbg=False):
    nc = bass.Bass("TRN2", target_bir_lowering=False, debug=False,
                   detect_race_conditions=False)

    def inp(name, shape, dtype=F32):
        return nc.dram_tensor(name, shape, dtype, kind="ExternalInput").ap()

    x_full = inp("x_full", [T, DM])
    x_own = inp("x_own", [TOWN, DM])
    wxT = inp("wxT", [DM, DI])
    wzT = inp("wzT", [DM, DI])
    wxpT = inp("wxpT", [DI, 96])
    wdtT = inp("wdtT", [64, DI])
    woutT = inp("woutT", [DI, DM])
    bxv = inp("bxv", [128, 8])
    bzv = inp("bzv", [128, 8])
    bdtv = inp("bdtv", [128, 8])
    convbv = inp("convbv", [128, 8])
    dvecv = inp("dvecv", [128, 8])
    convwv = inp("convwv", [128, 32])
    alogv = inp("alogv", [128, 8 * NST])
    boutv = inp("boutv", [1, DM])
    normwv = inp("normwv", [128, 8])
    gsl = inp("gsl", [4096, 128])
    usl = inp("usl", [4096, 128])
    dsl = inp("dsl", [DM, 512])
    s_in = inp("s_in", [1, 1])
    s_ind = inp("s_ind", [1, 1])
    ident = inp("ident", [128, 128])

    out_own = nc.dram_tensor("out_own", [TOWN, DM], F32, kind="ExternalOutput").ap()
    dbg_outs = {}
    if dbg:
        for nm, shp, dt_ in [("d_xbr", [DI, T], F32), ("d_sz", [DI, T], F32),
                             ("d_xdbl", [96, T], F32), ("d_ar", [DM, TOWN], F32),
                             ("d_x2", [TOWN, DM], F32), ("d_dacc", [TOWN, DM], F32),
                             ("d_rho", [128, 32], F32), ("d_gs", [128, 16], F32),
                             ("d_sums", [128, 64], F32), ("d_aret", [TOWN, DM], F32), ("d_apl", [TOWN, DM], F32),
                             ("d_st3", [128, 16], F32), ("d_xo", [128, DM], F32)]:
            dbg_outs[nm] = nc.dram_tensor(nm, shp, dt_, kind="ExternalOutput").ap()

    xbr_dram = nc.dram_tensor("xbr_dram", [DI, T], F32).ap()
    sz_dram = nc.dram_tensor("sz_dram", [DI, T], F32).ap()
    x2_dram = nc.dram_tensor("x2_dram", [TOWN, DM], F32).ap()
    dacc_dram = nc.dram_tensor("dacc_dram", [TOWN, DM], F32).ap()
    wred_dram = nc.dram_tensor("wred_dram", [128, 3], F32).ap()
    cc_xdbl_in = nc.dram_tensor("cc_xdbl_in", [96, T], F32)
    cc_xdbl_out = nc.dram_tensor("cc_xdbl_out", [96, T], F32)
    cc_wmax_in = nc.dram_tensor("cc_wmax_in", [1, 4], F32)
    cc_wmax_out = nc.dram_tensor("cc_wmax_out", [1, 4], F32)
    cc_ag_in = nc.dram_tensor("cc_ag_in", [1, AGN], BF16)
    cc_ag_out = nc.dram_tensor("cc_ag_out", [8, AGN], BF16)
    cc_ar_in = nc.dram_tensor("cc_ar_in", [4, 2, DM, TC1], F32)
    cc_ar_out = nc.dram_tensor("cc_ar_out", [4, DM, TC1], F32)

    def sba(name, shape, dtype=F32):
        return nc.alloc_sbuf_tensor(name, shape, dtype).ap()

    ids = sba("ids", [128, 128])
    idsb = sba("idsb", [128, 128], BF16)
    ones_s = sba("ones_s", [1, TC1])
    bout_s = sba("bout_s", [1, DM])
    bxs = sba("bxs", [128, 8])
    bzs = sba("bzs", [128, 8])
    bdts = sba("bdts", [128, 8])
    convbs = sba("convbs", [128, 8])
    dvs = sba("dvs", [128, 8])
    convws = sba("convws", [128, 32])
    a_s = sba("a_s", [128, 8 * NST])
    normws = sba("normws", [128, 8])
    sum_all = sba("sum_all", [128, 32])
    ssq_all = sba("ssq_all", [128, 32])
    rho_all = sba("rho_all", [128, 32])
    nmr_all = sba("nmr_all", [128, 32])
    sscr = sba("sscr", [128, 6 * 32])
    hstA = sba("hstA", [128, 8 * NST])
    hstB = sba("hstB", [128, 8 * NST])
    gs_cols = sba("gs_cols", [128, 16])
    wmax_s = sba("wmax_s", [1, 4])
    wred_s = sba("wred_s", [1, 3 * 128])

    AR_COLS = 48900
    arena_t = nc.alloc_sbuf_tensor("arena", [128, AR_COLS], F32).ap()
    AR = Arena(arena_t, AR_COLS)

    PS = nc.alloc_psum_tensor("psum", [128, 4096], F32).ap()

    def bank(i):
        return PS[:, i * 512:(i + 1) * 512]

    def bankb(i):
        return PS[:, i * 512:(i + 1) * 512].bitcast(BF16)

    sems = {n: nc.alloc_semaphore(name=n) for n in
            ["sP", "sV", "sC", "sG", "sCC", "sBar"]}
    C = {n: Ctr(sems[n]) for n in sems}
    R = {n: Rec(n) for n in ENGS}
    TN, V, S, G, SY = (R[n] for n in ENGS)
    nbar = [0]
    dsems = {}

    def dsem(fam):
        if fam not in dsems:
            dsems[fam] = Ctr(nc.alloc_semaphore(name="d_" + fam))
        return dsems[fam]

    def dma(rec, fam, out, in_, **kw):
        c = dsem(fam)
        rec.dma_start(out=out, in_=in_, **kw).then_inc(c.sem, 16)
        return c.bump(16)

    def dwait(rec, fam):
        c = dsem(fam)
        rec.wait_ge(c.sem, c.val)

    def barrier():
        for fam, c in dsems.items():
            SY.wait_ge(c.sem, c.val)
        nbar[0] += 1
        tgt = 5 * nbar[0]
        for n in ENGS:
            R[n].sem_inc(sems["sBar"], 1)
            R[n].wait_ge(sems["sBar"], tgt)

    inc_pos = {}

    def mark(rec, s):
        rec.items[-1][3].append((sems[s], 1))
        tgt = C[s].bump(1)
        inc_pos[(rec.name, len(rec.items) - 1)] = (sems[s], tgt)
        return tgt

    gapV = Ctr(nc.alloc_semaphore(name="gapV"))
    gapS = Ctr(nc.alloc_semaphore(name="gapS"))

    def _gap(rec, gap):
        assert rec.items, "gap needs a preceding instruction"
        key = (rec.name, len(rec.items) - 1)
        if key in inc_pos:
            sem, tgt = inc_pos[key]
        else:
            sem, tgt = gap.sem, gap.bump(1)
            rec.items[-1][3].append((sem, 1))
            inc_pos[key] = (sem, tgt)
        rec.wait_ge(sem, tgt)

    def VF():
        _gap(V, gapV)

    def SF():
        _gap(S, gapS)

    gapG = Ctr(nc.alloc_semaphore(name="gapG"))

    def GF():
        _gap(G, gapG)

    for dst, src in [(ids, ident), (bout_s, boutv), (bxs, bxv), (bzs, bzv),
                     (bdts, bdtv), (convbs, convbv), (dvs, dvecv),
                     (convws, convwv), (a_s, alogv), (normws, normwv)]:
        dma(SY, "pre", dst, src)
    dma(SY, "pre", gs_cols[:, 0:1], s_in.broadcast_to([128, 1]))
    dma(SY, "pre", gs_cols[:, 1:2], s_ind.broadcast_to([128, 1]))

    V.memset(ones_s, 1.0)
    V.memset(hstA, 0.0)
    V.memset(hstB, 0.0)
    dwait(V, "pre")
    V.tensor_copy(idsb, ids)
    m_idsb = mark(V, "sV")

    dwait(S, "pre")
    S.activation(a_s, a_s, AF.Exp)
    m_aexp = mark(S, "sC")
    V.wait_ge(sems["sC"], m_aexp)
    V.tensor_scalar(a_s, a_s, -1.0, None, ALU.mult)
    m_aneg = mark(V, "sV")

    AR.push()
    gsl_b = AR.alloc(32 * 128)
    usl_b = AR.alloc(32 * 128)
    dsl_b = AR.alloc(8 * 512)
    q_amax = AR.alloc(256)
    q_sc8 = AR.alloc(256)
    q_rg = AR.alloc(256)
    q_scr = AR.alloc(4 * 256)
    q_a = AR.alloc(4096)
    q_asq = AR.alloc(4096)
    q_sgn = AR.alloc(4096)
    q_tA = AR.alloc(4096)
    q_tB = AR.alloc(4096)
    q_out = AR.alloc(2048, BF16)
    q_outT = AR.alloc(2048, BF16)

    dma(SY, "wq", gsl_b.rearrange("p (t d) -> p t d", d=128),
        gsl.rearrange("(t p) d -> p t d", p=128))
    dma(SY, "wq", usl_b.rearrange("p (t d) -> p t d", d=128),
        usl.rearrange("(t p) d -> p t d", p=128))
    dma(SY, "wq", dsl_b.rearrange("p (t d) -> p t d", d=512),
        dsl.rearrange("(t p) d -> p t d", p=128))

    dwait(V, "wq")
    for i, (tag, buf) in enumerate([("g", gsl_b), ("u", usl_b), ("d", dsl_b)]):
        V.tensor_reduce(q_scr[:, i:i + 1], buf, AXX, ALU.max,
                        apply_absolute_value=True)
    m_am = mark(V, "sV")
    SY.wait_ge(sems["sV"], m_am)
    dma(SY, "wr1", wred_dram, q_scr[:, 0:3])
    dwait(SY, "wr1")
    dma(SY, "wr2", wred_s, wred_dram.rearrange("a b -> (a b)").unsqueeze(0))
    dwait(V, "wr2")
    wred3 = wred_s.rearrange("o (p i) -> o p i", i=3)
    for i in range(3):
        V.tensor_reduce(wmax_s[:, i:i + 1], wred3[:, :, i], AXX, ALU.max)
    V.tensor_copy(wmax_s[:, 3:4], wmax_s[:, 0:1])
    m_wm = mark(V, "sV")

    G.wait_ge(sems["sV"], m_wm)
    dma(G, "gwm1", cc_wmax_in.ap(), wmax_s)
    dwait(G, "gwm1")
    G.collective_compute("AllReduce", ALU.max, replica_groups=ALL8,
                         ins=[cc_wmax_in.ap().opt()],
                         outs=[cc_wmax_out.ap().opt()])
    m_ccw = mark(G, "sCC")
    G.wait_ge(sems["sCC"], m_ccw)
    dma(G, "gwm2", gs_cols[:, 2:5], cc_wmax_out.ap()[0:1, 0:3].broadcast_to([128, 3]))

    dwait(V, "gwm2")
    V.reciprocal(gs_cols[:, 2:5], gs_cols[:, 2:5])
    VF()
    V.tensor_scalar(gs_cols[:, 2:5], gs_cols[:, 2:5], 2688.0, None, ALU.mult)
    VF()
    V.tensor_tensor(gs_cols[:, 5:6], gs_cols[:, 0:1], gs_cols[:, 2:3], ALU.mult)
    V.tensor_tensor(gs_cols[:, 6:7], gs_cols[:, 0:1], gs_cols[:, 3:4], ALU.mult)
    V.tensor_tensor(gs_cols[:, 7:8], gs_cols[:, 1:2], gs_cols[:, 4:5], ALU.mult)
    VF()
    V.reciprocal(gs_cols[:, 5:8], gs_cols[:, 5:8])
    V.tensor_scalar(gs_cols[:, 8:13], gs_cols[:, 0:5], 1.0 / 6.0, None, ALU.mult)
    m_scales = mark(V, "sV")

    def emit_e4m3(sc_ap, sc8_ap, scr_ap, nb):
        s_ = scr_ap[:, 0:nb]
        p_ = scr_ap[:, nb:2 * nb]
        t_ = scr_ap[:, 2 * nb:3 * nb]
        w_ = scr_ap[:, 3 * nb:4 * nb]
        VF()
        V.tensor_scalar(s_, sc_ap, 64.0, None, ALU.mult)
        VF()
        first = True
        for w in (8, 4, 2, 1):
            th = float(2.0 ** w)
            cm1 = float(2.0 ** w - 1.0)
            im1 = float(2.0 ** (-w) - 1.0)
            V.tensor_scalar(w_, s_, th, im1, ALU.is_ge, ALU.mult)
            V.tensor_scalar(t_, s_, th, cm1, ALU.is_ge, ALU.mult)
            V.scalar_tensor_tensor(s_, w_, 1.0, s_, ALU.add, ALU.mult)
            if first:
                V.tensor_scalar(p_, t_, 1.0, 0.015625, ALU.add, ALU.mult)
                first = False
            else:
                V.scalar_tensor_tensor(p_, t_, 1.0, p_, ALU.add, ALU.mult)
        V.tensor_scalar(t_, s_, 0.0625, None, ALU.is_ge)
        V.tensor_scalar(w_, s_, 0.1875, None, ALU.is_ge)
        for k in range(1, 8):
            V.scalar_tensor_tensor(t_, s_, (2 * k + 0.5) * 0.125, t_,
                                   ALU.is_ge, ALU.add)
            V.scalar_tensor_tensor(w_, s_, (2 * k + 1.5) * 0.125, w_,
                                   ALU.is_ge, ALU.add)
        VF()
        V.scalar_tensor_tensor(t_, w_, 1.0, t_, ALU.mult, ALU.add)
        VF()
        V.scalar_tensor_tensor(sc8_ap, t_, 0.125, p_, ALU.mult, ALU.mult)

    def emit_quant(x_ap, out_ap, nb, gsi, amax_ap, sc8_ap, rg_ap, scr_ap,
                   a_ap, asq_ap, sgn_ap, tA_ap, tB_ap, wait_v=None):
        FD = nb * 16
        gs_col = gs_cols[:, gsi:gsi + 1]
        gs6_col = gs_cols[:, gsi + 8:gsi + 9]
        x3 = x_ap.rearrange("p (b e) -> p b e", e=16)
        if wait_v is not None:
            V.wait_ge(*wait_v)
        V.tensor_reduce(amax_ap[:, 0:nb], x3, AXX, ALU.max, apply_absolute_value=True)
        VF()
        V.tensor_scalar(amax_ap[:, 0:nb], amax_ap[:, 0:nb], gs6_col, 448.0,
                        ALU.mult, ALU.min)
        emit_e4m3(amax_ap[:, 0:nb], sc8_ap[:, 0:nb], scr_ap, nb)
        VF()
        V.tensor_scalar(rg_ap[:, 0:nb], sc8_ap[:, 0:nb], 1e-30, None, ALU.max)
        VF()
        V.reciprocal(rg_ap[:, 0:nb], rg_ap[:, 0:nb])
        VF()
        V.tensor_scalar(rg_ap[:, 0:nb], rg_ap[:, 0:nb], gs_col, None, ALU.mult)
        VF()
        V.tensor_tensor(a_ap[:, 0:FD].rearrange("p (b e) -> p b e", e=16), x3,
                        rg_ap[:, 0:nb, None].broadcast_to([128, nb, 16]), ALU.mult)
        mv_a = mark(V, "sV")
        S.wait_ge(sems["sV"], mv_a)
        S.activation(asq_ap[:, 0:FD], a_ap[:, 0:FD], AF.Square)
        S.activation(sgn_ap[:, 0:FD], a_ap[:, 0:FD], AF.Sign)
        ms_sq = mark(S, "sC")
        V.wait_ge(sems["sC"], ms_sq)
        V.tensor_scalar(tA_ap[:, 0:FD], asq_ap[:, 0:FD], MIDSQ_A[0], None, ALU.is_ge)
        V.tensor_scalar(tB_ap[:, 0:FD], asq_ap[:, 0:FD], MIDSQ_B[0], None, ALU.is_ge)
        for thA, thB in zip(MIDSQ_A[1:], MIDSQ_B[1:]):
            V.scalar_tensor_tensor(tA_ap[:, 0:FD], asq_ap[:, 0:FD], thA,
                                   tA_ap[:, 0:FD], ALU.is_ge, ALU.add)
            V.scalar_tensor_tensor(tB_ap[:, 0:FD], asq_ap[:, 0:FD], thB,
                                   tB_ap[:, 0:FD], ALU.is_ge, ALU.add)
        VF()
        V.scalar_tensor_tensor(tA_ap[:, 0:FD], tA_ap[:, 0:FD], 0.5, tB_ap[:, 0:FD],
                               ALU.mult, ALU.add)
        VF()
        V.tensor_tensor(tA_ap[:, 0:FD], tA_ap[:, 0:FD], sgn_ap[:, 0:FD], ALU.mult)
        VF()
        V.tensor_tensor(out_ap.rearrange("p (b e) -> p b e", e=16),
                        tA_ap[:, 0:FD].rearrange("p (b e) -> p b e", e=16),
                        sc8_ap[:, 0:nb, None].broadcast_to([128, nb, 16]), ALU.mult)
        return mark(V, "sV")

    ag_marks = []
    ev_marks_all = []
    tr_bank_state = [None, None]
    for wi, (tag, buf, gci, NB, BW) in enumerate([
            ("g", gsl_b, 2, 32, 128), ("u", usl_b, 3, 32, 128),
            ("d", dsl_b, 4, 8, 512)]):
        if wi == 0:
            V.wait_ge(sems["sV"], m_scales)
        nb = 256
        mq = emit_quant(buf, q_out, nb, gci, q_amax, q_sc8,
                        q_rg, q_scr, q_a, q_asq, q_sgn, q_tA, q_tB)
        TN.wait_ge(sems["sV"], mq)
        if wi == 0:
            TN.wait_ge(sems["sV"], m_idsb)
        total = NB * (BW // 128)
        for grp in range(total // 8):
            pb = grp % 2
            if tr_bank_state[pb] is not None:
                TN.wait_ge(sems["sC"], tr_bank_state[pb])
            for j in range(8):
                blkidx = grp * 8 + j
                TN.transpose(bankb(pb)[:, j * 128:(j + 1) * 128],
                             q_out[:, blkidx * 128:(blkidx + 1) * 128], idsb)
            mp = mark(TN, "sP")
            S.wait_ge(sems["sP"], mp)
            S.copy(q_outT[:, grp * 1024:(grp + 1) * 1024], bankb(pb)[:, 0:1024])
            tr_bank_state[pb] = mark(S, "sC")
        SY.wait_ge(sems["sC"], tr_bank_state[(total // 8 - 1) % 2])
        if total // 8 >= 2:
            SY.wait_ge(sems["sC"], tr_bank_state[(total // 8 - 2) % 2])
        if tag == "g":
            dst = cc_ag_in.ap()[0, 0:WSEC].rearrange("(p h) -> p h", p=128)
        elif tag == "u":
            dst = cc_ag_in.ap()[0, WSEC:2 * WSEC].rearrange("(p h) -> p h", p=128)
        else:
            dst = cc_ag_in.ap()[0, 2 * WSEC:AGN].rearrange("(p c) -> p c", p=128)
        dma(SY, "agst", dst, q_outT[:, 0:4096])
        V.wait_ge(sems["sP"], C["sP"].val)
        dwait(V, "agst")

    m_p0q = mq
    dwait(G, "agst")
    G.collective_compute("AllGather", ALU.bypass, replica_groups=ALL8,
                         ins=[cc_ag_in.ap().opt()],
                         outs=[cc_ag_out.ap().opt()])
    m_ag = mark(G, "sCC")
    AR.pop()

    SY.wait_ge(sems["sV"], m_p0q)
    dwait(SY, "agst")
    AR.push()
    xc_b = [AR.alloc(4 * DM) for _ in range(2)]
    sqscr = AR.alloc(DM)
    m1a = {}
    for ci in range(NCH1):
        par = ci % 2
        if ci >= 2:
            SY.wait_ge(sems["sV"], m1a[f"v{ci - 2}"])
        dma(SY, f"x1a_{par}", xc_b[par].rearrange("p (a b) -> p a b", b=DM),
            x_full[ci * TC1:(ci + 1) * TC1].rearrange("(a p) b -> p a b", p=128))
        dwait(V, f"x1a_{par}")
        for tau in range(4):
            g = ci * 4 + tau
            xt = xc_b[par][:, tau * DM:(tau + 1) * DM]
            V.tensor_tensor(sqscr, xt, xt, ALU.mult)
            V.tensor_reduce(sum_all[:, g:g + 1], xt, AXX, ALU.add)
            V.tensor_reduce(ssq_all[:, g:g + 1], sqscr, AXX, ALU.add)
        m1a[f"v{ci}"] = mark(V, "sV")
        m1a[f"s{ci}"] = m1a[f"v{ci}"]
    CN = 32
    mm_ = sscr[:, 0:CN]; var_ = sscr[:, CN:2 * CN]; a1_ = sscr[:, 2 * CN:3 * CN]
    t1_ = sscr[:, 3 * CN:4 * CN]; t2_ = sscr[:, 4 * CN:5 * CN]; t3_ = sscr[:, 5 * CN:6 * CN]
    V.tensor_scalar(mm_, sum_all, 1.0 / DM, None, ALU.mult)
    V.tensor_scalar(t2_, ssq_all, 1.0 / DM, None, ALU.mult)
    V.tensor_tensor(t1_, mm_, mm_, ALU.mult)
    VF()
    V.tensor_tensor(var_, t2_, t1_, ALU.subtract)
    VF()
    V.tensor_scalar(a1_, var_, EPS, None, ALU.add)
    m_v1 = mark(V, "sV")
    S.wait_ge(sems["sV"], m_v1)
    S.activation(t1_, a1_, AF.Sqrt)
    m_s1 = mark(S, "sC")
    V.wait_ge(sems["sC"], m_s1)
    V.reciprocal(t3_, a1_)
    V.reciprocal(t2_, t1_)
    V.tensor_tensor(t3_, var_, t3_, ALU.mult)
    VF()
    V.tensor_scalar(t3_, t3_, EPS, None, ALU.add)
    m_v2 = mark(V, "sV")
    S.wait_ge(sems["sV"], m_v2)
    S.activation(t3_, t3_, AF.Sqrt)
    m_s2 = mark(S, "sC")
    V.wait_ge(sems["sC"], m_s2)
    V.reciprocal(t1_, t3_)
    VF()
    V.tensor_tensor(rho_all, t2_, t1_, ALU.mult)
    VF()
    V.scalar_tensor_tensor(nmr_all, mm_, -1.0, rho_all, ALU.mult, ALU.mult)
    m_stats = mark(V, "sV")
    if dbg:
        SY.wait_ge(sems["sV"], m_stats)
        dma(SY, "dbg", dbg_outs["d_rho"], rho_all)
        dma(SY, "dbg", dbg_outs["d_gs"], gs_cols)
        dma(SY, "dbg", dbg_outs["d_sums"][:, 0:32], sum_all)
        dma(SY, "dbg", dbg_outs["d_sums"][:, 32:64], ssq_all)
    AR.pop()

    SY.wait_ge(sems["sV"], m_stats)
    AR.push()
    wx_b = AR.alloc(8 * DI)
    wz_b = AR.alloc(8 * DI)
    wxp_b = AR.alloc(8 * 96)
    xc2_b = AR.alloc(4 * DM)
    xn_b = AR.alloc(4 * DM)
    xnT_b = AR.alloc(8 * TC1)
    xbrp_b = AR.alloc(8 * (TC1 + 3) + 8)
    xbr_b = AR.alloc(8 * TC1)
    sz_b = [AR.alloc(TC1) for _ in range(2)]
    sqp_b = [AR.alloc(TC1) for _ in range(2)]
    xdbl_b = AR.alloc(T)[0:96, :]

    wxv = wx_b.rearrange("p (k m2) -> p k m2", m2=DI)
    wzv = wz_b.rearrange("p (k m2) -> p k m2", m2=DI)
    wxpv = wxp_b.rearrange("p (k m2) -> p k m2", m2=96)
    xbrpv = xbrp_b[:, 0:8 * (TC1 + 3)].rearrange("p (m c) -> p m c", c=TC1 + 3)

    dma(SY, "w1", wxv, wxT.rearrange("(k p) m -> p k m", p=128))
    dma(SY, "w1", wzv, wzT.rearrange("(k p) m -> p k m", p=128))
    dma(SY, "w1", wxpv, wxpT.rearrange("(k p) m -> p k m", p=128))
    dwait(V, "w1")
    for k in range(8):
        V.tensor_scalar(wxv[:, k, :], wxv[:, k, :], normws[:, k:k + 1], None, ALU.mult)
        V.tensor_scalar(wzv[:, k, :], wzv[:, k, :], normws[:, k:k + 1], None, ALU.mult)
    m_wfold = mark(V, "sV")
    V.memset(xbrpv[:, :, 0:3], 0.0)
    m_convpad = mark(V, "sV")

    p1 = {}
    for ci in range(NCH1):
        if ci >= 1:
            SY.wait_ge(sems["sC"], p1[f"xn{ci - 1}"])
        dma(SY, "x1b", xc2_b.rearrange("p (a b) -> p a b", b=DM),
            x_full[ci * TC1:(ci + 1) * TC1].rearrange("(a p) b -> p a b", p=128))
        dwait(S, "x1b")
        if ci == 0:
            S.wait_ge(sems["sV"], m_stats)
        else:
            S.wait_ge(sems["sP"], p1[f"tr{ci - 1}"])
        for tau in range(4):
            g = ci * 4 + tau
            S.activation(xn_b[:, tau * DM:(tau + 1) * DM],
                         xc2_b[:, tau * DM:(tau + 1) * DM], AF.Identity,
                         bias=nmr_all[:, g:g + 1], scale=rho_all[:, g:g + 1])
        p1[f"xn{ci}"] = mark(S, "sC")
        TN.wait_ge(sems["sC"], p1[f"xn{ci}"])
        for k in range(8):
            if ci > 0 or k >= 2:
                wmark = p1[f"trev{ci}_{k - 2}"] if k >= 2 else p1[f"trev{ci - 1}_{6 + k}"]
                TN.wait_ge(sems["sC"], wmark)
            for tau in range(4):
                TN.transpose(bank(k % 2)[:, tau * 128:(tau + 1) * 128],
                             xn_b[:, tau * DM + k * 128: tau * DM + (k + 1) * 128],
                             ids)
            p1[f"trmm{ci}_{k}"] = mark(TN, "sP")
            S.wait_ge(sems["sP"], p1[f"trmm{ci}_{k}"])
            S.copy(xnT_b[:, k * TC1:(k + 1) * TC1], bank(k % 2))
            p1[f"trev{ci}_{k}"] = mark(S, "sC")
        p1[f"tr{ci}"] = p1[f"trmm{ci}_7"]
        TN.wait_ge(sems["sC"], p1[f"trev{ci}_7"])
        if ci == 0:
            TN.wait_ge(sems["sV"], m_wfold)
        for m in range(8):
            bwx = 2 + (m % 2)
            bwz = 4 + (m % 2)
            if ci > 0 or m >= 2:
                wm1 = p1.get(f"wxev{ci}_{m - 2}", p1.get(f"wxev{ci - 1}_{6 + m}"))
                TN.wait_ge(sems["sC"], wm1)
                wm2 = p1.get(f"szev{ci}_{m - 2}", p1.get(f"szev{ci - 1}_{6 + m}"))
                TN.wait_ge(sems["sC"], wm2)
            for k in range(8):
                TN.matmul(bank(bwx), wxv[:, k, m * 128:(m + 1) * 128],
                          xnT_b[:, k * TC1:(k + 1) * TC1],
                          start=(k == 0), stop=(k == 7))
            p1[f"wxmm{ci}_{m}"] = mark(TN, "sP")
            for k in range(8):
                TN.matmul(bank(bwz), wzv[:, k, m * 128:(m + 1) * 128],
                          xnT_b[:, k * TC1:(k + 1) * TC1],
                          start=(k == 0), stop=(k == 7))
            p1[f"wzmm{ci}_{m}"] = mark(TN, "sP")
            S.wait_ge(sems["sP"], p1[f"wxmm{ci}_{m}"])
            if ci == 0 and m == 0:
                S.wait_ge(sems["sV"], m_convpad)
            if ci > 0:
                S.wait_ge(sems["sV"], p1[f"halo{ci - 1}_{m}"])
            S.activation(xbrpv[:, m, 3:3 + TC1], bank(bwx), AF.Identity,
                         bias=bxs[:, m:m + 1])
            p1[f"wxev{ci}_{m}"] = mark(S, "sC")
            S.wait_ge(sems["sP"], p1[f"wzmm{ci}_{m}"])
            if ci > 0 or m >= 2:
                dwait(S, f"szst{m % 2}")
            szt = sz_b[m % 2]
            S.activation(szt, bank(bwz), AF.Silu, bias=bzs[:, m:m + 1])
            p1[f"szev{ci}_{m}"] = mark(S, "sC")
            SY.wait_ge(sems["sC"], p1[f"szev{ci}_{m}"])
            dma(SY, f"szst{m % 2}",
                sz_dram[m * 128:(m + 1) * 128, ci * TC1:(ci + 1) * TC1], szt)
            V.wait_ge(sems["sC"], p1[f"wxev{ci}_{m}"])
            if ci > 0:
                V.wait_ge(sems["sP"], p1[f"wxpmm{ci - 1}"])
                dwait(V, f"xbst{m % 2}")
            xbr_m = xbr_b[:, m * TC1:(m + 1) * TC1]
            accB = sqp_b[m % 2]
            V.tensor_scalar(xbr_m, xbrpv[:, m, 0:TC1], convws[:, 4 * m:4 * m + 1],
                            None, ALU.mult)
            V.tensor_scalar(accB, xbrpv[:, m, 1:1 + TC1],
                            convws[:, 4 * m + 1:4 * m + 2], None, ALU.mult)
            V.scalar_tensor_tensor(xbr_m, xbrpv[:, m, 2:2 + TC1],
                                   convws[:, 4 * m + 2:4 * m + 3],
                                   xbr_m, ALU.mult, ALU.add)
            V.scalar_tensor_tensor(accB, xbrpv[:, m, 3:3 + TC1],
                                   convws[:, 4 * m + 3:4 * m + 4],
                                   accB, ALU.mult, ALU.add)
            V.tensor_copy(xbrpv[:, m, 0:3], xbrpv[:, m, TC1:TC1 + 3])
            p1[f"halo{ci}_{m}"] = mark(V, "sV")
            V.tensor_tensor(xbr_m, xbr_m, accB, ALU.add)
            mconv = mark(V, "sV")
            S.wait_ge(sems["sV"], mconv)
            S.activation(xbr_m, xbr_m, AF.Silu, bias=convbs[:, m:m + 1])
            p1[f"xbr{ci}_{m}"] = mark(S, "sC")
            SY.wait_ge(sems["sC"], p1[f"xbr{ci}_{m}"])
            dma(SY, f"xbst{m % 2}",
                xbr_dram[m * 128:(m + 1) * 128, ci * TC1:(ci + 1) * TC1], xbr_m)
        TN.wait_ge(sems["sC"], p1[f"xbr{ci}_7"])
        if ci > 0:
            TN.wait_ge(sems["sC"], p1[f"wxpev{ci - 1}"])
        for k in range(8):
            TN.matmul(bank(6)[0:96, :], wxpv[:, k, :],
                      xbr_b[:, k * TC1:(k + 1) * TC1], start=(k == 0), stop=(k == 7))
        p1[f"wxpmm{ci}"] = mark(TN, "sP")
        S.wait_ge(sems["sP"], p1[f"wxpmm{ci}"])
        S.copy(xdbl_b[:, ci * TC1:(ci + 1) * TC1], bank(6)[0:96, :])
        p1[f"wxpev{ci}"] = mark(S, "sC")

    SY.wait_ge(sems["sC"], p1[f"wxpev{NCH1 - 1}"])
    dma(SY, "xdw", cc_xdbl_in.ap(), xdbl_b)
    dwait(G, "xdw")
    G.collective_compute("AllReduce", ALU.add, replica_groups=PAIRS,
                         ins=[cc_xdbl_in.ap().opt()],
                         outs=[cc_xdbl_out.ap().opt()])
    m_ccx = mark(G, "sCC")
    if dbg:
        SY.wait_ge(sems["sCC"], m_ccx)
        dma(SY, "dbg", dbg_outs["d_xdbl"], cc_xdbl_out.ap())
        dma(SY, "dbg", dbg_outs["d_xbr"], xbr_dram)
        dma(SY, "dbg", dbg_outs["d_sz"], sz_dram)
    AR.pop()

    barrier()
    AR.push()
    wout_b = AR.alloc(8 * DM)
    bc_bs = [AR.alloc(32 * TC2) for _ in range(2)]
    h_bs = [AR.alloc(NST * TC2) for _ in range(2)]
    da3_b = AR.alloc(NST * TC2)
    db3_b = AR.alloc(NST * TC2)
    dt_b = [AR.alloc(TC2) for _ in range(2)]
    dts_b = [AR.alloc(TC2) for _ in range(2)]
    u_b = AR.alloc(TC2)
    xbrr_b = [AR.alloc(TC2) for _ in range(4)]
    szr_b = [AR.alloc(TC2) for _ in range(4)]
    yacc_b = AR.alloc(TC2)
    yf_b = AR.alloc(TC2)
    p_b = AR.alloc(8 * TC2)
    aev_b = [AR.alloc(TC2) for _ in range(2)]
    wdt_b2 = AR.alloc(DI)
    dtl_bs = [AR.alloc(TC2) for _ in range(2)]

    woutv = wout_b.rearrange("p (k m2) -> p k m2", m2=DM)
    h3s = [h.rearrange("p (t n) -> p t n", n=NST) for h in h_bs]
    bc3s = [b.rearrange("p (n t) -> p n t", t=TC2) for b in bc_bs]
    cv3s = [b[:, NST * TC2:32 * TC2].rearrange("p (n t) -> p t n", t=TC2)
            for b in bc_bs]
    da3 = da3_b.rearrange("p (n t) -> p n t", t=TC2)
    db3 = db3_b.rearrange("p (n t) -> p n t", t=TC2)

    dma(SY, "w2", woutv, woutT.rearrange("(k p) m -> p k m", p=128))
    dma(SY, "w2", wdt_b2[0:64, :], wdtT)

    p2 = {}

    def bc_load(cj):
        cb = cj % 2
        cwj = slice(cj * TC2, (cj + 1) * TC2)
        if cj == 0:
            G.wait_ge(sems["sCC"], m_ccx)
        elif cj >= 2:
            G.wait_ge(sems["sV"], p2[f"pfin{cj - 2}_7"])
        dma(G, f"bc{cb}", bc3s[cb],
            cc_xdbl_out.ap()[64:96, cwj].unsqueeze(0).broadcast_to([128, 32, TC2]))
        if cj == 0:
            SY.wait_ge(sems["sCC"], m_ccx)
        elif cj >= 2:
            SY.wait_ge(sems["sP"], p2[f"dtmm{cj - 2}_7"])
        dma(SY, f"dtl{cb}", dtl_bs[cb][0:64, :], cc_xdbl_out.ap()[0:64, cwj])

    def emit_wout(cj, d0, d1):
        if d0 == 0:
            TN.wait_ge(sems["sV"], p2[f"pfin{cj}_7"])
        for dmt in range(d0, d1):
            pb = 1 + (dmt % 2)
            if cj > 0 or dmt >= 2:
                wm = p2.get(f"oev{cj}_{dmt - 2}", p2.get(f"oev{cj - 1}_{6 + dmt}"))
                TN.wait_ge(sems["sC"], wm)
            for k in range(8):
                TN.matmul(bank(pb)[:, 0:TC2], woutv[:, k, dmt * 128:(dmt + 1) * 128],
                          p_b[:, k * TC2:(k + 1) * TC2], start=(k == 0), stop=False)
            TN.matmul(bank(pb)[:, 0:TC2], bout_s[:, dmt * 128:(dmt + 1) * 128],
                      ones_s[:, 0:TC2], start=False, stop=True)
            p2[f"omm{cj}_{dmt}"] = mark(TN, "sP")
            S.wait_ge(sems["sP"], p2[f"omm{cj}_{dmt}"])
            if cj > 0 or dmt >= 2:
                dwait(S, f"ost{dmt % 2}")
            aev = aev_b[dmt % 2]
            S.copy(aev, bank(pb)[:, 0:TC2])
            p2[f"oev{cj}_{dmt}"] = mark(S, "sC")
            SY.wait_ge(sems["sC"], p2[f"oev{cj}_{dmt}"])
            hi = cj // 8
            g4 = (cj % 8) // 2
            tloc = (cj % 2) * TC2
            dma(SY, f"ost{dmt % 2}",
                cc_ar_in.ap()[g4, hi, dmt * 128:(dmt + 1) * 128,
                              tloc:tloc + TC2], aev)
        if d1 == 8 and cj >= 9 and cj % 2 == 1:
            g4 = (cj % 8) // 2
            G.wait_ge(sems["sC"], p2[f"oev{cj}_7"])
            dwait(G, "ost0")
            dwait(G, "ost1")
            G.collective_compute("ReduceScatter", ALU.add, replica_groups=PAIRS,
                                 ins=[cc_ar_in.ap()[g4].opt()],
                                 outs=[cc_ar_out.ap()[g4].opt()])
            p2[f"ccar{g4}"] = mark(G, "sCC")

    bc_load(0)

    def bdt(i):
        return bank(0 if i % 2 == 0 else 3)

    def emit_dt(i):
        ci1, m1 = i // 8, i % 8
        par1 = m1 % 2
        if m1 == 0:
            dwait(TN, f"dtl{ci1 % 2}")
        if i == 0:
            dwait(TN, "w2")
        if i >= 2:
            TN.wait_ge(sems["sC"], p2[f"dtev{i - 2}"])
        TN.matmul(bdt(i)[:, 0:TC2], wdt_b2[0:64, m1 * 128:(m1 + 1) * 128],
                  dtl_bs[ci1 % 2][0:64, :], start=True, stop=True)
        p2[f"dtmm{ci1}_{m1}"] = mark(TN, "sP")
        S.wait_ge(sems["sP"], p2[f"dtmm{ci1}_{m1}"])
        if i >= 2:
            S.wait_ge(sems["sV"], p2[f"hout{i - 2}"])
        S.activation(dts_b[par1], bdt(i)[:, 0:TC2], AF.Exp, bias=bdts[:, m1:m1 + 1])
        SF()
        S.activation(dt_b[par1], dts_b[par1], AF.Ln, bias=1.0)
        SF()
        p2[f"dtev{i}"] = mark(S, "sC")

    def epi_reduce(eidx):
        V.wait_ge(sems["sG"], p2[f"ghc{eidx}"])
        V.tensor_reduce(yacc_b, h3s[eidx % 2], AXX, ALU.add)

    def epi_yfull(eidx):
        em = eidx % 8
        V.scalar_tensor_tensor(yf_b, xbrr_b[eidx % 4], dvs[:, em:em + 1],
                               yacc_b, ALU.mult, ALU.add)

    def epi_p(eidx):
        eci, em = eidx // 8, eidx % 8
        if em == 0 and eci > 0:
            V.wait_ge(sems["sP"], p2[f"omm{eci - 1}_7"])
        V.tensor_tensor(p_b[:, em * TC2:(em + 1) * TC2], yf_b,
                        szr_b[eidx % 4], ALU.mult)
        p2[f"pfin{eci}_{em}"] = mark(V, "sV")

    emit_dt(0)
    for ci in range(NCH2):
        cbuf = ci % 2
        cw = slice(ci * TC2, (ci + 1) * TC2)
        hin = hstA if ci % 2 == 0 else hstB
        hout = hstB if ci % 2 == 0 else hstA
        for m in range(8):
            if m == 2 and ci + 1 < NCH2:
                bc_load(ci + 1)
            par = m % 2
            idx = ci * 8 + m
            eidx = idx - 2
            tb = idx % 2
            qb = idx % 4
            if idx >= 4:
                qci, qm = (idx - 4) // 8, (idx - 4) % 8
                SY.wait_ge(sems["sV"], p2[f"pfin{qci}_{qm}"])
            dma(SY, f"ld2_{qb}", xbrr_b[qb], xbr_dram[m * 128:(m + 1) * 128, cw])
            dma(SY, f"ld2_{qb}", szr_b[qb], sz_dram[m * 128:(m + 1) * 128, cw])
            if idx + 1 < NCH2 * 8:
                emit_dt(idx + 1)
            if idx == 0:
                S.wait_ge(sems["sV"], m_aneg)
            if idx >= 1:
                S.wait_ge(sems["sV"], p2[f"hout{idx - 1}"])
            for n in range(NST):
                S.activation(da3[:, n, :], dt_b[par], AF.Exp,
                             scale=a_s[:, m * NST + n:m * NST + n + 1])
            p2[f"da{idx}"] = mark(S, "sC")
            if ci > 0 and m == 2:
                emit_wout(ci - 1, 4, 8)
            V.wait_ge(sems["sC"], p2[f"dtev{idx}"])
            dwait(V, f"ld2_{qb}")
            if m == 0:
                dwait(V, f"bc{cbuf}")
            V.tensor_tensor(u_b, dt_b[par], xbrr_b[qb], ALU.mult)
            if eidx >= 0:
                epi_reduce(eidx)
            else:
                VF()
            V.tensor_tensor(db3, u_b[:, None, 0:TC2].broadcast_to([128, NST, TC2]),
                            bc3s[cbuf][:, 0:NST, :], ALU.mult)
            if eidx >= 0:
                epi_yfull(eidx)
            else:
                VF()
            V.wait_ge(sems["sC"], p2[f"da{idx}"])
            V.tensor_tensor_scan(h3s[tb][:, :, 0], da3[:, 0, :], db3[:, 0, :],
                                 hin[:, m * NST:m * NST + 1], ALU.mult, ALU.add)
            if eidx >= 0:
                epi_p(eidx)
            for n in range(1, NST):
                V.tensor_tensor_scan(h3s[tb][:, :, n], da3[:, n, :], db3[:, n, :],
                                     hin[:, m * NST + n:m * NST + n + 1],
                                     ALU.mult, ALU.add)
            V.tensor_copy(hout[:, m * NST:m * NST + 14], h3s[tb][:, TC2 - 1, 0:14])
            V.tensor_copy(hout[:, m * NST + 14:m * NST + 16],
                          h3s[tb][:, TC2 - 1, 14:16])
            p2[f"hout{idx}"] = mark(V, "sV")
            G.wait_ge(sems["sV"], p2[f"hout{idx}"])
            G.tensor_tensor(h3s[tb], h3s[tb], cv3s[cbuf], ALU.mult)
            p2[f"ghc{idx}"] = mark(G, "sG")
            if ci > 0 and m == 1:
                emit_wout(ci - 1, 0, 4)
        del cw
    for eidx in (NCH2 * 8 - 2, NCH2 * 8 - 1):
        epi_reduce(eidx)
        VF()
        epi_yfull(eidx)
        VF()
        epi_p(eidx)
    emit_wout(NCH2 - 1, 0, 8)

    if dbg:
        SY.wait_ge(sems["sCC"], p2["ccar3"])
        dma(SY, "dbg", dbg_outs["d_ar"].rearrange("d (g t) -> g d t", t=TC1),
            cc_ar_out.ap())
    AR.pop()
    barrier()
    AR.push()
    xs_dm = AR.alloc(8 * 16 * 64, BF16)
    xsv = xs_dm.rearrange("p (k g2 d) -> p k g2 d", g2=16, d=128)

    AR.push()
    at_b = AR.alloc(8 * TC1)
    atT_b = AR.alloc(DM)
    xo_b = AR.alloc(4 * DM)
    x2_b = AR.alloc(DM)
    apl_b = AR.alloc(DM)
    sq2 = AR.alloc(DM)
    st3 = AR.alloc(16)
    f_amax = AR.alloc(64)
    f_sc8 = AR.alloc(64)
    f_rg = AR.alloc(64)
    f_scr = AR.alloc(4 * 64)
    f_a = AR.alloc(DM)
    f_asq = AR.alloc(DM)
    f_sgn = AR.alloc(DM)
    f_tA = AR.alloc(DM)
    f_tB = AR.alloc(DM)
    xs_t = AR.alloc(DM // 2, BF16)

    atv = at_b.rearrange("p (k t) -> p k t", t=TC1)

    def ln_stats_tile(x_ap, rho_c, nmr_c, pre_v=None, pre_s=None):
        if pre_v is not None:
            V.wait_ge(*pre_v)
        V.tensor_tensor(sq2, x_ap, x_ap, ALU.mult)
        V.tensor_reduce(st3[:, 0:1], x_ap, AXX, ALU.add)
        V.tensor_reduce(st3[:, 1:2], sq2, AXX, ALU.add)
        a_ = st3[:, 2:3]; t_ = st3[:, 3:4]; m_ = st3[:, 4:5]
        V.tensor_scalar(m_, st3[:, 0:1], 1.0 / DM, None, ALU.mult)
        V.tensor_scalar(t_, st3[:, 1:2], 1.0 / DM, None, ALU.mult)
        V.tensor_tensor(a_, m_, m_, ALU.mult)
        VF()
        V.tensor_tensor(a_, t_, a_, ALU.subtract)
        VF()
        V.tensor_scalar(a_, a_, EPS, None, ALU.add)
        mv2 = mark(V, "sV")
        S.wait_ge(sems["sV"], mv2)
        S.activation(t_, a_, AF.Sqrt)
        ms2 = mark(S, "sC")
        V.wait_ge(sems["sC"], ms2)
        V.reciprocal(rho_c, t_)
        VF()
        V.scalar_tensor_tensor(nmr_c, m_, -1.0, rho_c, ALU.mult, ALU.mult)
        return mark(V, "sV")

    p3 = {}
    prev_tile_v = None
    prev_xtr = None
    for grp in range(4):
        if grp == 0:
            SY.wait_ge(sems["sCC"], p2["ccar0"])
        else:
            SY.wait_ge(sems["sCC"], p2[f"ccar{grp}"])
            SY.wait_ge(sems["sC"], p3[f"atT{grp * 4 - 1}"])
            SY.wait_ge(sems["sV"], prev_tile_v)
        dma(SY, "ld3a", atv,
            cc_ar_out.ap()[grp].rearrange("(k p) t -> p k t", p=128))
        dma(SY, "ld3a", xo_b.rearrange("p (a b) -> p a b", b=DM),
            x_own[grp * TC1:(grp + 1) * TC1].rearrange("(a p) b -> p a b", p=128))
        for tau in range(4):
            g = grp * 4 + tau
            xo_t = xo_b[:, tau * DM:(tau + 1) * DM]
            dwait(TN, "ld3a")
            if prev_tile_v is not None:
                TN.wait_ge(sems["sV"], prev_tile_v)
            for k in range(8):
                TN.transpose(PS[:, k * 128:(k + 1) * 128],
                             atv[:, k, tau * 128:(tau + 1) * 128], ids)
            m_attr = mark(TN, "sP")
            S.wait_ge(sems["sP"], m_attr)
            if prev_tile_v is not None:
                S.wait_ge(sems["sV"], prev_tile_v)
            S.copy(atT_b, PS[:, 0:1024])
            p3[f"atT{g}"] = mark(S, "sC")
            if tau == 0:
                dwait(V, "ld3a")
            mln1 = ln_stats_tile(xo_t, st3[:, 8:9], st3[:, 9:10])
            if dbg and g == 0:
                SY.wait_ge(sems["sV"], mln1)
                dma(SY, "dbg", dbg_outs["d_st3"], st3)
                dma(SY, "dbg", dbg_outs["d_xo"], xo_t)
                dwait(V, "dbg")
                dwait(S, "dbg")
            S.wait_ge(sems["sV"], mln1)
            S.activation(apl_b, xo_t, AF.Identity, bias=st3[:, 9:10],
                         scale=st3[:, 8:9])
            ms_lnx = mark(S, "sC")
            if dbg:
                SY.wait_ge(sems["sC"], p3[f"atT{g}"])
                dma(SY, "dbg", dbg_outs["d_aret"][g * 128:(g + 1) * 128, :], atT_b)
                SY.wait_ge(sems["sC"], ms_lnx)
                dma(SY, "dbg", dbg_outs["d_apl"][g * 128:(g + 1) * 128, :], apl_b)
                dwait(V, "dbg")
                dwait(S, "dbg")
            V.wait_ge(sems["sC"], ms_lnx)
            V.wait_ge(sems["sC"], p3[f"atT{g}"])
            V.tensor_tensor(atT_b, atT_b, apl_b, ALU.add)
            m_ar_ = mark(V, "sV")
            mln2 = ln_stats_tile(atT_b, st3[:, 10:11], st3[:, 11:12],
                                 pre_v=(sems["sV"], m_ar_),
                                 pre_s=(sems["sV"], m_ar_))
            S.wait_ge(sems["sV"], mln2)
            S.activation(apl_b, atT_b, AF.Identity, bias=st3[:, 11:12],
                         scale=st3[:, 10:11])
            ms_ln2 = mark(S, "sC")
            V.wait_ge(sems["sC"], ms_ln2)
            if g > 0:
                dwait(V, "x2st")
            V.tensor_tensor(x2_b, xo_t, apl_b, ALU.add)
            m_x2 = mark(V, "sV")
            SY.wait_ge(sems["sV"], m_x2)
            dma(SY, "x2st", x2_dram[g * 128:(g + 1) * 128, :], x2_b)
            mln3 = ln_stats_tile(x2_b, st3[:, 12:13], st3[:, 13:14],
                                 pre_v=(sems["sV"], m_x2),
                                 pre_s=(sems["sV"], m_x2))
            S.wait_ge(sems["sV"], mln3)
            S.activation(apl_b, x2_b, AF.Identity, bias=st3[:, 13:14],
                         scale=st3[:, 12:13])
            ms_ln3 = mark(S, "sC")
            if prev_xtr is not None:
                V.wait_ge(sems["sP"], prev_xtr)
            mq = emit_quant(apl_b, xs_t, 64, 0, f_amax, f_sc8,
                            f_rg, f_scr, f_a, f_asq, f_sgn, f_tA, f_tB,
                            wait_v=(sems["sC"], ms_ln3))
            prev_tile_v = mq
            TN.wait_ge(sems["sV"], mq)
            if g > 0:
                TN.wait_ge(sems["sC"], p3[f"xsg{g - 1}"])
            for k in range(8):
                TN.transpose(bankb(2)[:, k * 128:(k + 1) * 128],
                             xs_t[:, k * 128:(k + 1) * 128], idsb)
            prev_xtr = mark(TN, "sP")
            S.wait_ge(sems["sP"], prev_xtr)
            S.copy(xsv[:, :, g, :], bankb(2)[:, 0:1024].rearrange("p (k d) -> p k d", d=128))
            p3[f"xsg{g}"] = mark(S, "sC")
    m_xs_all = p3["xsg15"]
    AR.pop()

    AR.push()
    gws_b = AR.alloc(8 * 512, BF16)
    ups_b = AR.alloc(8 * 512, BF16)
    dws_b = AR.alloc(8 * 512, BF16)
    gb_b = AR.alloc(1024)
    ub_b = AR.alloc(1024)
    hid_b = AR.alloc(1024)
    h_amax = AR.alloc(64)
    h_sc8 = AR.alloc(64)
    h_rg = AR.alloc(64)
    h_scr = AR.alloc(4 * 64)
    h_a = AR.alloc(1024)
    h_asq = AR.alloc(1024)
    h_sgn = AR.alloc(1024)
    h_tA = AR.alloc(1024)
    h_tB = AR.alloc(1024)
    hs_b = AR.alloc(512, BF16)
    hsT_b = AR.alloc(512, BF16)
    do_b = [AR.alloc(DM) for _ in range(2)]

    gwv = gws_b.rearrange("p (k h) -> p k h", h=1024)
    upv = ups_b.rearrange("p (k h) -> p k h", h=1024)
    dwv = dws_b.rearrange("p (j d) -> p j d", d=1024)
    hstv = hsT_b.rearrange("p (j d) -> p j d", d=128)

    for hq in range(4):
        if hq == 0:
            SY.wait_ge(sems["sCC"], m_ag)
        else:
            SY.wait_ge(sems["sP"], p3[f"dmm{hq - 1}_15"])
        for k in range(8):
            sec = cc_ag_out.ap()[k, 0:WSEC].rearrange("(p h) -> p h", p=128)
            dma(SY, "w3", gwv[:, k, :], sec[:, hq * 1024:(hq + 1) * 1024])
            sec2 = cc_ag_out.ap()[k, WSEC:2 * WSEC].rearrange("(p h) -> p h", p=128)
            dma(SY, "w3", upv[:, k, :], sec2[:, hq * 1024:(hq + 1) * 1024])
        for j in range(8):
            gr = hq * 1024 + j * 128
            rk = gr // 512
            jloc = (gr % 512) // 128
            sec3 = cc_ag_out.ap()[rk, 2 * WSEC:AGN].rearrange(
                "(p i j2 d) -> p i j2 d", p=128, i=8, j2=4)
            dma(SY, "w3", dwv[:, j, :].rearrange("p (i d) -> p i d", d=128),
                sec3[:, :, jloc, :])
        def emit_gu(g):
            if g == 0:
                dwait(TN, "w3")
            if hq == 0 and g == 0:
                TN.wait_ge(sems["sC"], m_xs_all)
            for s2 in range(2):
                if hq > 0 or g > 0:
                    wm = p3.get(f"gev{hq}_{g - 1}_{s2}", p3.get(f"gev{hq - 1}_15_{s2}"))
                    TN.wait_ge(sems["sC"], wm)
                for k in range(8):
                    TN.matmul(bank(s2), xsv[:, k, g, :],
                              gwv[:, k, s2 * 512:(s2 + 1) * 512],
                              start=(k == 0), stop=(k == 7))
                m_gmm = mark(TN, "sP")
                if hq > 0 or g > 0:
                    wm = p3.get(f"uev{hq}_{g - 1}_{s2}", p3.get(f"uev{hq - 1}_15_{s2}"))
                    TN.wait_ge(sems["sC"], wm)
                for k in range(8):
                    TN.matmul(bank(2 + s2), xsv[:, k, g, :],
                              upv[:, k, s2 * 512:(s2 + 1) * 512],
                              start=(k == 0), stop=(k == 7))
                m_umm = mark(TN, "sP")
                S.wait_ge(sems["sP"], m_gmm)
                if hq > 0 or g > 0:
                    S.wait_ge(sems["sV"], p3.get(f"hid{hq}_{g - 1}",
                                                 p3.get(f"hid{hq - 1}_15")))
                S.activation(gb_b[:, s2 * 512:(s2 + 1) * 512], bank(s2), AF.Silu,
                             scale=gs_cols[:, 5:6])
                p3[f"gev{hq}_{g}_{s2}"] = mark(S, "sC")
                S.wait_ge(sems["sP"], m_umm)
                S.activation(ub_b[:, s2 * 512:(s2 + 1) * 512], bank(2 + s2),
                             AF.Identity, scale=gs_cols[:, 6:7])
                p3[f"uev{hq}_{g}_{s2}"] = mark(S, "sC")

        for g in range(16):
            if g == 0:
                emit_gu(0)
            V.wait_ge(sems["sC"], p3[f"uev{hq}_{g}_1"])
            V.wait_ge(sems["sC"], p3[f"gev{hq}_{g}_1"])
            V.tensor_tensor(hid_b, gb_b, ub_b, ALU.mult)
            p3[f"hid{hq}_{g}"] = mark(V, "sV")
            if hq > 0 or g > 0:
                V.wait_ge(sems["sP"], p3.get(f"htr{hq}_{g - 1}",
                                             p3.get(f"htr{hq - 1}_15")))
            mqh = emit_quant(hid_b, hs_b, 64, 1, h_amax, h_sc8,
                             h_rg, h_scr, h_a, h_asq, h_sgn, h_tA, h_tB,
                             wait_v=(sems["sV"], p3[f"hid{hq}_{g}"]))
            if g + 1 < 16:
                emit_gu(g + 1)
            TN.wait_ge(sems["sV"], mqh)
            if hq > 0 or g > 0:
                TN.wait_ge(sems["sC"], p3.get(f"hsT{hq}_{g - 1}",
                                              p3.get(f"hsT{hq - 1}_15")))
            for j in range(8):
                TN.transpose(bankb(6)[:, j * 128:(j + 1) * 128],
                             hs_b[:, j * 128:(j + 1) * 128], idsb)
            p3[f"htr{hq}_{g}"] = mark(TN, "sP")
            S.wait_ge(sems["sP"], p3[f"htr{hq}_{g}"])
            S.copy(hsT_b, bankb(6)[:, 0:1024])
            p3[f"hsT{hq}_{g}"] = mark(S, "sC")
            TN.wait_ge(sems["sC"], p3[f"hsT{hq}_{g}"])
            if hq > 0 or g > 0:
                wm = p3.get(f"dev{hq}_{g - 1}", p3.get(f"dev{hq - 1}_15"))
                TN.wait_ge(sems["sC"], wm)
            for s2 in range(2):
                for j in range(8):
                    TN.matmul(bank(4 + s2), hstv[:, j, :],
                              dwv[:, j, s2 * 512:(s2 + 1) * 512],
                              start=(j == 0), stop=(j == 7))
            p3[f"dmm{hq}_{g}"] = mark(TN, "sP")
            dpar = g % 2
            S.wait_ge(sems["sP"], p3[f"dmm{hq}_{g}"])
            if hq > 0 or g >= 2:
                dwait(S, f"dst{dpar}")
            S.activation(do_b[dpar][:, 0:512], bank(4), AF.Identity,
                         scale=gs_cols[:, 7:8])
            S.activation(do_b[dpar][:, 512:1024], bank(5), AF.Identity,
                         scale=gs_cols[:, 7:8])
            p3[f"dev{hq}_{g}"] = mark(S, "sC")
            G.wait_ge(sems["sC"], p3[f"dev{hq}_{g}"])
            if hq == 0:
                dma(G, f"dst{dpar}", dacc_dram[g * 128:(g + 1) * 128, :], do_b[dpar])
            else:
                dma(G, f"dst{dpar}", dacc_dram[g * 128:(g + 1) * 128, :], do_b[dpar],
                    accum_op=ALU.add)
    AR.pop()
    AR.pop()

    barrier()
    AR.push()
    da_in = [AR.alloc(DM) for _ in range(2)]
    x2_in = [AR.alloc(DM) for _ in range(2)]
    o_b = [AR.alloc(DM) for _ in range(2)]
    sq3 = AR.alloc(DM)
    st4 = AR.alloc(16)
    mo = {}
    for g in range(16):
        par = g % 2
        if g >= 2:
            SY.wait_ge(sems["sV"], mo[f"o{g - 2}"])
        dma(SY, f"ld3c_{par}", da_in[par], dacc_dram[g * 128:(g + 1) * 128, :])
        dma(SY, f"ld3c_{par}", x2_in[par], x2_dram[g * 128:(g + 1) * 128, :])
        dwait(V, f"ld3c_{par}")
        V.tensor_tensor(sq3, da_in[par], da_in[par], ALU.mult)
        V.tensor_reduce(st4[:, 0:1], da_in[par], AXX, ALU.add)
        V.tensor_reduce(st4[:, 1:2], sq3, AXX, ALU.add)
        a_ = st4[:, 2:3]; t_ = st4[:, 3:4]; m_ = st4[:, 4:5]
        V.tensor_scalar(m_, st4[:, 0:1], 1.0 / DM, None, ALU.mult)
        V.tensor_scalar(t_, st4[:, 1:2], 1.0 / DM, None, ALU.mult)
        V.tensor_tensor(a_, m_, m_, ALU.mult)
        VF()
        V.tensor_tensor(a_, t_, a_, ALU.subtract)
        VF()
        V.tensor_scalar(a_, a_, EPS, None, ALU.add)
        mv2 = mark(V, "sV")
        S.wait_ge(sems["sV"], mv2)
        S.activation(t_, a_, AF.Sqrt)
        ms2 = mark(S, "sC")
        V.wait_ge(sems["sC"], ms2)
        V.reciprocal(st4[:, 5:6], t_)
        VF()
        V.scalar_tensor_tensor(st4[:, 6:7], m_, -1.0, st4[:, 5:6], ALU.mult, ALU.mult)
        mv3 = mark(V, "sV")
        S.wait_ge(sems["sV"], mv3)
        S.activation(o_b[par], da_in[par], AF.Identity, bias=st4[:, 6:7],
                     scale=st4[:, 5:6])
        ms3 = mark(S, "sC")
        V.wait_ge(sems["sC"], ms3)
        V.tensor_tensor(o_b[par], o_b[par], x2_in[par], ALU.add)
        mo[f"o{g}"] = mark(V, "sV")
        SY.wait_ge(sems["sV"], mo[f"o{g}"])
        dma(SY, "outst", out_own[g * 128:(g + 1) * 128, :], o_b[par])
    if dbg:
        dma(SY, "dbg", dbg_outs["d_x2"], x2_dram)
        dma(SY, "dbg", dbg_outs["d_dacc"], dacc_dram)
    for fam in list(dsems):
        dwait(SY, fam)
    AR.pop()

    with nc.Block() as blk:
        for n in ENGS:
            getattr(blk, n)(lambda eng, n=n: R[n].replay(eng))

    counts = {n: len(R[n].items) for n in ENGS}
    counts["arena_hi"] = AR.hi
    return nc, counts


def make_in_maps(inputs):
    x = np.ascontiguousarray(inputs["L0_attn_x"], dtype=np.float32)
    Wx = np.asarray(inputs["L0_attn_Wx"])
    Wz = np.asarray(inputs["L0_attn_Wz"])
    Wxp = np.asarray(inputs["L0_attn_Wxp"])
    Wdt = np.asarray(inputs["L0_attn_Wdt"])
    Wout = np.asarray(inputs["L0_attn_Wout"])
    gw = np.asarray(inputs["L0_ffn_gate_w"])
    uw = np.asarray(inputs["L0_ffn_up_w"])
    dw = np.asarray(inputs["L0_ffn_down_w"])

    def col8(vec):
        return np.ascontiguousarray(np.asarray(vec).reshape(8, 128).T, np.float32)

    maps = []
    for c in range(8):
        b, hf = c // 2, c % 2
        sl = slice(hf * DI, (hf + 1) * DI)
        m = {
            "x_full": np.ascontiguousarray(x[b]),
            "x_own": np.ascontiguousarray(x[b, hf * TOWN:(hf + 1) * TOWN]),
            "wxT": np.ascontiguousarray(Wx[sl].T, np.float32),
            "wzT": np.ascontiguousarray(Wz[sl].T, np.float32),
            "wxpT": np.ascontiguousarray(Wxp[:, sl].T, np.float32),
            "wdtT": np.ascontiguousarray(Wdt[sl].T, np.float32),
            "woutT": np.ascontiguousarray(Wout[:, sl].T, np.float32),
            "bxv": col8(np.asarray(inputs["L0_attn_bx"])[sl]),
            "bzv": col8(np.asarray(inputs["L0_attn_bz"])[sl]),
            "bdtv": col8(np.asarray(inputs["L0_attn_bdt"])[sl]),
            "convbv": col8(np.asarray(inputs["L0_attn_conv_b"])[sl]),
            "dvecv": col8(np.asarray(inputs["L0_attn_D"])[sl]),
            "convwv": np.ascontiguousarray(
                np.asarray(inputs["L0_attn_conv_w"])[sl, 0, :].reshape(8, 128, 4)
                .transpose(1, 0, 2).reshape(128, 32), np.float32),
            "alogv": np.ascontiguousarray(
                np.asarray(inputs["L0_attn_A_log"])[sl].reshape(8, 128, NST)
                .transpose(1, 0, 2).reshape(128, 8 * NST), np.float32),
            "boutv": (np.asarray(inputs["L0_attn_bout"], np.float32).reshape(1, DM)
                      if hf == 0 else np.zeros((1, DM), np.float32)),
            "normwv": col8(np.asarray(inputs["L0_attn_norm_w"])),
            "gsl": np.ascontiguousarray(gw[:, c * 128:(c + 1) * 128], np.float32),
            "usl": np.ascontiguousarray(uw[:, c * 128:(c + 1) * 128], np.float32),
            "dsl": np.ascontiguousarray(dw[:, c * 512:(c + 1) * 512], np.float32),
            "s_in": np.asarray(inputs["L0_ffn_s_in"], np.float32).reshape(1, 1),
            "s_ind": np.asarray(inputs["L0_ffn_s_ind" if "L0_ffn_s_ind" in inputs
                                       else "L0_ffn_s_in_down"], np.float32).reshape(1, 1),
            "ident": np.eye(128, dtype=np.float32),
        }
        maps.append(m)
    return maps


def kernel(**inputs):
    nc, counts = build_program(dbg=False)
    maps = make_in_maps(inputs)
    res = run_bass_kernel_spmd(nc, maps, core_ids=list(range(8)))
    out = np.empty((4, T, DM), np.float32)
    for c in range(8):
        b, hf = c // 2, c % 2
        out[b, hf * TOWN:(hf + 1) * TOWN] = res.results[c]["out_own"]
    return out


if __name__ == "__main__":
    nc, counts = build_program(dbg=False)
    print("built:", counts)





# version 07 (final)
# speedup vs baseline: 1.1041x; 1.0308x over previous
import numpy as np

import concourse.bass as bass
import concourse.mybir as mybir
from concourse.bass_utils import run_bass_kernel_spmd

F32 = mybir.dt.float32
BF16 = mybir.dt.bfloat16
AF = mybir.ActivationFunctionType
ALU = mybir.AluOpType
AXX = mybir.AxisListType.X

DM = 1024
DI = 1024
T = 4096
TOWN = 2048
NST = 16
TC1 = 512
NCH1 = T // TC1
TC2 = 256
NCH2 = T // TC2
EPS = 1e-5
PAIRS = [[0, 1], [2, 3], [4, 5], [6, 7]]
ALL8 = [[0, 1, 2, 3, 4, 5, 6, 7]]
MIDSQ_A = [0.0625, 0.5625, 1.5625, 3.0625]
MIDSQ_B = [6.25, 12.25, 25.0, 25.0]
WSEC = 128 * 4096
DSEC = 512 * 1024
AGN = 2 * WSEC + DSEC
ENGS = ("tensor", "vector", "scalar", "gpsimd", "sync")


class Rec:
    class _Inst:
        def __init__(self, item):
            self.item = item

        def then_inc(self, sem, val):
            self.item[3].append((sem, val))
            return self

    def __init__(self, name):
        self.name = name
        self.items = []

    def __getattr__(self, meth):
        if meth.startswith("_"):
            raise AttributeError(meth)

        def call(*args, **kwargs):
            item = (meth, args, kwargs, [])
            self.items.append(item)
            return Rec._Inst(item)
        return call

    def replay(self, eng):
        for meth, args, kwargs, incs in self.items:
            inst = getattr(eng, meth)(*args, **kwargs)
            for sem, val in incs:
                inst.then_inc(sem, val)


class Ctr:
    def __init__(self, sem):
        self.sem = sem
        self.val = 0

    def bump(self, n=1):
        self.val += n
        return self.val


class Arena:
    def __init__(self, tensor, cols):
        self.t = tensor
        self.cols = cols
        self.off = 0
        self.marks = []
        self.hi = 0

    def alloc(self, ncols, dtype=F32):
        assert self.off + ncols <= self.cols, \
            f"arena overflow {self.off}+{ncols}>{self.cols}"
        ap = self.t[:, self.off:self.off + ncols]
        self.off += ncols
        self.hi = max(self.hi, self.off)
        if dtype == BF16:
            ap = ap.bitcast(BF16)
        return ap

    def push(self):
        self.marks.append(self.off)

    def pop(self):
        self.off = self.marks.pop()


def build_program(dbg=False):
    nc = bass.Bass("TRN2", target_bir_lowering=False, debug=False,
                   detect_race_conditions=False)

    def inp(name, shape, dtype=F32):
        return nc.dram_tensor(name, shape, dtype, kind="ExternalInput").ap()

    x_full = inp("x_full", [T, DM])
    x_own = inp("x_own", [TOWN, DM])
    wxT = inp("wxT", [DM, DI])
    wzT = inp("wzT", [DM, DI])
    wxpT = inp("wxpT", [DI, 96])
    wdtT = inp("wdtT", [64, DI])
    woutT = inp("woutT", [DI, DM])
    bxv = inp("bxv", [128, 8])
    bzv = inp("bzv", [128, 8])
    bdtv = inp("bdtv", [128, 8])
    convbv = inp("convbv", [128, 8])
    dvecv = inp("dvecv", [128, 8])
    convwv = inp("convwv", [128, 32])
    alogv = inp("alogv", [128, 8 * NST])
    boutv = inp("boutv", [1, DM])
    normwv = inp("normwv", [128, 8])
    gsl = inp("gsl", [4096, 128])
    usl = inp("usl", [4096, 128])
    dsl = inp("dsl", [DM, 512])
    s_in = inp("s_in", [1, 1])
    s_ind = inp("s_ind", [1, 1])
    ident = inp("ident", [128, 128])

    out_own = nc.dram_tensor("out_own", [TOWN, DM], F32, kind="ExternalOutput").ap()
    dbg_outs = {}
    if dbg:
        for nm, shp, dt_ in [("d_xbr", [DI, T], F32), ("d_sz", [DI, T], F32),
                             ("d_xdbl", [96, T], F32), ("d_ar", [DM, TOWN], F32),
                             ("d_x2", [TOWN, DM], F32), ("d_dacc", [TOWN, DM], F32),
                             ("d_rho", [128, 32], F32), ("d_gs", [128, 16], F32),
                             ("d_sums", [128, 64], F32), ("d_aret", [TOWN, DM], F32), ("d_apl", [TOWN, DM], F32),
                             ("d_st3", [128, 16], F32), ("d_xo", [128, DM], F32)]:
            dbg_outs[nm] = nc.dram_tensor(nm, shp, dt_, kind="ExternalOutput").ap()

    xbr_dram = nc.dram_tensor("xbr_dram", [DI, T], F32).ap()
    sz_dram = nc.dram_tensor("sz_dram", [DI, T], F32).ap()
    x2_dram = nc.dram_tensor("x2_dram", [TOWN, DM], F32).ap()
    dacc_dram = nc.dram_tensor("dacc_dram", [TOWN, DM], F32).ap()
    wred_dram = nc.dram_tensor("wred_dram", [128, 3], F32).ap()
    cc_xdbl_in = nc.dram_tensor("cc_xdbl_in", [96, T], F32)
    cc_xdbl_out = nc.dram_tensor("cc_xdbl_out", [96, T], F32)
    cc_wmax_in = nc.dram_tensor("cc_wmax_in", [1, 4], F32)
    cc_wmax_out = nc.dram_tensor("cc_wmax_out", [1, 4], F32)
    cc_ag_in = nc.dram_tensor("cc_ag_in", [1, AGN], BF16)
    cc_ag_out = nc.dram_tensor("cc_ag_out", [8, AGN], BF16)
    cc_ar_in = nc.dram_tensor("cc_ar_in", [4, 2, DM, TC1], F32)
    cc_ar_out = nc.dram_tensor("cc_ar_out", [4, DM, TC1], F32)

    def sba(name, shape, dtype=F32):
        return nc.alloc_sbuf_tensor(name, shape, dtype).ap()

    ids = sba("ids", [128, 128])
    idsb = sba("idsb", [128, 128], BF16)
    ones_s = sba("ones_s", [1, TC1])
    bout_s = sba("bout_s", [1, DM])
    bxs = sba("bxs", [128, 8])
    bzs = sba("bzs", [128, 8])
    bdts = sba("bdts", [128, 8])
    convbs = sba("convbs", [128, 8])
    dvs = sba("dvs", [128, 8])
    convws = sba("convws", [128, 32])
    a_s = sba("a_s", [128, 8 * NST])
    normws = sba("normws", [128, 8])
    sum_all = sba("sum_all", [128, 32])
    ssq_all = sba("ssq_all", [128, 32])
    rho_all = sba("rho_all", [128, 32])
    nmr_all = sba("nmr_all", [128, 32])
    sscr = sba("sscr", [128, 6 * 32])
    hstA = sba("hstA", [128, 8 * NST])
    hstB = sba("hstB", [128, 8 * NST])
    gs_cols = sba("gs_cols", [128, 16])
    wmax_s = sba("wmax_s", [1, 4])
    wred_s = sba("wred_s", [1, 3 * 128])

    AR_COLS = 48900
    arena_t = nc.alloc_sbuf_tensor("arena", [128, AR_COLS], F32).ap()
    AR = Arena(arena_t, AR_COLS)

    PS = nc.alloc_psum_tensor("psum", [128, 4096], F32).ap()

    def bank(i):
        return PS[:, i * 512:(i + 1) * 512]

    def bankb(i):
        return PS[:, i * 512:(i + 1) * 512].bitcast(BF16)

    sems = {n: nc.alloc_semaphore(name=n) for n in
            ["sP", "sV", "sC", "sG", "sCC", "sBar"]}
    C = {n: Ctr(sems[n]) for n in sems}
    R = {n: Rec(n) for n in ENGS}
    TN, V, S, G, SY = (R[n] for n in ENGS)
    nbar = [0]
    dsems = {}

    def dsem(fam):
        if fam not in dsems:
            dsems[fam] = Ctr(nc.alloc_semaphore(name="d_" + fam))
        return dsems[fam]

    def dma(rec, fam, out, in_, **kw):
        c = dsem(fam)
        rec.dma_start(out=out, in_=in_, **kw).then_inc(c.sem, 16)
        return c.bump(16)

    def dwait(rec, fam):
        c = dsem(fam)
        rec.wait_ge(c.sem, c.val)

    def barrier():
        for fam, c in dsems.items():
            SY.wait_ge(c.sem, c.val)
        nbar[0] += 1
        tgt = 5 * nbar[0]
        for n in ENGS:
            R[n].sem_inc(sems["sBar"], 1)
            R[n].wait_ge(sems["sBar"], tgt)

    inc_pos = {}

    def mark(rec, s):
        rec.items[-1][3].append((sems[s], 1))
        tgt = C[s].bump(1)
        inc_pos[(rec.name, len(rec.items) - 1)] = (sems[s], tgt)
        return tgt

    gapV = Ctr(nc.alloc_semaphore(name="gapV"))
    gapS = Ctr(nc.alloc_semaphore(name="gapS"))

    def _gap(rec, gap):
        assert rec.items, "gap needs a preceding instruction"
        key = (rec.name, len(rec.items) - 1)
        if key in inc_pos:
            sem, tgt = inc_pos[key]
        else:
            sem, tgt = gap.sem, gap.bump(1)
            rec.items[-1][3].append((sem, 1))
            inc_pos[key] = (sem, tgt)
        rec.wait_ge(sem, tgt)

    def VF():
        _gap(V, gapV)

    def SF():
        _gap(S, gapS)

    gapG = Ctr(nc.alloc_semaphore(name="gapG"))

    def GF():
        _gap(G, gapG)

    for dst, src in [(ids, ident), (bout_s, boutv), (bxs, bxv), (bzs, bzv),
                     (bdts, bdtv), (convbs, convbv), (dvs, dvecv),
                     (convws, convwv), (a_s, alogv), (normws, normwv)]:
        dma(SY, "pre", dst, src)
    dma(SY, "pre", gs_cols[:, 0:1], s_in.broadcast_to([128, 1]))
    dma(SY, "pre", gs_cols[:, 1:2], s_ind.broadcast_to([128, 1]))

    V.memset(ones_s, 1.0)
    V.memset(hstA, 0.0)
    V.memset(hstB, 0.0)
    dwait(V, "pre")
    V.tensor_copy(idsb, ids)
    m_idsb = mark(V, "sV")

    dwait(S, "pre")
    S.activation(a_s, a_s, AF.Exp)
    m_aexp = mark(S, "sC")
    V.wait_ge(sems["sC"], m_aexp)
    V.tensor_scalar(a_s, a_s, -1.0, None, ALU.mult)
    m_aneg = mark(V, "sV")

    AR.push()
    gsl_b = AR.alloc(32 * 128)
    usl_b = AR.alloc(32 * 128)
    dsl_b = AR.alloc(8 * 512)
    q_amax = AR.alloc(256)
    q_sc8 = AR.alloc(256)
    q_rg = AR.alloc(256)
    q_scr = AR.alloc(4 * 256)
    q_a = AR.alloc(4096)
    q_asq = AR.alloc(4096)
    q_sgn = AR.alloc(4096)
    q_tA = AR.alloc(4096)
    q_tB = AR.alloc(4096)
    q_out = AR.alloc(2048, BF16)
    q_outT = AR.alloc(2048, BF16)

    dma(SY, "wq", gsl_b.rearrange("p (t d) -> p t d", d=128),
        gsl.rearrange("(t p) d -> p t d", p=128))
    dma(SY, "wq", usl_b.rearrange("p (t d) -> p t d", d=128),
        usl.rearrange("(t p) d -> p t d", p=128))
    dma(SY, "wq", dsl_b.rearrange("p (t d) -> p t d", d=512),
        dsl.rearrange("(t p) d -> p t d", p=128))

    dwait(V, "wq")
    for i, (tag, buf) in enumerate([("g", gsl_b), ("u", usl_b), ("d", dsl_b)]):
        V.tensor_reduce(q_scr[:, i:i + 1], buf, AXX, ALU.max,
                        apply_absolute_value=True)
    m_am = mark(V, "sV")
    SY.wait_ge(sems["sV"], m_am)
    dma(SY, "wr1", wred_dram, q_scr[:, 0:3])
    dwait(SY, "wr1")
    dma(SY, "wr2", wred_s, wred_dram.rearrange("a b -> (a b)").unsqueeze(0))
    dwait(V, "wr2")
    wred3 = wred_s.rearrange("o (p i) -> o p i", i=3)
    for i in range(3):
        V.tensor_reduce(wmax_s[:, i:i + 1], wred3[:, :, i], AXX, ALU.max)
    V.tensor_copy(wmax_s[:, 3:4], wmax_s[:, 0:1])
    m_wm = mark(V, "sV")

    G.wait_ge(sems["sV"], m_wm)
    dma(G, "gwm1", cc_wmax_in.ap(), wmax_s)
    dwait(G, "gwm1")
    G.collective_compute("AllReduce", ALU.max, replica_groups=ALL8,
                         ins=[cc_wmax_in.ap().opt()],
                         outs=[cc_wmax_out.ap().opt()])
    m_ccw = mark(G, "sCC")
    G.wait_ge(sems["sCC"], m_ccw)
    dma(G, "gwm2", gs_cols[:, 2:5], cc_wmax_out.ap()[0:1, 0:3].broadcast_to([128, 3]))

    dwait(V, "gwm2")
    V.reciprocal(gs_cols[:, 2:5], gs_cols[:, 2:5])
    VF()
    V.tensor_scalar(gs_cols[:, 2:5], gs_cols[:, 2:5], 2688.0, None, ALU.mult)
    VF()
    V.tensor_tensor(gs_cols[:, 5:6], gs_cols[:, 0:1], gs_cols[:, 2:3], ALU.mult)
    V.tensor_tensor(gs_cols[:, 6:7], gs_cols[:, 0:1], gs_cols[:, 3:4], ALU.mult)
    V.tensor_tensor(gs_cols[:, 7:8], gs_cols[:, 1:2], gs_cols[:, 4:5], ALU.mult)
    VF()
    V.reciprocal(gs_cols[:, 5:8], gs_cols[:, 5:8])
    V.tensor_scalar(gs_cols[:, 8:13], gs_cols[:, 0:5], 1.0 / 6.0, None, ALU.mult)
    m_scales = mark(V, "sV")

    def emit_e4m3(sc_ap, sc8_ap, scr_ap, nb):
        s_ = scr_ap[:, 0:nb]
        p_ = scr_ap[:, nb:2 * nb]
        t_ = scr_ap[:, 2 * nb:3 * nb]
        w_ = scr_ap[:, 3 * nb:4 * nb]
        VF()
        V.tensor_scalar(s_, sc_ap, 64.0, None, ALU.mult)
        VF()
        first = True
        for w in (8, 4, 2, 1):
            th = float(2.0 ** w)
            cm1 = float(2.0 ** w - 1.0)
            im1 = float(2.0 ** (-w) - 1.0)
            V.tensor_scalar(w_, s_, th, im1, ALU.is_ge, ALU.mult)
            V.tensor_scalar(t_, s_, th, cm1, ALU.is_ge, ALU.mult)
            V.scalar_tensor_tensor(s_, w_, 1.0, s_, ALU.add, ALU.mult)
            if first:
                V.tensor_scalar(p_, t_, 1.0, 0.015625, ALU.add, ALU.mult)
                first = False
            else:
                V.scalar_tensor_tensor(p_, t_, 1.0, p_, ALU.add, ALU.mult)
        V.tensor_scalar(t_, s_, 0.0625, None, ALU.is_ge)
        V.tensor_scalar(w_, s_, 0.1875, None, ALU.is_ge)
        for k in range(1, 8):
            V.scalar_tensor_tensor(t_, s_, (2 * k + 0.5) * 0.125, t_,
                                   ALU.is_ge, ALU.add)
            V.scalar_tensor_tensor(w_, s_, (2 * k + 1.5) * 0.125, w_,
                                   ALU.is_ge, ALU.add)
        VF()
        V.scalar_tensor_tensor(t_, w_, 1.0, t_, ALU.mult, ALU.add)
        VF()
        V.scalar_tensor_tensor(sc8_ap, t_, 0.125, p_, ALU.mult, ALU.mult)

    def emit_quant(x_ap, out_ap, nb, gsi, amax_ap, sc8_ap, rg_ap, scr_ap,
                   a_ap, asq_ap, sgn_ap, tA_ap, tB_ap, wait_v=None):
        FD = nb * 16
        gs_col = gs_cols[:, gsi:gsi + 1]
        gs6_col = gs_cols[:, gsi + 8:gsi + 9]
        x3 = x_ap.rearrange("p (b e) -> p b e", e=16)
        if wait_v is not None:
            V.wait_ge(*wait_v)
        V.tensor_reduce(amax_ap[:, 0:nb], x3, AXX, ALU.max, apply_absolute_value=True)
        VF()
        V.tensor_scalar(amax_ap[:, 0:nb], amax_ap[:, 0:nb], gs6_col, 448.0,
                        ALU.mult, ALU.min)
        emit_e4m3(amax_ap[:, 0:nb], sc8_ap[:, 0:nb], scr_ap, nb)
        VF()
        V.tensor_scalar(rg_ap[:, 0:nb], sc8_ap[:, 0:nb], 1e-30, None, ALU.max)
        VF()
        V.reciprocal(rg_ap[:, 0:nb], rg_ap[:, 0:nb])
        VF()
        V.tensor_scalar(rg_ap[:, 0:nb], rg_ap[:, 0:nb], gs_col, None, ALU.mult)
        VF()
        V.tensor_tensor(a_ap[:, 0:FD].rearrange("p (b e) -> p b e", e=16), x3,
                        rg_ap[:, 0:nb, None].broadcast_to([128, nb, 16]), ALU.mult)
        mv_a = mark(V, "sV")
        S.wait_ge(sems["sV"], mv_a)
        S.activation(asq_ap[:, 0:FD], a_ap[:, 0:FD], AF.Square)
        S.activation(sgn_ap[:, 0:FD], a_ap[:, 0:FD], AF.Sign)
        ms_sq = mark(S, "sC")
        V.wait_ge(sems["sC"], ms_sq)
        V.tensor_scalar(tA_ap[:, 0:FD], asq_ap[:, 0:FD], MIDSQ_A[0], None, ALU.is_ge)
        V.tensor_scalar(tB_ap[:, 0:FD], asq_ap[:, 0:FD], MIDSQ_B[0], None, ALU.is_ge)
        for thA, thB in zip(MIDSQ_A[1:], MIDSQ_B[1:]):
            V.scalar_tensor_tensor(tA_ap[:, 0:FD], asq_ap[:, 0:FD], thA,
                                   tA_ap[:, 0:FD], ALU.is_ge, ALU.add)
            V.scalar_tensor_tensor(tB_ap[:, 0:FD], asq_ap[:, 0:FD], thB,
                                   tB_ap[:, 0:FD], ALU.is_ge, ALU.add)
        VF()
        V.scalar_tensor_tensor(tA_ap[:, 0:FD], tA_ap[:, 0:FD], 0.5, tB_ap[:, 0:FD],
                               ALU.mult, ALU.add)
        VF()
        V.tensor_tensor(tA_ap[:, 0:FD], tA_ap[:, 0:FD], sgn_ap[:, 0:FD], ALU.mult)
        VF()
        V.tensor_tensor(out_ap.rearrange("p (b e) -> p b e", e=16),
                        tA_ap[:, 0:FD].rearrange("p (b e) -> p b e", e=16),
                        sc8_ap[:, 0:nb, None].broadcast_to([128, nb, 16]), ALU.mult)
        return mark(V, "sV")

    ag_marks = []
    ev_marks_all = []
    tr_bank_state = [None, None]
    for wi, (tag, buf, gci, NB, BW) in enumerate([
            ("g", gsl_b, 2, 32, 128), ("u", usl_b, 3, 32, 128),
            ("d", dsl_b, 4, 8, 512)]):
        if wi == 0:
            V.wait_ge(sems["sV"], m_scales)
        nb = 256
        mq = emit_quant(buf, q_out, nb, gci, q_amax, q_sc8,
                        q_rg, q_scr, q_a, q_asq, q_sgn, q_tA, q_tB)
        TN.wait_ge(sems["sV"], mq)
        if wi == 0:
            TN.wait_ge(sems["sV"], m_idsb)
        total = NB * (BW // 128)
        for grp in range(total // 8):
            pb = grp % 2
            if tr_bank_state[pb] is not None:
                TN.wait_ge(sems["sC"], tr_bank_state[pb])
            for j in range(8):
                blkidx = grp * 8 + j
                TN.transpose(bankb(pb)[:, j * 128:(j + 1) * 128],
                             q_out[:, blkidx * 128:(blkidx + 1) * 128], idsb)
            mp = mark(TN, "sP")
            S.wait_ge(sems["sP"], mp)
            S.copy(q_outT[:, grp * 1024:(grp + 1) * 1024], bankb(pb)[:, 0:1024])
            tr_bank_state[pb] = mark(S, "sC")
        SY.wait_ge(sems["sC"], tr_bank_state[(total // 8 - 1) % 2])
        if total // 8 >= 2:
            SY.wait_ge(sems["sC"], tr_bank_state[(total // 8 - 2) % 2])
        if tag == "g":
            dst = cc_ag_in.ap()[0, 0:WSEC].rearrange("(p h) -> p h", p=128)
        elif tag == "u":
            dst = cc_ag_in.ap()[0, WSEC:2 * WSEC].rearrange("(p h) -> p h", p=128)
        else:
            dst = cc_ag_in.ap()[0, 2 * WSEC:AGN].rearrange("(p c) -> p c", p=128)
        dma(SY, "agst", dst, q_outT[:, 0:4096])
        V.wait_ge(sems["sP"], C["sP"].val)
        dwait(V, "agst")

    m_p0q = mq
    dwait(G, "agst")
    G.collective_compute("AllGather", ALU.bypass, replica_groups=ALL8,
                         ins=[cc_ag_in.ap().opt()],
                         outs=[cc_ag_out.ap().opt()])
    m_ag = mark(G, "sCC")
    AR.pop()

    SY.wait_ge(sems["sV"], m_p0q)
    dwait(SY, "agst")
    AR.push()
    xc_b = [AR.alloc(4 * DM) for _ in range(2)]
    sqscr = AR.alloc(DM)
    m1a = {}
    for ci in range(NCH1):
        par = ci % 2
        if ci >= 2:
            SY.wait_ge(sems["sV"], m1a[f"v{ci - 2}"])
        dma(SY, f"x1a_{par}", xc_b[par].rearrange("p (a b) -> p a b", b=DM),
            x_full[ci * TC1:(ci + 1) * TC1].rearrange("(a p) b -> p a b", p=128))
        dwait(V, f"x1a_{par}")
        for tau in range(4):
            g = ci * 4 + tau
            xt = xc_b[par][:, tau * DM:(tau + 1) * DM]
            V.tensor_tensor(sqscr, xt, xt, ALU.mult)
            V.tensor_reduce(sum_all[:, g:g + 1], xt, AXX, ALU.add)
            V.tensor_reduce(ssq_all[:, g:g + 1], sqscr, AXX, ALU.add)
        m1a[f"v{ci}"] = mark(V, "sV")
        m1a[f"s{ci}"] = m1a[f"v{ci}"]
    CN = 32
    mm_ = sscr[:, 0:CN]; var_ = sscr[:, CN:2 * CN]; a1_ = sscr[:, 2 * CN:3 * CN]
    t1_ = sscr[:, 3 * CN:4 * CN]; t2_ = sscr[:, 4 * CN:5 * CN]; t3_ = sscr[:, 5 * CN:6 * CN]
    V.tensor_scalar(mm_, sum_all, 1.0 / DM, None, ALU.mult)
    V.tensor_scalar(t2_, ssq_all, 1.0 / DM, None, ALU.mult)
    V.tensor_tensor(t1_, mm_, mm_, ALU.mult)
    VF()
    V.tensor_tensor(var_, t2_, t1_, ALU.subtract)
    VF()
    V.tensor_scalar(a1_, var_, EPS, None, ALU.add)
    m_v1 = mark(V, "sV")
    S.wait_ge(sems["sV"], m_v1)
    S.activation(t1_, a1_, AF.Sqrt)
    m_s1 = mark(S, "sC")
    V.wait_ge(sems["sC"], m_s1)
    V.reciprocal(t3_, a1_)
    V.reciprocal(t2_, t1_)
    V.tensor_tensor(t3_, var_, t3_, ALU.mult)
    VF()
    V.tensor_scalar(t3_, t3_, EPS, None, ALU.add)
    m_v2 = mark(V, "sV")
    S.wait_ge(sems["sV"], m_v2)
    S.activation(t3_, t3_, AF.Sqrt)
    m_s2 = mark(S, "sC")
    V.wait_ge(sems["sC"], m_s2)
    V.reciprocal(t1_, t3_)
    VF()
    V.tensor_tensor(rho_all, t2_, t1_, ALU.mult)
    VF()
    V.scalar_tensor_tensor(nmr_all, mm_, -1.0, rho_all, ALU.mult, ALU.mult)
    m_stats = mark(V, "sV")
    if dbg:
        SY.wait_ge(sems["sV"], m_stats)
        dma(SY, "dbg", dbg_outs["d_rho"], rho_all)
        dma(SY, "dbg", dbg_outs["d_gs"], gs_cols)
        dma(SY, "dbg", dbg_outs["d_sums"][:, 0:32], sum_all)
        dma(SY, "dbg", dbg_outs["d_sums"][:, 32:64], ssq_all)
    AR.pop()

    SY.wait_ge(sems["sV"], m_stats)
    AR.push()
    wx_b = AR.alloc(8 * DI)
    wz_b = AR.alloc(8 * DI)
    wxp_b = AR.alloc(8 * 96)
    xc2_b = AR.alloc(4 * DM)
    xn_b = AR.alloc(4 * DM)
    xnT_b = AR.alloc(8 * TC1)
    xbrp_b = AR.alloc(8 * (TC1 + 3) + 8)
    xbr_b = AR.alloc(8 * TC1)
    sz_b = [AR.alloc(TC1) for _ in range(2)]
    sqp_b = [AR.alloc(TC1) for _ in range(2)]
    xdbl_b = AR.alloc(T)[0:96, :]

    wxv = wx_b.rearrange("p (k m2) -> p k m2", m2=DI)
    wzv = wz_b.rearrange("p (k m2) -> p k m2", m2=DI)
    wxpv = wxp_b.rearrange("p (k m2) -> p k m2", m2=96)
    xbrpv = xbrp_b[:, 0:8 * (TC1 + 3)].rearrange("p (m c) -> p m c", c=TC1 + 3)

    dma(SY, "w1", wxv, wxT.rearrange("(k p) m -> p k m", p=128))
    dma(SY, "w1", wzv, wzT.rearrange("(k p) m -> p k m", p=128))
    dma(SY, "w1", wxpv, wxpT.rearrange("(k p) m -> p k m", p=128))
    dwait(V, "w1")
    for k in range(8):
        V.tensor_scalar(wxv[:, k, :], wxv[:, k, :], normws[:, k:k + 1], None, ALU.mult)
        V.tensor_scalar(wzv[:, k, :], wzv[:, k, :], normws[:, k:k + 1], None, ALU.mult)
    m_wfold = mark(V, "sV")
    V.memset(xbrpv[:, :, 0:3], 0.0)
    m_convpad = mark(V, "sV")

    p1 = {}
    for ci in range(NCH1):
        if ci >= 1:
            SY.wait_ge(sems["sC"], p1[f"xn{ci - 1}"])
        dma(SY, "x1b", xc2_b.rearrange("p (a b) -> p a b", b=DM),
            x_full[ci * TC1:(ci + 1) * TC1].rearrange("(a p) b -> p a b", p=128))
        dwait(S, "x1b")
        if ci == 0:
            S.wait_ge(sems["sV"], m_stats)
        else:
            S.wait_ge(sems["sP"], p1[f"tr{ci - 1}"])
        for tau in range(4):
            g = ci * 4 + tau
            S.activation(xn_b[:, tau * DM:(tau + 1) * DM],
                         xc2_b[:, tau * DM:(tau + 1) * DM], AF.Identity,
                         bias=nmr_all[:, g:g + 1], scale=rho_all[:, g:g + 1])
        p1[f"xn{ci}"] = mark(S, "sC")
        TN.wait_ge(sems["sC"], p1[f"xn{ci}"])
        for k in range(8):
            if ci > 0 or k >= 2:
                wmark = p1[f"trev{ci}_{k - 2}"] if k >= 2 else p1[f"trev{ci - 1}_{6 + k}"]
                TN.wait_ge(sems["sC"], wmark)
            for tau in range(4):
                TN.transpose(bank(k % 2)[:, tau * 128:(tau + 1) * 128],
                             xn_b[:, tau * DM + k * 128: tau * DM + (k + 1) * 128],
                             ids)
            p1[f"trmm{ci}_{k}"] = mark(TN, "sP")
            S.wait_ge(sems["sP"], p1[f"trmm{ci}_{k}"])
            S.copy(xnT_b[:, k * TC1:(k + 1) * TC1], bank(k % 2))
            p1[f"trev{ci}_{k}"] = mark(S, "sC")
        p1[f"tr{ci}"] = p1[f"trmm{ci}_7"]
        TN.wait_ge(sems["sC"], p1[f"trev{ci}_7"])
        if ci == 0:
            TN.wait_ge(sems["sV"], m_wfold)
        for m in range(8):
            bwx = 2 + (m % 2)
            bwz = 4 + (m % 2)
            if ci > 0 or m >= 2:
                wm1 = p1.get(f"wxev{ci}_{m - 2}", p1.get(f"wxev{ci - 1}_{6 + m}"))
                TN.wait_ge(sems["sC"], wm1)
                wm2 = p1.get(f"szev{ci}_{m - 2}", p1.get(f"szev{ci - 1}_{6 + m}"))
                TN.wait_ge(sems["sC"], wm2)
            for k in range(8):
                TN.matmul(bank(bwx), wxv[:, k, m * 128:(m + 1) * 128],
                          xnT_b[:, k * TC1:(k + 1) * TC1],
                          start=(k == 0), stop=(k == 7))
            p1[f"wxmm{ci}_{m}"] = mark(TN, "sP")
            for k in range(8):
                TN.matmul(bank(bwz), wzv[:, k, m * 128:(m + 1) * 128],
                          xnT_b[:, k * TC1:(k + 1) * TC1],
                          start=(k == 0), stop=(k == 7))
            p1[f"wzmm{ci}_{m}"] = mark(TN, "sP")
            S.wait_ge(sems["sP"], p1[f"wxmm{ci}_{m}"])
            if ci == 0 and m == 0:
                S.wait_ge(sems["sV"], m_convpad)
            if ci > 0:
                S.wait_ge(sems["sV"], p1[f"halo{ci - 1}_{m}"])
            S.activation(xbrpv[:, m, 3:3 + TC1], bank(bwx), AF.Identity,
                         bias=bxs[:, m:m + 1])
            p1[f"wxev{ci}_{m}"] = mark(S, "sC")
            S.wait_ge(sems["sP"], p1[f"wzmm{ci}_{m}"])
            if ci > 0 or m >= 2:
                dwait(S, f"szst{m % 2}")
            szt = sz_b[m % 2]
            S.activation(szt, bank(bwz), AF.Silu, bias=bzs[:, m:m + 1])
            p1[f"szev{ci}_{m}"] = mark(S, "sC")
            SY.wait_ge(sems["sC"], p1[f"szev{ci}_{m}"])
            dma(SY, f"szst{m % 2}",
                sz_dram[m * 128:(m + 1) * 128, ci * TC1:(ci + 1) * TC1], szt)
            V.wait_ge(sems["sC"], p1[f"wxev{ci}_{m}"])
            if ci > 0:
                V.wait_ge(sems["sP"], p1[f"wxpmm{ci - 1}"])
                dwait(V, f"xbst{m % 2}")
            xbr_m = xbr_b[:, m * TC1:(m + 1) * TC1]
            accB = sqp_b[m % 2]
            V.tensor_scalar(xbr_m, xbrpv[:, m, 0:TC1], convws[:, 4 * m:4 * m + 1],
                            None, ALU.mult)
            V.tensor_scalar(accB, xbrpv[:, m, 1:1 + TC1],
                            convws[:, 4 * m + 1:4 * m + 2], None, ALU.mult)
            V.scalar_tensor_tensor(xbr_m, xbrpv[:, m, 2:2 + TC1],
                                   convws[:, 4 * m + 2:4 * m + 3],
                                   xbr_m, ALU.mult, ALU.add)
            V.scalar_tensor_tensor(accB, xbrpv[:, m, 3:3 + TC1],
                                   convws[:, 4 * m + 3:4 * m + 4],
                                   accB, ALU.mult, ALU.add)
            V.tensor_copy(xbrpv[:, m, 0:3], xbrpv[:, m, TC1:TC1 + 3])
            p1[f"halo{ci}_{m}"] = mark(V, "sV")
            V.tensor_tensor(xbr_m, xbr_m, accB, ALU.add)
            mconv = mark(V, "sV")
            S.wait_ge(sems["sV"], mconv)
            S.activation(xbr_m, xbr_m, AF.Silu, bias=convbs[:, m:m + 1])
            p1[f"xbr{ci}_{m}"] = mark(S, "sC")
            SY.wait_ge(sems["sC"], p1[f"xbr{ci}_{m}"])
            dma(SY, f"xbst{m % 2}",
                xbr_dram[m * 128:(m + 1) * 128, ci * TC1:(ci + 1) * TC1], xbr_m)
        TN.wait_ge(sems["sC"], p1[f"xbr{ci}_7"])
        if ci > 0:
            TN.wait_ge(sems["sC"], p1[f"wxpev{ci - 1}"])
        for k in range(8):
            TN.matmul(bank(6)[0:96, :], wxpv[:, k, :],
                      xbr_b[:, k * TC1:(k + 1) * TC1], start=(k == 0), stop=(k == 7))
        p1[f"wxpmm{ci}"] = mark(TN, "sP")
        S.wait_ge(sems["sP"], p1[f"wxpmm{ci}"])
        S.copy(xdbl_b[:, ci * TC1:(ci + 1) * TC1], bank(6)[0:96, :])
        p1[f"wxpev{ci}"] = mark(S, "sC")

    SY.wait_ge(sems["sC"], p1[f"wxpev{NCH1 - 1}"])
    dma(SY, "xdw", cc_xdbl_in.ap(), xdbl_b)
    dwait(G, "xdw")
    G.collective_compute("AllReduce", ALU.add, replica_groups=PAIRS,
                         ins=[cc_xdbl_in.ap().opt()],
                         outs=[cc_xdbl_out.ap().opt()])
    m_ccx = mark(G, "sCC")
    if dbg:
        SY.wait_ge(sems["sCC"], m_ccx)
        dma(SY, "dbg", dbg_outs["d_xdbl"], cc_xdbl_out.ap())
        dma(SY, "dbg", dbg_outs["d_xbr"], xbr_dram)
        dma(SY, "dbg", dbg_outs["d_sz"], sz_dram)
    AR.pop()

    barrier()
    AR.push()
    wout_b = AR.alloc(8 * DM)
    bB_bs = [AR.alloc(NST * TC2) for _ in range(2)]
    cC_b = AR.alloc(NST * TC2)
    h_bs = [AR.alloc(NST * TC2) for _ in range(2)]
    da3_b = AR.alloc(NST * TC2)
    db3_b = AR.alloc(NST * TC2)
    dt_b = [AR.alloc(TC2) for _ in range(2)]
    dts_b = [AR.alloc(TC2) for _ in range(2)]
    u_b = AR.alloc(TC2)
    xbrr_b = [AR.alloc(TC2) for _ in range(4)]
    szr_b = [AR.alloc(TC2) for _ in range(4)]
    yacc_b = AR.alloc(TC2)
    yf_b = AR.alloc(TC2)
    p_bs = [AR.alloc(8 * TC2) for _ in range(2)]
    aev_b = [AR.alloc(TC2) for _ in range(2)]
    wdt_b2 = AR.alloc(DI)
    dtl_bs = [AR.alloc(TC2) for _ in range(2)]

    woutv = wout_b.rearrange("p (k m2) -> p k m2", m2=DM)
    h3s = [h.rearrange("p (t n) -> p t n", n=NST) for h in h_bs]
    bB3s = [b.rearrange("p (n t) -> p n t", t=TC2) for b in bB_bs]
    cv3 = cC_b.rearrange("p (n t) -> p t n", t=TC2)
    da3 = da3_b.rearrange("p (n t) -> p n t", t=TC2)
    db3 = db3_b.rearrange("p (n t) -> p n t", t=TC2)

    dma(SY, "w2", woutv, woutT.rearrange("(k p) m -> p k m", p=128))
    dma(SY, "w2", wdt_b2[0:64, :], wdtT)

    p2 = {}

    def b_load(cj):
        cb = cj % 2
        cwj = slice(cj * TC2, (cj + 1) * TC2)
        if cj == 0:
            SY.wait_ge(sems["sCC"], m_ccx)
        elif cj >= 2:
            SY.wait_ge(sems["sV"], p2[f"db3m{cj - 2}"])
        for r4 in range(4):
            dma(SY, f"bB{cb}", bB3s[cb][:, r4 * 4:(r4 + 1) * 4, :],
                cc_xdbl_out.ap()[64 + r4 * 4:64 + (r4 + 1) * 4, cwj]
                .unsqueeze(0).broadcast_to([128, 4, TC2]))
        if cj >= 2:
            SY.wait_ge(sems["sP"], p2[f"dtmm{cj - 2}_7"])
        dma(SY, f"dtl{cb}", dtl_bs[cb][0:64, :], cc_xdbl_out.ap()[0:64, cwj])

    def c_load(cj):
        cwj = slice(cj * TC2, (cj + 1) * TC2)
        if cj == 0:
            G.wait_ge(sems["sCC"], m_ccx)
        cC3 = cC_b.rearrange("p (n t) -> p n t", t=TC2)
        for r4 in range(4):
            dma(G, "fC", cC3[:, r4 * 4:(r4 + 1) * 4, :],
                cc_xdbl_out.ap()[80 + r4 * 4:80 + (r4 + 1) * 4, cwj]
                .unsqueeze(0).broadcast_to([128, 4, TC2]))

    OWB = (1, 2, 4, 5)
    sOW = Ctr(nc.alloc_semaphore(name="sOW"))
    sOE = Ctr(nc.alloc_semaphore(name="sOE"))

    def markc(rec, ctr):
        rec.items[-1][3].append((ctr.sem, 1))
        return ctr.bump(1)

    def emit_wout_mm(cj, d0, d1):
        if d0 == 0:
            TN.wait_ge(sems["sV"], p2[f"pfin{cj}_7"])
        for dmt in range(d0, d1):
            pb = OWB[dmt % 4]
            prev_ev = 8 * cj + dmt - 4 + 1
            if prev_ev > 0:
                TN.wait_ge(sOE.sem, prev_ev)
            for k in range(8):
                TN.matmul(bank(pb)[:, 0:TC2], woutv[:, k, dmt * 128:(dmt + 1) * 128],
                          p_bs[cj % 2][:, k * TC2:(k + 1) * TC2],
                          start=(k == 0), stop=False)
            TN.matmul(bank(pb)[:, 0:TC2], bout_s[:, dmt * 128:(dmt + 1) * 128],
                      ones_s[:, 0:TC2], start=False, stop=True)
            markc(TN, sOW)

    def emit_wout_ev(cj, d0, d1):
        for dmt in range(d0, d1):
            pb = OWB[dmt % 4]
            S.wait_ge(sOW.sem, 8 * cj + dmt + 1)
            if cj > 0 or dmt >= 2:
                dwait(S, f"ost{dmt % 2}")
            aev = aev_b[dmt % 2]
            S.copy(aev, bank(pb)[:, 0:TC2])
            markc(S, sOE)
            SY.wait_ge(sOE.sem, 8 * cj + dmt + 1)
            hi = cj // 8
            g4 = (cj % 8) // 2
            tloc = (cj % 2) * TC2
            dma(SY, f"ost{dmt % 2}",
                cc_ar_in.ap()[g4, hi, dmt * 128:(dmt + 1) * 128,
                              tloc:tloc + TC2], aev)
        if d1 == 8 and cj >= 9 and cj % 2 == 1:
            g4 = (cj % 8) // 2
            G.wait_ge(sOE.sem, 8 * cj + 8)
            dwait(G, "ost0")
            dwait(G, "ost1")
            G.collective_compute("ReduceScatter", ALU.add, replica_groups=PAIRS,
                                 ins=[cc_ar_in.ap()[g4].opt()],
                                 outs=[cc_ar_out.ap()[g4].opt()])
            p2[f"ccar{g4}"] = mark(G, "sCC")

    b_load(0)
    c_load(0)

    def bdt(i):
        return bank(0 if i % 2 == 0 else 3)

    def emit_dt(i):
        ci1, m1 = i // 8, i % 8
        par1 = m1 % 2
        if m1 == 0:
            dwait(TN, f"dtl{ci1 % 2}")
        if i == 0:
            dwait(TN, "w2")
        if i >= 2:
            TN.wait_ge(sems["sC"], p2[f"dtev{i - 2}"])
        TN.matmul(bdt(i)[:, 0:TC2], wdt_b2[0:64, m1 * 128:(m1 + 1) * 128],
                  dtl_bs[ci1 % 2][0:64, :], start=True, stop=True)
        p2[f"dtmm{ci1}_{m1}"] = mark(TN, "sP")
        S.wait_ge(sems["sP"], p2[f"dtmm{ci1}_{m1}"])
        if i >= 2:
            S.wait_ge(sems["sV"], p2[f"hout{i - 2}"])
        S.activation(dts_b[par1], bdt(i)[:, 0:TC2], AF.Exp, bias=bdts[:, m1:m1 + 1])
        SF()
        S.activation(dt_b[par1], dts_b[par1], AF.Ln, bias=1.0)
        SF()
        p2[f"dtev{i}"] = mark(S, "sC")

    def epi_reduce(eidx):
        V.wait_ge(sems["sG"], p2[f"ghc{eidx}"])
        V.tensor_reduce(yacc_b, h3s[eidx % 2], AXX, ALU.add)

    def epi_yfull(eidx):
        em = eidx % 8
        V.scalar_tensor_tensor(yf_b, xbrr_b[eidx % 4], dvs[:, em:em + 1],
                               yacc_b, ALU.mult, ALU.add)

    def epi_p(eidx):
        eci, em = eidx // 8, eidx % 8
        if em == 0 and eci >= 2:
            V.wait_ge(sOW.sem, 8 * (eci - 2) + 8)
        V.tensor_tensor(p_bs[eci % 2][:, em * TC2:(em + 1) * TC2], yf_b,
                        szr_b[eidx % 4], ALU.mult)
        p2[f"pfin{eci}_{em}"] = mark(V, "sV")

    emit_dt(0)
    for ci in range(NCH2):
        cbuf = ci % 2
        cw = slice(ci * TC2, (ci + 1) * TC2)
        hin = hstA if ci % 2 == 0 else hstB
        hout = hstB if ci % 2 == 0 else hstA
        for m in range(8):
            if m == 0 and ci + 1 < NCH2:
                b_load(ci + 1)
            par = m % 2
            idx = ci * 8 + m
            eidx = idx - 2
            tb = idx % 2
            qb = idx % 4
            if idx >= 4:
                qci, qm = (idx - 4) // 8, (idx - 4) % 8
                SY.wait_ge(sems["sV"], p2[f"pfin{qci}_{qm}"])
            dma(SY, f"ld2_{qb}", xbrr_b[qb], xbr_dram[m * 128:(m + 1) * 128, cw])
            dma(SY, f"ld2_{qb}", szr_b[qb], sz_dram[m * 128:(m + 1) * 128, cw])
            if idx + 1 < NCH2 * 8:
                emit_dt(idx + 1)
            if idx == 0:
                S.wait_ge(sems["sV"], m_aneg)
            if idx >= 1:
                S.wait_ge(sems["sV"], p2[f"hout{idx - 1}"])
            for n in range(NST):
                S.activation(da3[:, n, :], dt_b[par], AF.Exp,
                             scale=a_s[:, m * NST + n:m * NST + n + 1])
            p2[f"da{idx}"] = mark(S, "sC")
            V.wait_ge(sems["sC"], p2[f"dtev{idx}"])
            dwait(V, f"ld2_{qb}")
            if m == 0:
                dwait(V, f"bB{cbuf}")
            V.tensor_tensor(u_b, dt_b[par], xbrr_b[qb], ALU.mult)
            if eidx >= 0:
                epi_reduce(eidx)
            else:
                VF()
            V.tensor_tensor(db3, u_b[:, None, 0:TC2].broadcast_to([128, NST, TC2]),
                            bB3s[cbuf][:, 0:NST, :], ALU.mult)
            if m == 7:
                p2[f"db3m{ci}"] = mark(V, "sV")
            if eidx >= 0:
                epi_yfull(eidx)
            else:
                VF()
            V.wait_ge(sems["sC"], p2[f"da{idx}"])
            V.tensor_tensor_scan(h3s[tb][:, :, 0], da3[:, 0, :], db3[:, 0, :],
                                 hin[:, m * NST:m * NST + 1], ALU.mult, ALU.add)
            if eidx >= 0:
                epi_p(eidx)
            for n in range(1, NST):
                V.tensor_tensor_scan(h3s[tb][:, :, n], da3[:, n, :], db3[:, n, :],
                                     hin[:, m * NST + n:m * NST + n + 1],
                                     ALU.mult, ALU.add)
            V.tensor_copy(hout[:, m * NST:m * NST + 14], h3s[tb][:, TC2 - 1, 0:14])
            V.tensor_copy(hout[:, m * NST + 14:m * NST + 16],
                          h3s[tb][:, TC2 - 1, 14:16])
            p2[f"hout{idx}"] = mark(V, "sV")
            G.wait_ge(sems["sV"], p2[f"hout{idx}"])
            if m == 0:
                dwait(G, "fC")
            G.tensor_tensor(h3s[tb], h3s[tb], cv3, ALU.mult)
            p2[f"ghc{idx}"] = mark(G, "sG")
            if m == 7 and ci + 1 < NCH2:
                c_load(ci + 1)
            if ci > 0 and m == 1:
                emit_wout_mm(ci - 1, 0, 4)
            if ci > 0 and m == 2:
                emit_wout_ev(ci - 1, 0, 4)
                emit_wout_mm(ci - 1, 4, 8)
            if ci > 0 and m == 3:
                emit_wout_ev(ci - 1, 4, 8)
        del cw
    for eidx in (NCH2 * 8 - 2, NCH2 * 8 - 1):
        epi_reduce(eidx)
        VF()
        epi_yfull(eidx)
        VF()
        epi_p(eidx)
    emit_wout_mm(NCH2 - 1, 0, 4)
    emit_wout_ev(NCH2 - 1, 0, 4)
    emit_wout_mm(NCH2 - 1, 4, 8)
    emit_wout_ev(NCH2 - 1, 4, 8)

    if dbg:
        SY.wait_ge(sems["sCC"], p2["ccar3"])
        dma(SY, "dbg", dbg_outs["d_ar"].rearrange("d (g t) -> g d t", t=TC1),
            cc_ar_out.ap())
    AR.pop()
    barrier()
    AR.push()
    xs_dm = AR.alloc(8 * 16 * 64, BF16)
    xsv = xs_dm.rearrange("p (k g2 d) -> p k g2 d", g2=16, d=128)

    AR.push()
    at_b = AR.alloc(8 * TC1)
    atT_b = AR.alloc(DM)
    xo_b = AR.alloc(4 * DM)
    x2_b = AR.alloc(DM)
    apl_b = AR.alloc(DM)
    sq2 = AR.alloc(DM)
    st3 = AR.alloc(16)
    f_amax = AR.alloc(64)
    f_sc8 = AR.alloc(64)
    f_rg = AR.alloc(64)
    f_scr = AR.alloc(4 * 64)
    f_a = AR.alloc(DM)
    f_asq = AR.alloc(DM)
    f_sgn = AR.alloc(DM)
    f_tA = AR.alloc(DM)
    f_tB = AR.alloc(DM)
    xs_t = AR.alloc(DM // 2, BF16)

    atv = at_b.rearrange("p (k t) -> p k t", t=TC1)

    def ln_stats_tile(x_ap, rho_c, nmr_c, pre_v=None, pre_s=None):
        if pre_v is not None:
            V.wait_ge(*pre_v)
        V.tensor_tensor(sq2, x_ap, x_ap, ALU.mult)
        V.tensor_reduce(st3[:, 0:1], x_ap, AXX, ALU.add)
        V.tensor_reduce(st3[:, 1:2], sq2, AXX, ALU.add)
        a_ = st3[:, 2:3]; t_ = st3[:, 3:4]; m_ = st3[:, 4:5]
        V.tensor_scalar(m_, st3[:, 0:1], 1.0 / DM, None, ALU.mult)
        V.tensor_scalar(t_, st3[:, 1:2], 1.0 / DM, None, ALU.mult)
        V.tensor_tensor(a_, m_, m_, ALU.mult)
        VF()
        V.tensor_tensor(a_, t_, a_, ALU.subtract)
        VF()
        V.tensor_scalar(a_, a_, EPS, None, ALU.add)
        mv2 = mark(V, "sV")
        S.wait_ge(sems["sV"], mv2)
        S.activation(t_, a_, AF.Sqrt)
        ms2 = mark(S, "sC")
        V.wait_ge(sems["sC"], ms2)
        V.reciprocal(rho_c, t_)
        VF()
        V.scalar_tensor_tensor(nmr_c, m_, -1.0, rho_c, ALU.mult, ALU.mult)
        return mark(V, "sV")

    p3 = {}
    prev_tile_v = None
    prev_xtr = None
    for grp in range(4):
        if grp == 0:
            SY.wait_ge(sems["sCC"], p2["ccar0"])
        else:
            SY.wait_ge(sems["sCC"], p2[f"ccar{grp}"])
            SY.wait_ge(sems["sC"], p3[f"atT{grp * 4 - 1}"])
            SY.wait_ge(sems["sV"], prev_tile_v)
        dma(SY, "ld3a", atv,
            cc_ar_out.ap()[grp].rearrange("(k p) t -> p k t", p=128))
        dma(SY, "ld3a", xo_b.rearrange("p (a b) -> p a b", b=DM),
            x_own[grp * TC1:(grp + 1) * TC1].rearrange("(a p) b -> p a b", p=128))
        for tau in range(4):
            g = grp * 4 + tau
            xo_t = xo_b[:, tau * DM:(tau + 1) * DM]
            dwait(TN, "ld3a")
            if prev_tile_v is not None:
                TN.wait_ge(sems["sV"], prev_tile_v)
            for k in range(8):
                TN.transpose(PS[:, k * 128:(k + 1) * 128],
                             atv[:, k, tau * 128:(tau + 1) * 128], ids)
            m_attr = mark(TN, "sP")
            S.wait_ge(sems["sP"], m_attr)
            if prev_tile_v is not None:
                S.wait_ge(sems["sV"], prev_tile_v)
            S.copy(atT_b, PS[:, 0:1024])
            p3[f"atT{g}"] = mark(S, "sC")
            if tau == 0:
                dwait(V, "ld3a")
            mln1 = ln_stats_tile(xo_t, st3[:, 8:9], st3[:, 9:10])
            if dbg and g == 0:
                SY.wait_ge(sems["sV"], mln1)
                dma(SY, "dbg", dbg_outs["d_st3"], st3)
                dma(SY, "dbg", dbg_outs["d_xo"], xo_t)
                dwait(V, "dbg")
                dwait(S, "dbg")
            S.wait_ge(sems["sV"], mln1)
            S.activation(apl_b, xo_t, AF.Identity, bias=st3[:, 9:10],
                         scale=st3[:, 8:9])
            ms_lnx = mark(S, "sC")
            if dbg:
                SY.wait_ge(sems["sC"], p3[f"atT{g}"])
                dma(SY, "dbg", dbg_outs["d_aret"][g * 128:(g + 1) * 128, :], atT_b)
                SY.wait_ge(sems["sC"], ms_lnx)
                dma(SY, "dbg", dbg_outs["d_apl"][g * 128:(g + 1) * 128, :], apl_b)
                dwait(V, "dbg")
                dwait(S, "dbg")
            V.wait_ge(sems["sC"], ms_lnx)
            V.wait_ge(sems["sC"], p3[f"atT{g}"])
            V.tensor_tensor(atT_b, atT_b, apl_b, ALU.add)
            m_ar_ = mark(V, "sV")
            mln2 = ln_stats_tile(atT_b, st3[:, 10:11], st3[:, 11:12],
                                 pre_v=(sems["sV"], m_ar_),
                                 pre_s=(sems["sV"], m_ar_))
            S.wait_ge(sems["sV"], mln2)
            S.activation(apl_b, atT_b, AF.Identity, bias=st3[:, 11:12],
                         scale=st3[:, 10:11])
            ms_ln2 = mark(S, "sC")
            V.wait_ge(sems["sC"], ms_ln2)
            if g > 0:
                dwait(V, "x2st")
            V.tensor_tensor(x2_b, xo_t, apl_b, ALU.add)
            m_x2 = mark(V, "sV")
            SY.wait_ge(sems["sV"], m_x2)
            dma(SY, "x2st", x2_dram[g * 128:(g + 1) * 128, :], x2_b)
            mln3 = ln_stats_tile(x2_b, st3[:, 12:13], st3[:, 13:14],
                                 pre_v=(sems["sV"], m_x2),
                                 pre_s=(sems["sV"], m_x2))
            S.wait_ge(sems["sV"], mln3)
            S.activation(apl_b, x2_b, AF.Identity, bias=st3[:, 13:14],
                         scale=st3[:, 12:13])
            ms_ln3 = mark(S, "sC")
            if prev_xtr is not None:
                V.wait_ge(sems["sP"], prev_xtr)
            mq = emit_quant(apl_b, xs_t, 64, 0, f_amax, f_sc8,
                            f_rg, f_scr, f_a, f_asq, f_sgn, f_tA, f_tB,
                            wait_v=(sems["sC"], ms_ln3))
            prev_tile_v = mq
            TN.wait_ge(sems["sV"], mq)
            if g > 0:
                TN.wait_ge(sems["sC"], p3[f"xsg{g - 1}"])
            for k in range(8):
                TN.transpose(bankb(2)[:, k * 128:(k + 1) * 128],
                             xs_t[:, k * 128:(k + 1) * 128], idsb)
            prev_xtr = mark(TN, "sP")
            S.wait_ge(sems["sP"], prev_xtr)
            S.copy(xsv[:, :, g, :], bankb(2)[:, 0:1024].rearrange("p (k d) -> p k d", d=128))
            p3[f"xsg{g}"] = mark(S, "sC")
    m_xs_all = p3["xsg15"]
    AR.pop()

    AR.push()
    gws_b = AR.alloc(8 * 512, BF16)
    ups_b = AR.alloc(8 * 512, BF16)
    dws_b = AR.alloc(8 * 512, BF16)
    gb_b = AR.alloc(1024)
    ub_b = AR.alloc(1024)
    hid_b = AR.alloc(1024)
    h_amax = AR.alloc(64)
    h_sc8 = AR.alloc(64)
    h_rg = AR.alloc(64)
    h_scr = AR.alloc(4 * 64)
    h_a = AR.alloc(1024)
    h_asq = AR.alloc(1024)
    h_sgn = AR.alloc(1024)
    h_tA = AR.alloc(1024)
    h_tB = AR.alloc(1024)
    hs_b = AR.alloc(512, BF16)
    hsT_b = AR.alloc(512, BF16)
    do_b = [AR.alloc(DM) for _ in range(2)]

    gwv = gws_b.rearrange("p (k h) -> p k h", h=1024)
    upv = ups_b.rearrange("p (k h) -> p k h", h=1024)
    dwv = dws_b.rearrange("p (j d) -> p j d", d=1024)
    hstv = hsT_b.rearrange("p (j d) -> p j d", d=128)

    for hq in range(4):
        if hq == 0:
            SY.wait_ge(sems["sCC"], m_ag)
        else:
            SY.wait_ge(sems["sP"], p3[f"dmm{hq - 1}_15"])
        for k in range(8):
            sec = cc_ag_out.ap()[k, 0:WSEC].rearrange("(p h) -> p h", p=128)
            dma(SY, "w3", gwv[:, k, :], sec[:, hq * 1024:(hq + 1) * 1024])
            sec2 = cc_ag_out.ap()[k, WSEC:2 * WSEC].rearrange("(p h) -> p h", p=128)
            dma(SY, "w3", upv[:, k, :], sec2[:, hq * 1024:(hq + 1) * 1024])
        for j in range(8):
            gr = hq * 1024 + j * 128
            rk = gr // 512
            jloc = (gr % 512) // 128
            sec3 = cc_ag_out.ap()[rk, 2 * WSEC:AGN].rearrange(
                "(p i j2 d) -> p i j2 d", p=128, i=8, j2=4)
            dma(SY, "w3", dwv[:, j, :].rearrange("p (i d) -> p i d", d=128),
                sec3[:, :, jloc, :])
        def emit_gu(g):
            if g == 0:
                dwait(TN, "w3")
            if hq == 0 and g == 0:
                TN.wait_ge(sems["sC"], m_xs_all)
            for s2 in range(2):
                if hq > 0 or g > 0:
                    wm = p3.get(f"gev{hq}_{g - 1}_{s2}", p3.get(f"gev{hq - 1}_15_{s2}"))
                    TN.wait_ge(sems["sC"], wm)
                for k in range(8):
                    TN.matmul(bank(s2), xsv[:, k, g, :],
                              gwv[:, k, s2 * 512:(s2 + 1) * 512],
                              start=(k == 0), stop=(k == 7))
                m_gmm = mark(TN, "sP")
                if hq > 0 or g > 0:
                    wm = p3.get(f"uev{hq}_{g - 1}_{s2}", p3.get(f"uev{hq - 1}_15_{s2}"))
                    TN.wait_ge(sems["sC"], wm)
                for k in range(8):
                    TN.matmul(bank(2 + s2), xsv[:, k, g, :],
                              upv[:, k, s2 * 512:(s2 + 1) * 512],
                              start=(k == 0), stop=(k == 7))
                m_umm = mark(TN, "sP")
                S.wait_ge(sems["sP"], m_gmm)
                if hq > 0 or g > 0:
                    S.wait_ge(sems["sV"], p3.get(f"hid{hq}_{g - 1}",
                                                 p3.get(f"hid{hq - 1}_15")))
                S.activation(gb_b[:, s2 * 512:(s2 + 1) * 512], bank(s2), AF.Silu,
                             scale=gs_cols[:, 5:6])
                p3[f"gev{hq}_{g}_{s2}"] = mark(S, "sC")
                S.wait_ge(sems["sP"], m_umm)
                S.activation(ub_b[:, s2 * 512:(s2 + 1) * 512], bank(2 + s2),
                             AF.Identity, scale=gs_cols[:, 6:7])
                p3[f"uev{hq}_{g}_{s2}"] = mark(S, "sC")

        for g in range(16):
            if g == 0:
                emit_gu(0)
            V.wait_ge(sems["sC"], p3[f"uev{hq}_{g}_1"])
            V.wait_ge(sems["sC"], p3[f"gev{hq}_{g}_1"])
            V.tensor_tensor(hid_b, gb_b, ub_b, ALU.mult)
            p3[f"hid{hq}_{g}"] = mark(V, "sV")
            if hq > 0 or g > 0:
                V.wait_ge(sems["sP"], p3.get(f"htr{hq}_{g - 1}",
                                             p3.get(f"htr{hq - 1}_15")))
            mqh = emit_quant(hid_b, hs_b, 64, 1, h_amax, h_sc8,
                             h_rg, h_scr, h_a, h_asq, h_sgn, h_tA, h_tB,
                             wait_v=(sems["sV"], p3[f"hid{hq}_{g}"]))
            if g + 1 < 16:
                emit_gu(g + 1)
            TN.wait_ge(sems["sV"], mqh)
            if hq > 0 or g > 0:
                TN.wait_ge(sems["sC"], p3.get(f"hsT{hq}_{g - 1}",
                                              p3.get(f"hsT{hq - 1}_15")))
            for j in range(8):
                TN.transpose(bankb(6)[:, j * 128:(j + 1) * 128],
                             hs_b[:, j * 128:(j + 1) * 128], idsb)
            p3[f"htr{hq}_{g}"] = mark(TN, "sP")
            S.wait_ge(sems["sP"], p3[f"htr{hq}_{g}"])
            S.copy(hsT_b, bankb(6)[:, 0:1024])
            p3[f"hsT{hq}_{g}"] = mark(S, "sC")
            TN.wait_ge(sems["sC"], p3[f"hsT{hq}_{g}"])
            if hq > 0 or g > 0:
                wm = p3.get(f"dev{hq}_{g - 1}", p3.get(f"dev{hq - 1}_15"))
                TN.wait_ge(sems["sC"], wm)
            for s2 in range(2):
                for j in range(8):
                    TN.matmul(bank(4 + s2), hstv[:, j, :],
                              dwv[:, j, s2 * 512:(s2 + 1) * 512],
                              start=(j == 0), stop=(j == 7))
            p3[f"dmm{hq}_{g}"] = mark(TN, "sP")
            dpar = g % 2
            S.wait_ge(sems["sP"], p3[f"dmm{hq}_{g}"])
            if hq > 0 or g >= 2:
                dwait(S, f"dst{dpar}")
            S.activation(do_b[dpar][:, 0:512], bank(4), AF.Identity,
                         scale=gs_cols[:, 7:8])
            S.activation(do_b[dpar][:, 512:1024], bank(5), AF.Identity,
                         scale=gs_cols[:, 7:8])
            p3[f"dev{hq}_{g}"] = mark(S, "sC")
            G.wait_ge(sems["sC"], p3[f"dev{hq}_{g}"])
            if hq == 0:
                dma(G, f"dst{dpar}", dacc_dram[g * 128:(g + 1) * 128, :], do_b[dpar])
            else:
                dma(G, f"dst{dpar}", dacc_dram[g * 128:(g + 1) * 128, :], do_b[dpar],
                    accum_op=ALU.add)
    AR.pop()
    AR.pop()

    barrier()
    AR.push()
    da_in = [AR.alloc(DM) for _ in range(2)]
    x2_in = [AR.alloc(DM) for _ in range(2)]
    o_b = [AR.alloc(DM) for _ in range(2)]
    sq3 = AR.alloc(DM)
    st4 = AR.alloc(16)
    mo = {}
    for g in range(16):
        par = g % 2
        if g >= 2:
            SY.wait_ge(sems["sV"], mo[f"o{g - 2}"])
        dma(SY, f"ld3c_{par}", da_in[par], dacc_dram[g * 128:(g + 1) * 128, :])
        dma(SY, f"ld3c_{par}", x2_in[par], x2_dram[g * 128:(g + 1) * 128, :])
        dwait(V, f"ld3c_{par}")
        V.tensor_tensor(sq3, da_in[par], da_in[par], ALU.mult)
        V.tensor_reduce(st4[:, 0:1], da_in[par], AXX, ALU.add)
        V.tensor_reduce(st4[:, 1:2], sq3, AXX, ALU.add)
        a_ = st4[:, 2:3]; t_ = st4[:, 3:4]; m_ = st4[:, 4:5]
        V.tensor_scalar(m_, st4[:, 0:1], 1.0 / DM, None, ALU.mult)
        V.tensor_scalar(t_, st4[:, 1:2], 1.0 / DM, None, ALU.mult)
        V.tensor_tensor(a_, m_, m_, ALU.mult)
        VF()
        V.tensor_tensor(a_, t_, a_, ALU.subtract)
        VF()
        V.tensor_scalar(a_, a_, EPS, None, ALU.add)
        mv2 = mark(V, "sV")
        S.wait_ge(sems["sV"], mv2)
        S.activation(t_, a_, AF.Sqrt)
        ms2 = mark(S, "sC")
        V.wait_ge(sems["sC"], ms2)
        V.reciprocal(st4[:, 5:6], t_)
        VF()
        V.scalar_tensor_tensor(st4[:, 6:7], m_, -1.0, st4[:, 5:6], ALU.mult, ALU.mult)
        mv3 = mark(V, "sV")
        S.wait_ge(sems["sV"], mv3)
        S.activation(o_b[par], da_in[par], AF.Identity, bias=st4[:, 6:7],
                     scale=st4[:, 5:6])
        ms3 = mark(S, "sC")
        V.wait_ge(sems["sC"], ms3)
        V.tensor_tensor(o_b[par], o_b[par], x2_in[par], ALU.add)
        mo[f"o{g}"] = mark(V, "sV")
        SY.wait_ge(sems["sV"], mo[f"o{g}"])
        dma(SY, "outst", out_own[g * 128:(g + 1) * 128, :], o_b[par])
    if dbg:
        dma(SY, "dbg", dbg_outs["d_x2"], x2_dram)
        dma(SY, "dbg", dbg_outs["d_dacc"], dacc_dram)
    for fam in list(dsems):
        dwait(SY, fam)
    AR.pop()

    with nc.Block() as blk:
        for n in ENGS:
            getattr(blk, n)(lambda eng, n=n: R[n].replay(eng))

    counts = {n: len(R[n].items) for n in ENGS}
    counts["arena_hi"] = AR.hi
    return nc, counts


def make_in_maps(inputs):
    x = np.ascontiguousarray(inputs["L0_attn_x"], dtype=np.float32)
    Wx = np.asarray(inputs["L0_attn_Wx"])
    Wz = np.asarray(inputs["L0_attn_Wz"])
    Wxp = np.asarray(inputs["L0_attn_Wxp"])
    Wdt = np.asarray(inputs["L0_attn_Wdt"])
    Wout = np.asarray(inputs["L0_attn_Wout"])
    gw = np.asarray(inputs["L0_ffn_gate_w"])
    uw = np.asarray(inputs["L0_ffn_up_w"])
    dw = np.asarray(inputs["L0_ffn_down_w"])

    def col8(vec):
        return np.ascontiguousarray(np.asarray(vec).reshape(8, 128).T, np.float32)

    maps = []
    for c in range(8):
        b, hf = c // 2, c % 2
        sl = slice(hf * DI, (hf + 1) * DI)
        m = {
            "x_full": np.ascontiguousarray(x[b]),
            "x_own": np.ascontiguousarray(x[b, hf * TOWN:(hf + 1) * TOWN]),
            "wxT": np.ascontiguousarray(Wx[sl].T, np.float32),
            "wzT": np.ascontiguousarray(Wz[sl].T, np.float32),
            "wxpT": np.ascontiguousarray(Wxp[:, sl].T, np.float32),
            "wdtT": np.ascontiguousarray(Wdt[sl].T, np.float32),
            "woutT": np.ascontiguousarray(Wout[:, sl].T, np.float32),
            "bxv": col8(np.asarray(inputs["L0_attn_bx"])[sl]),
            "bzv": col8(np.asarray(inputs["L0_attn_bz"])[sl]),
            "bdtv": col8(np.asarray(inputs["L0_attn_bdt"])[sl]),
            "convbv": col8(np.asarray(inputs["L0_attn_conv_b"])[sl]),
            "dvecv": col8(np.asarray(inputs["L0_attn_D"])[sl]),
            "convwv": np.ascontiguousarray(
                np.asarray(inputs["L0_attn_conv_w"])[sl, 0, :].reshape(8, 128, 4)
                .transpose(1, 0, 2).reshape(128, 32), np.float32),
            "alogv": np.ascontiguousarray(
                np.asarray(inputs["L0_attn_A_log"])[sl].reshape(8, 128, NST)
                .transpose(1, 0, 2).reshape(128, 8 * NST), np.float32),
            "boutv": (np.asarray(inputs["L0_attn_bout"], np.float32).reshape(1, DM)
                      if hf == 0 else np.zeros((1, DM), np.float32)),
            "normwv": col8(np.asarray(inputs["L0_attn_norm_w"])),
            "gsl": np.ascontiguousarray(gw[:, c * 128:(c + 1) * 128], np.float32),
            "usl": np.ascontiguousarray(uw[:, c * 128:(c + 1) * 128], np.float32),
            "dsl": np.ascontiguousarray(dw[:, c * 512:(c + 1) * 512], np.float32),
            "s_in": np.asarray(inputs["L0_ffn_s_in"], np.float32).reshape(1, 1),
            "s_ind": np.asarray(inputs["L0_ffn_s_ind" if "L0_ffn_s_ind" in inputs
                                       else "L0_ffn_s_in_down"], np.float32).reshape(1, 1),
            "ident": np.eye(128, dtype=np.float32),
        }
        maps.append(m)
    return maps


def kernel(**inputs):
    nc, counts = build_program(dbg=False)
    maps = make_in_maps(inputs)
    res = run_bass_kernel_spmd(nc, maps, core_ids=list(range(8)))
    out = np.empty((4, T, DM), np.float32)
    for c in range(8):
        b, hf = c // 2, c % 2
        out[b, hf * TOWN:(hf + 1) * TOWN] = res.results[c]["out_own"]
    return out


if __name__ == "__main__":
    nc, counts = build_program(dbg=False)
    print("built:", counts)



